# version 06
# speedup vs baseline: 1.4597x; 1.0517x over previous
import numpy as np
import ml_dtypes
from contextlib import ExitStack

import concourse.bass as bass
import concourse.mybir as mybir
from concourse.bass_utils import run_bass_kernel_spmd

F32 = mybir.dt.float32
BF16 = mybir.dt.bfloat16
AF = mybir.ActivationFunctionType
ALU = mybir.AluOpType
AX = mybir.AxisListType
BF16NP = ml_dtypes.bfloat16

T, D, H, HD = 4096, 1024, 16, 64
NCH = 32
EPS_A = 1e-6
EPS_LN = 1e-5
EPS_GN = 1e-6

ENGS = ("pe", "dve", "act", "pool", "sp")
SECT = {"pe": "tensor", "dve": "vector", "act": "scalar", "pool": "gpsimd", "sp": "sync"}


class Buf:
    __slots__ = ("name", "writer", "readers")
    def __init__(self, name):
        self.name = name
        self.writer = None
        self.readers = []


class TBuf:
    __slots__ = ("ap", "buf")
    def __init__(self, ap, buf):
        self.ap = ap
        self.buf = buf


class Ctx:
    def __init__(self, nc):
        self.nc = nc
        self.stream = {e: [] for e in ENGS}
        self.ccount = {e: 0 for e in ENGS}
        self.qcount = {}
        self.know = {e: {} for e in ENGS}
        self.same_engine_wait = {"pe": False, "dve": True, "act": True,
                                 "pool": True, "sp": False}
        self.nops = 0
        self.pending_barrier = {e: [] for e in ENGS}

    def barrier(self):
        toks = self.barrier_token_all()
        for e in ENGS:
            self.pending_barrier[e] = list(toks)

    def op(self, eng, fn, reads=(), writes=(), dma=False, tag="", lane=None):
        if dma:
            lane = lane or (writes[0].name if writes else "misc")
            key = ("q", lane)
        else:
            key = ("c", eng)
        need = {}
        if self.pending_barrier[eng]:
            for tok in self.pending_barrier[eng]:
                k, v = tok
                if v > need.get(k, 0):
                    need[k] = v
            self.pending_barrier[eng] = []
        def _need(tok):
            k, v = tok
            if v > need.get(k, 0):
                need[k] = v
        for b in reads:
            if b.writer:
                _need(b.writer)
        for b in writes:
            if b.writer:
                _need(b.writer)
            for tok in b.readers:
                _need(tok)
        kn = self.know[eng]
        for k in sorted(need, key=repr):
            v = need[k]
            if k == ("c", eng) and not dma and not self.same_engine_wait[eng]:
                continue
            if kn.get(k, 0) >= v:
                continue
            self.stream[eng].append(("w", k, v))
            kn[k] = v
        if dma:
            self.qcount[lane] = self.qcount.get(lane, 0) + 1
            val = self.qcount[lane] * 16
        else:
            self.ccount[eng] += 1
            val = self.ccount[eng]
        self.stream[eng].append(("o", fn, key, 16 if dma else 1, tag))
        self.nops += 1
        tok = (key, val)
        for b in reads:
            b.readers.append(tok)
        for b in writes:
            b.writer = tok
            b.readers = []
        return tok

    def barrier_token_all(self):
        toks = []
        for e in ENGS:
            if self.ccount[e]:
                toks.append((("c", e), self.ccount[e]))
        for lane, n in self.qcount.items():
            if n:
                toks.append((("q", lane), n * 16))
        return toks

    def finalize(self):
        nc = self.nc
        es = ExitStack()
        sems = {}
        for e in ENGS:
            sems[("c", e)] = es.enter_context(nc.semaphore(f"c_{e}"))
        for i, lane in enumerate(self.qcount):
            sems[("q", lane)] = es.enter_context(nc.semaphore(f"q{i}"))
        for e in ENGS:
            if self.ccount[e]:
                self.stream["sp"].append(("w", ("c", e), self.ccount[e]))
        for lane, n in self.qcount.items():
            if n:
                self.stream["sp"].append(("w", ("q", lane), n * 16))
        with nc.Block() as block:
            for e in ENGS:
                entries = self.stream[e]
                if not entries:
                    continue
                def mk(entries=entries):
                    def run(engobj):
                        for ent in entries:
                            if ent[0] == "w":
                                engobj.wait_ge(sems[ent[1]], ent[2])
                            else:
                                _, fn, key, inc, tag = ent
                                inst = fn(engobj)
                                inst.then_inc(sems[key], inc)
                    return run
                getattr(block, SECT[e])(mk())
        es.close()


def gamma_vec():
    return 1.0 - np.exp2(-5.0 - np.arange(H, dtype=np.float64))


def host_consts():
    c = {}
    c["ident"] = np.eye(128, dtype=BF16NP)
    tri = (np.arange(128)[:, None] <= np.arange(128)[None, :])
    c["triU"] = tri.astype(BF16NP)
    g = gamma_vec()
    tp = np.arange(128)
    dm = np.where(tp[None, :, None] >= tp[:, None, None],
                  g[None, None, :] ** (tp[None, :, None] - tp[:, None, None]), 0.0)
    c["D16"] = dm.transpose(0, 2, 1).reshape(128, H * 128).astype(BF16NP)
    tok = np.arange(T, dtype=np.float64).reshape(NCH, 128).T
    c["normal"] = (tok + 1.0).astype(np.float32)
    c["inv_normal"] = (1.0 / (tok + 1.0)).astype(np.float32)
    c["q2s"] = (g[None, :] ** (tp[:, None] + 1.0)).astype(np.float32)
    c["k3s"] = (g[None, :] ** (127.0 - tp[:, None])).astype(np.float32)
    c["gCt"] = np.broadcast_to((g ** 128.0)[None, :], (64, H)).astype(np.float32).copy()
    c["ones_b"] = np.ones((1, 128), dtype=BF16NP)
    c["ones_f"] = np.ones((1, 128), dtype=np.float32)
    c["onescol_b"] = np.ones((128, 1), dtype=BF16NP)
    c["one1_f"] = np.ones((1, 1), dtype=np.float32)
    c["one1_b"] = np.ones((1, 1), dtype=BF16NP)
    c["ones512_b"] = np.ones((1, 512), dtype=BF16NP)
    return c

CONST_DTYPES = {"ident": BF16, "triU": BF16, "D16": BF16, "normal": F32,
                "inv_normal": F32, "q2s": F32, "k3s": F32, "gCt": F32,
                "ones_b": BF16, "ones_f": F32, "onescol_b": BF16, "one1_f": F32, "one1_b": BF16,
                "ones512_b": BF16}


class K:
    def __init__(self, inputs, outputs, consts=True):
        nc = bass.Bass("TRN2", target_bir_lowering=False)
        self.nc = nc
        self.ctx = Ctx(nc)
        self.din = {}
        self.dout = {}
        for n, (shape, dt) in inputs.items():
            self.din[n] = nc.dram_tensor(n, list(shape), dt, kind="ExternalInput").ap()
        for n, (shape, dt) in outputs.items():
            self.dout[n] = nc.dram_tensor(n, list(shape), dt, kind="ExternalOutput").ap()
        self.sb_bytes = 0
        self._uniq = 0
        self._arena = None
        self._arena_bytes = 0
        self._region = None
        if consts:
            for n, dt in CONST_DTYPES.items():
                pass
        self.P0 = self.psum("P0", [128, 1024], F32)
        self.P1 = self.psum("P1", [128, 1024], F32)
        self.P2 = self.psum("P2", [128, 1024], F32)
        self.PT = self.psum("PTT", [128, 1024], BF16)
        self.P3 = self.psum("P3", [128, 512], F32)

    def debug_dump(self, name, tb, shape=None, rows=128):
        import os as _os
        if not _os.environ.get("MKDBG"):
            return
        ap = tb.ap
        shape = list(ap.shape) if shape is None else shape
        o = self.nc.dram_tensor("dbg_" + name, shape, F32, kind="ExternalOutput").ap()
        self.ctx.op("pool", lambda s: s.dma_start(o, ap),
                    reads=[tb.buf], dma=True, tag="dbgst")

    def constcol(self, value):
        if not hasattr(self, "_ccols"):
            self._ccols = {}
        if value not in self._ccols:
            tb = self.sb("ccol", [128, 1], F32, perm=True)
            self.ctx.op("dve", lambda v, tb=tb, value=value: v.memset(tb.ap, value),
                        writes=[tb.buf], tag="ccol")
            self._ccols[value] = tb
        return self._ccols[value]

    def uniq(self):
        self._uniq += 1
        return self._uniq

    def sb(self, name, shape, dtype, perm=None):
        per_part = int(np.prod(shape[1:])) * (2 if dtype == BF16 else 4)
        if self._region is None or perm:
            h = self.nc.alloc_sbuf_tensor(f"{name}_{self.uniq()}", list(shape), dtype)
            self.sb_bytes += per_part
            return TBuf(h.ap(), Buf(name))
        off = (self._region + 3) & ~3
        assert off + per_part <= self._arena_bytes, \
            f"arena overflow at {name}: {off}+{per_part} > {self._arena_bytes}"
        self._region = off + per_part
        self.region_peak = max(self.region_peak, self._region)
        base = self._arena.ap()
        if dtype == BF16:
            ap = base[0:shape[0], off // 2: off // 2 + per_part // 2]
        else:
            ap = base.bitcast(F32)[0:shape[0], off // 4: off // 4 + per_part // 4]
        if len(shape) == 2 and shape[1] != ap.shape[1]:
            ap = ap[:, 0:shape[1]]
        return TBuf(ap, Buf(name))

    def mk_arena(self, nbytes):
        self._arena = self.nc.alloc_sbuf_tensor("ARENA", [128, nbytes // 2], BF16)
        self._arena_bytes = nbytes
        self.sb_bytes += nbytes
        self.region_peak = 0

    def push_region(self):
        assert self._region is None
        self._region = 0
        self.ctx.barrier()

    def pop_region(self):
        self._region = None
        self.ctx.barrier()

    def psum(self, name, shape, dtype):
        h = self.nc.alloc_psum_tensor(name, list(shape), dtype)
        return TBuf(h.ap(), Buf(name))

    def carve(self, tb, n, width):
        return [TBuf(tb.ap[:, i * width:(i + 1) * width], Buf(f"{tb.buf.name}[{i}]"))
                for i in range(n)]

    def add_const_inputs(self, names=None):
        names = list(CONST_DTYPES) if names is None else names
        for n in names:
            dt = CONST_DTYPES[n]
            shp = {"ident": (128, 128), "triU": (128, 128), "D16": (128, H * 128),
                   "normal": (128, NCH), "inv_normal": (128, NCH), "q2s": (128, H),
                   "k3s": (128, H), "gCt": (64, H), "ones_b": (1, 128),
                   "ones_f": (1, 128), "onescol_b": (128, 1), "one1_f": (1, 1),
                   "one1_b": (1, 1), "ones512_b": (1, 512)}[n]
            self.din["c_" + n] = self.nc.dram_tensor("c_" + n, list(shp), dt,
                                                     kind="ExternalInput").ap()
        self.C = {}
        cbuf = Buf("consts")
        for n in names:
            dt = CONST_DTYPES[n]
            src = self.din["c_" + n]
            tb = self.sb("c" + n, list(src.shape), dt, perm=True)
            self.ctx.op("sp", lambda s, tb=tb, src=src: s.dma_start(tb.ap, src),
                        writes=[cbuf], dma=True, tag=f"ld_const_{n}", lane="consts")
            tb.buf = cbuf
            self.C[n] = tb

    def load_stream_from_dram(self, src_ap, cast):
        eng = "pool" if cast else "sp"
        view = src_ap.rearrange("(t p) f -> p t f", p=128)
        self.ctx.op(eng, lambda g: g.dma_start(
            self.STR.ap.rearrange("p (t f) -> p t f", t=NCH), view),
            writes=[self.STR.buf] + [t.buf for t in self.str_tiles],
            dma=True, tag="ld_stream")

    def store_stream_to_dram(self, dst_ap, cast=False):
        eng = "pool" if cast else "sp"
        view = dst_ap.rearrange("(t p) f -> p t f", p=128)
        self.ctx.op(eng, lambda g: g.dma_start(
            view, self.STR.ap.rearrange("p (t f) -> p t f", t=NCH)),
            reads=[t.buf for t in self.str_tiles] + [self.STR.buf],
            dma=True, tag="st_stream")

    def store_stream_to_dram_f32(self, dst_ap):
        view = dst_ap.rearrange("(t p) f -> p t f", p=128)
        self.ctx.op("pool", lambda g: g.dma_start(
            view, self.STR.ap.rearrange("p (t f) -> p t f", t=NCH)),
            reads=[t.buf for t in self.str_tiles] + [self.STR.buf],
            dma=True, tag="st_stream_f32")

    def alloc_stream(self):
        self.STR = self.sb("STR", [128, NCH * 1024], BF16, perm=True)
        self.str_tiles = self.carve(self.STR, NCH, 1024)
        self.stats_s = self.sb("stats_s", [128, NCH], F32, perm=True)
        self.stats_q = self.sb("stats_q", [128, NCH], F32, perm=True)
        self.stats_s_t = self.carve(self.stats_s, NCH, 1)
        self.stats_q_t = self.carve(self.stats_q, NCH, 1)

    def transpose8(self, src_tb, src_off, dst_tb, dst_view_fn=None, n=8, w=128,
                   rbuf=None):
        ctx = self.ctx
        rb = rbuf if rbuf is not None else src_tb.buf
        for i in range(n):
            ctx.op("pe", lambda t, i=i: t.transpose(
                self.PT.ap[:, i * w:(i + 1) * w],
                src_tb.ap[:, src_off + i * w: src_off + (i + 1) * w],
                self.C["ident"].ap),
                reads=[rb, self.C["ident"].buf], writes=[self.PT.buf],
                tag="T")
        dst_ap = dst_tb.ap if dst_view_fn is None else dst_view_fn(dst_tb.ap)
        src_ap = self.PT.ap[:, :n * w]
        ctx.op("act", lambda s: s.copy(dst_ap, src_ap),
               reads=[self.PT.buf], writes=[dst_tb.buf], tag="Tcopy")

    def transpose_heads(self, src_tb, dst_tb, rbuf=None):
        ctx = self.ctx
        rb = rbuf if rbuf is not None else src_tb.buf
        for b in range(2):
            for hh in range(8):
                h = 8 * b + hh
                ctx.op("pe", lambda t, h=h, hh=hh: t.transpose(
                    self.PT.ap[0:64, 128 * hh:128 * (hh + 1)],
                    src_tb.ap[:, 64 * h:64 * (h + 1)],
                    self.C["ident"].ap),
                    reads=[rb, self.C["ident"].buf], writes=[self.PT.buf],
                    tag="Th")
            ctx.op("act", lambda s, b=b: s.copy(
                dst_tb.ap[0:64, 1024 * b:1024 * (b + 1)], self.PT.ap[0:64, 0:1024]),
                reads=[self.PT.buf], writes=[dst_tb.buf], tag="Thcopy")

    def proj_tok(self, dst_ps, xT_tb, w_tb, bias_row=None, eng_bias_sb=None):
        ctx = self.ctx
        for half in range(2):
            o = 512 * half
            for dc in range(8):
                last = (dc == 7 and bias_row is None)
                ctx.op("pe", lambda t, dc=dc, o=o, last=last: t.matmul(
                    dst_ps.ap[:, o:o + 512],
                    xT_tb.ap[:, 128 * dc:128 * (dc + 1)],
                    w_tb.ap[:, 1024 * dc + o: 1024 * dc + o + 512],
                    start=(dc == 0), stop=last),
                    reads=[xT_tb.buf, w_tb.buf], writes=[dst_ps.buf], tag="mmproj")
            if bias_row is not None:
                ctx.op("pe", lambda t, o=o: t.matmul(
                    dst_ps.ap[:, o:o + 512], self.C["ones_b"].ap,
                    bias_row.ap[:, o:o + 512], start=False, stop=True),
                    reads=[bias_row.buf, self.C["ones_b"].buf],
                    writes=[dst_ps.buf], tag="mmbias")


def as3d(ap, d=64):
    return ap.rearrange("p (h d) -> p h d", d=d)


def bc16(ap2d, d=64, n=128):
    return ap2d.unsqueeze(2).broadcast_to([n, ap2d.shape[1], d])


STAGE = 99
def emit_phaseA(K, nch=NCH):
    ctx = K.ctx
    C = K.C
    W = {}
    for nm in ("Wq", "Wk", "Wv", "Wo"):
        w = K.sb("A_" + nm, [128, 8 * 1024], BF16)
        src = K.din["L0_attn_" + nm]
        ctx.op("pool", lambda g, w=w, src=src: g.dma_start(
            w.ap.rearrange("p (c f) -> p c f", c=8),
            src.rearrange("(c p) f -> p c f", p=128)),
            writes=[w.buf], dma=True, tag="ldw")
        W[nm] = w
    BR = {}
    for nm in ("bq", "bk", "bv", "bo"):
        b = K.sb("A_" + nm, [1, 1024], BF16)
        src = K.din["L0_attn_" + nm]
        ctx.op("pool", lambda g, b=b, src=src: g.dma_start(
            b.ap, src.rearrange("(a f) -> a f", a=1)),
            writes=[b.buf], dma=True, tag="ldb")
        BR[nm] = b

    xT = K.sb("A_xT", [128, 8 * 128], BF16)
    vb = K.sb("A_vb", [128, 1024], BF16)
    qb = K.sb("A_qb", [128, 1024], BF16)
    kb = K.sb("A_kb", [128, 1024], BF16)
    ksob = K.sb("A_ksob", [128, 1024], BF16)
    qsib = K.sb("A_qsib", [128, 1024], BF16)
    qpb = K.sb("A_qpb", [128, 1024], BF16)
    vpb = K.sb("A_vpb", [128, 1024], BF16)
    scr = K.sb("A_scr", [128, 1024], F32)
    qT = K.sb("A_qT", [64, 16 * 128], BF16)
    kT = K.sb("A_kT", [64, 16 * 128], BF16)
    scm = K.sb("A_scm", [128, 1024], BF16)
    aob = K.sb("A_aob", [128, 1024], BF16)
    aoT = K.sb("A_aoT", [128, 8 * 128], BF16)
    KVsb = K.sb("A_KV", [64, 1024], BF16)
    cars = {nm: K.sb("A_car_" + nm, [1, 1024], F32) for nm in ("k", "q", "3", "4")}
    carCS = K.sb("A_carCS", [1, 16], F32)
    f16 = {nm: K.sb("A_f_" + nm, [128, 16], F32)
           for nm in ("siv", "qsc", "si", "sov", "so", "csk", "alo", "csr",
                      "sg", "cs", "ccs", "cmp", "t0", "cor")}
    csb = K.sb("A_csb", [128, 16], BF16)

    ctx.op("dve", lambda v: v.memset(KVsb.ap, 0.0), writes=[KVsb.buf], tag="kv0")
    for nm in cars:
        ctx.op("dve", lambda v, nm=nm: v.memset(cars[nm].ap, 0.0),
               writes=[cars[nm].buf], tag="car0")
    ctx.op("dve", lambda v: v.memset(carCS.ap, 0.0), writes=[carCS.buf])

    P0, P1, P2, P3 = K.P0, K.P1, K.P2, K.P3

    def cumsum_mm(ps, src_b, car, tag):
        for half in range(2):
            o = 512 * half
            ctx.op("pe", lambda t, o=o: t.matmul(
                ps.ap[:, o:o + 512], C["triU"].ap, src_b.ap[:, o:o + 512],
                start=True, stop=False),
                reads=[C["triU"].buf, src_b.buf], writes=[ps.buf], tag="cum" + tag)
            ctx.op("pe", lambda t, o=o: t.matmul(
                ps.ap[:, o:o + 512], C["ones_f"].ap, car.ap[:, o:o + 512],
                start=False, stop=True),
                reads=[C["ones_f"].buf, car.buf], writes=[ps.buf], tag="cumcar")
        for half in range(2):
            o = 512 * half
            ctx.op("pe", lambda t, o=o: t.matmul(
                P3.ap[0:1, 0:512], C["onescol_b"].ap, src_b.ap[:, o:o + 512],
                start=True, stop=False),
                reads=[C["onescol_b"].buf, src_b.buf], writes=[P3.buf], tag="colsum")
            ctx.op("pe", lambda t, o=o: t.matmul(
                P3.ap[0:1, 0:512], C["one1_f"].ap, car.ap[:, o:o + 512],
                start=False, stop=True),
                reads=[C["one1_f"].buf, car.buf], writes=[P3.buf], tag="colcar")
            ctx.op("dve", lambda v, o=o: v.tensor_copy(
                car.ap[:, o:o + 512], P3.ap[0:1, 0:512]),
                reads=[P3.buf], writes=[car.buf], tag="carcopy")

    for c in range(nch):
        tile = K.str_tiles[c]
        ncol = C["normal"].ap[:, c:c + 1]
        incol = C["inv_normal"].ap[:, c:c + 1]
        K.transpose8(K.STR, 1024 * c, xT, rbuf=tile.buf)
        if STAGE < 2:
            continue
        K.proj_tok(P0, xT, W["Wq"], BR["bq"])
        ctx.op("act", lambda s: s.activation(qb.ap, P0.ap, AF.Sigmoid),
               reads=[P0.buf], writes=[qb.buf], tag="sigq")
        if STAGE < 3:
            continue
        K.proj_tok(P1, xT, W["Wk"], BR["bk"])
        ctx.op("act", lambda s: s.activation(kb.ap, P1.ap, AF.Sigmoid),
               reads=[P1.buf], writes=[kb.buf], tag="sigk")
        if STAGE < 4:
            continue
        K.proj_tok(P0, xT, W["Wv"], BR["bv"])
        ctx.op("act", lambda s: s.copy(vb.ap, P0.ap),
               reads=[P0.buf], writes=[vb.buf], tag="evacv")
        if STAGE < 5:
            continue
        if STAGE < 6:
            continue
        cumsum_mm(P1, kb, cars["k"], "k")
        cumsum_mm(P0, qb, cars["q"], "q")
        if STAGE < 7:
            continue
        ctx.op("dve", lambda v: v.scalar_tensor_tensor(
            scr.ap, P1.ap, EPS_A, qb.ap, ALU.add, ALU.mult),
            reads=[P1.buf, qb.buf], writes=[scr.buf], tag="t1")
        ctx.op("dve", lambda v: v.tensor_reduce(
            f16["siv"].ap, as3d(scr.ap), AX.X, ALU.add),
            reads=[scr.buf], writes=[f16["siv"].buf], tag="red_t1")
        ctx.op("dve", lambda v: v.tensor_reduce(
            f16["cor"].ap, as3d(P1.ap), AX.X, ALU.add),
            reads=[P1.buf], writes=[f16["cor"].buf], tag="redc_t1")
        ctx.op("dve", lambda v: v.tensor_scalar(
            f16["cor"].ap, f16["cor"].ap, EPS_A, 64.0 * EPS_A * EPS_A,
            ALU.mult, ALU.add),
            reads=[f16["cor"].buf], writes=[f16["cor"].buf], tag="corsc_t1")
        ctx.op("dve", lambda v: v.tensor_add(
            f16["siv"].ap, f16["siv"].ap, f16["cor"].ap),
            reads=[f16["siv"].buf, f16["cor"].buf], writes=[f16["siv"].buf],
            tag="coradd_t1")
        ctx.op("dve", lambda v: v.reciprocal(f16["qsc"].ap, f16["siv"].ap),
               reads=[f16["siv"].buf], writes=[f16["qsc"].buf])
        ctx.op("dve", lambda v, ncol=ncol: v.tensor_scalar_mul(
            f16["si"].ap, f16["qsc"].ap, ncol),
            reads=[f16["qsc"].buf], writes=[f16["si"].buf])
        ctx.op("dve", lambda v: v.scalar_tensor_tensor(
            scr.ap, P0.ap, EPS_A, kb.ap, ALU.add, ALU.mult),
            reads=[P0.buf, kb.buf], writes=[scr.buf], tag="t2")
        ctx.op("dve", lambda v: v.tensor_reduce(
            f16["sov"].ap, as3d(scr.ap), AX.X, ALU.add),
            reads=[scr.buf], writes=[f16["sov"].buf], tag="red_t2")
        ctx.op("dve", lambda v: v.tensor_reduce(
            f16["cor"].ap, as3d(P0.ap), AX.X, ALU.add),
            reads=[P0.buf], writes=[f16["cor"].buf], tag="redc_t2")
        ctx.op("dve", lambda v: v.tensor_scalar(
            f16["cor"].ap, f16["cor"].ap, EPS_A, 64.0 * EPS_A * EPS_A,
            ALU.mult, ALU.add),
            reads=[f16["cor"].buf], writes=[f16["cor"].buf], tag="corsc_t2")
        ctx.op("dve", lambda v: v.tensor_add(
            f16["sov"].ap, f16["sov"].ap, f16["cor"].ap),
            reads=[f16["sov"].buf, f16["cor"].buf], writes=[f16["sov"].buf],
            tag="coradd_t2")
        ctx.op("dve", lambda v: v.reciprocal(f16["t0"].ap, f16["sov"].ap),
               reads=[f16["sov"].buf], writes=[f16["t0"].buf])
        ctx.op("dve", lambda v, ncol=ncol: v.tensor_scalar_mul(
            f16["so"].ap, f16["t0"].ap, ncol),
            reads=[f16["t0"].buf], writes=[f16["so"].buf])
        if STAGE < 8:
            continue
        ctx.op("dve", lambda v: v.tensor_tensor(
            as3d(ksob.ap), as3d(kb.ap), bc16(f16["so"].ap), ALU.mult),
            reads=[kb.buf, f16["so"].buf], writes=[ksob.buf], tag="kso")
        ctx.op("dve", lambda v: v.tensor_tensor(
            as3d(qsib.ap), as3d(qb.ap), bc16(f16["si"].ap), ALU.mult),
            reads=[qb.buf, f16["si"].buf], writes=[qsib.buf], tag="qsi")
        cumsum_mm(P1, ksob, cars["3"], "3")
        cumsum_mm(P0, qsib, cars["4"], "4")
        if STAGE < 9:
            continue
        ctx.op("dve", lambda v: v.scalar_tensor_tensor(
            scr.ap, P1.ap, EPS_A, qb.ap, ALU.add, ALU.mult),
            reads=[P1.buf, qb.buf], writes=[scr.buf], tag="t3")
        ctx.op("dve", lambda v: v.tensor_reduce(
            f16["csk"].ap, as3d(scr.ap), AX.X, ALU.add),
            reads=[scr.buf], writes=[f16["csk"].buf], tag="red_t3")
        ctx.op("dve", lambda v: v.tensor_reduce(
            f16["cor"].ap, as3d(P1.ap), AX.X, ALU.add),
            reads=[P1.buf], writes=[f16["cor"].buf], tag="redc_t3")
        ctx.op("dve", lambda v: v.tensor_scalar(
            f16["cor"].ap, f16["cor"].ap, EPS_A, 64.0 * EPS_A * EPS_A,
            ALU.mult, ALU.add),
            reads=[f16["cor"].buf], writes=[f16["cor"].buf], tag="corsc_t3")
        ctx.op("dve", lambda v: v.tensor_add(
            f16["csk"].ap, f16["csk"].ap, f16["cor"].ap),
            reads=[f16["csk"].buf, f16["cor"].buf], writes=[f16["csk"].buf],
            tag="coradd_t3")
        ctx.op("dve", lambda v, incol=incol: v.tensor_scalar_mul(
            f16["csk"].ap, f16["csk"].ap, incol),
            reads=[f16["csk"].buf], writes=[f16["csk"].buf])
        ctx.op("act", lambda s: s.activation(f16["alo"].ap, f16["csk"].ap, AF.Sigmoid),
               reads=[f16["csk"].buf], writes=[f16["alo"].buf], tag="sig_alloc")
        ctx.op("dve", lambda v: v.scalar_tensor_tensor(
            scr.ap, P0.ap, EPS_A, kb.ap, ALU.add, ALU.mult),
            reads=[P0.buf, kb.buf], writes=[scr.buf], tag="t4")
        ctx.op("dve", lambda v: v.tensor_reduce(
            f16["csr"].ap, as3d(scr.ap), AX.X, ALU.add),
            reads=[scr.buf], writes=[f16["csr"].buf], tag="red_t4")
        ctx.op("dve", lambda v: v.tensor_reduce(
            f16["cor"].ap, as3d(P0.ap), AX.X, ALU.add),
            reads=[P0.buf], writes=[f16["cor"].buf], tag="redc_t4")
        ctx.op("dve", lambda v: v.tensor_scalar(
            f16["cor"].ap, f16["cor"].ap, EPS_A, 64.0 * EPS_A * EPS_A,
            ALU.mult, ALU.add),
            reads=[f16["cor"].buf], writes=[f16["cor"].buf], tag="corsc_t4")
        ctx.op("dve", lambda v: v.tensor_add(
            f16["csr"].ap, f16["csr"].ap, f16["cor"].ap),
            reads=[f16["csr"].buf, f16["cor"].buf], writes=[f16["csr"].buf],
            tag="coradd_t4")
        ctx.op("dve", lambda v, incol=incol: v.tensor_scalar_mul(
            f16["csr"].ap, f16["csr"].ap, incol),
            reads=[f16["csr"].buf], writes=[f16["csr"].buf])
        ctx.op("act", lambda s: s.activation(f16["sg"].ap, f16["csr"].ap, AF.Sigmoid),
               reads=[f16["csr"].buf], writes=[f16["sg"].buf], tag="sig_cs")
        ctx.op("dve", lambda v: v.tensor_scalar(
            f16["t0"].ap, f16["sg"].ap, -1.0, 1.0, ALU.mult, ALU.add),
            reads=[f16["sg"].buf], writes=[f16["t0"].buf])
        ctx.op("dve", lambda v: v.reciprocal(f16["t0"].ap, f16["t0"].ap),
               reads=[f16["t0"].buf], writes=[f16["t0"].buf])
        ctx.op("dve", lambda v: v.tensor_mul(f16["cs"].ap, f16["sg"].ap, f16["t0"].ap),
               reads=[f16["sg"].buf, f16["t0"].buf], writes=[f16["cs"].buf])
        ctx.op("dve", lambda v: v.tensor_copy(csb.ap, f16["cs"].ap),
               reads=[f16["cs"].buf], writes=[csb.buf])
        if STAGE < 10:
            continue
        ctx.op("pe", lambda t: t.matmul(P1.ap[:, 0:16], C["triU"].ap, csb.ap,
                                        start=True, stop=False),
               reads=[C["triU"].buf, csb.buf], writes=[P1.buf], tag="cumcs")
        ctx.op("pe", lambda t: t.matmul(P1.ap[:, 0:16], C["ones_f"].ap,
                                        carCS.ap, start=False, stop=True),
               reads=[C["ones_f"].buf, carCS.buf], writes=[P1.buf])
        ctx.op("dve", lambda v: v.tensor_copy(f16["ccs"].ap, P1.ap[:, 0:16]),
               reads=[P1.buf], writes=[f16["ccs"].buf])
        ctx.op("pe", lambda t: t.matmul(P3.ap[0:1, 0:16], C["onescol_b"].ap, csb.ap,
                                        start=True, stop=False),
               reads=[C["onescol_b"].buf, csb.buf], writes=[P3.buf])
        ctx.op("pe", lambda t: t.matmul(P3.ap[0:1, 0:16], C["one1_f"].ap,
                                        carCS.ap, start=False, stop=True),
               reads=[C["one1_f"].buf, carCS.buf], writes=[P3.buf])
        ctx.op("dve", lambda v: v.tensor_copy(carCS.ap, P3.ap[0:1, 0:16]),
               reads=[P3.buf], writes=[carCS.buf])
        ctx.op("dve", lambda v: v.reciprocal(f16["ccs"].ap, f16["ccs"].ap),
               reads=[f16["ccs"].buf], writes=[f16["ccs"].buf])
        ctx.op("dve", lambda v: v.tensor_mul(f16["cmp"].ap, f16["cs"].ap, f16["ccs"].ap),
               reads=[f16["cs"].buf, f16["ccs"].buf], writes=[f16["cmp"].buf])
        ctx.op("dve", lambda v, ncol=ncol: v.tensor_scalar_mul(
            f16["cmp"].ap, f16["cmp"].ap, ncol),
            reads=[f16["cmp"].buf], writes=[f16["cmp"].buf])
        if STAGE < 11:
            continue
        ctx.op("dve", lambda v: v.tensor_tensor(
            as3d(qpb.ap), as3d(qb.ap), bc16(f16["qsc"].ap), ALU.mult),
            reads=[qb.buf, f16["qsc"].buf], writes=[qpb.buf], tag="qp")
        ctx.op("dve", lambda v: v.tensor_tensor(
            as3d(vpb.ap), as3d(vb.ap), bc16(f16["cmp"].ap), ALU.mult),
            reads=[vb.buf, f16["cmp"].buf], writes=[vpb.buf], tag="vp")
        K.transpose_heads(qpb, qT)
        K.transpose_heads(kb, kT)
        if STAGE < 12:
            continue
        for grp in range(2):
            for hh in range(8):
                h = grp * 8 + hh
                ctx.op("pe", lambda t, hh=hh, h=h: t.matmul(
                    P1.ap[:, 128 * hh:128 * (hh + 1)],
                    kT.ap[0:64, 128 * h:128 * (h + 1)],
                    qT.ap[0:64, 128 * h:128 * (h + 1)],
                    start=True, stop=True),
                    reads=[kT.buf, qT.buf], writes=[P1.buf], tag="scores")
            ctx.op("dve", lambda v: v.tensor_tensor(
                scm.ap.rearrange("p (h w) -> p h w", w=128),
                P1.ap.rearrange("p (h w) -> p h w", w=128),
                C["triU"].ap.unsqueeze(1).broadcast_to([128, 8, 128]),
                ALU.mult),
                reads=[P1.buf, C["triU"].buf], writes=[scm.buf], tag="mask")
            if STAGE < 13:
                continue
            for hh in range(8):
                h = grp * 8 + hh
                ctx.op("pe", lambda t, h=h, hh=hh, c=c: t.matmul(
                    P0.ap[:, 64 * h:64 * (h + 1)],
                    scm.ap[:, 128 * hh:128 * (hh + 1)],
                    vpb.ap[:, 64 * h:64 * (h + 1)], start=True, stop=(c == 0)),
                    reads=[scm.buf, vpb.buf], writes=[P0.buf], tag="intra")
                if c > 0:
                    ctx.op("pe", lambda t, h=h: t.matmul(
                        P0.ap[:, 64 * h:64 * (h + 1)],
                        qT.ap[0:64, 128 * h:128 * (h + 1)],
                        KVsb.ap[0:64, 64 * h:64 * (h + 1)], start=False, stop=True),
                        reads=[qT.buf, KVsb.buf], writes=[P0.buf], tag="inter")
        if STAGE < 14:
            continue
        for h in range(H):
            ctx.op("pe", lambda t, h=h, c=c: t.matmul(
                P2.ap[0:64, 64 * h:64 * (h + 1)],
                kb.ap[:, 64 * h:64 * (h + 1)],
                vpb.ap[:, 64 * h:64 * (h + 1)],
                start=(c == 0), stop=(c == nch - 1)),
                reads=[kb.buf, vpb.buf], writes=[P2.buf], tag="kvupd")
        ctx.op("dve", lambda v: v.tensor_copy(KVsb.ap, P2.ap[0:64, :]),
               reads=[P2.buf], writes=[KVsb.buf], tag="kvevac")
        if STAGE < 15:
            continue
        ctx.op("dve", lambda v: v.tensor_tensor(
            as3d(aob.ap), as3d(P0.ap), bc16(f16["alo"].ap), ALU.mult),
            reads=[P0.buf, f16["alo"].buf], writes=[aob.buf], tag="gate_alloc")
        K.transpose8(aob, 0, aoT)
        K.proj_tok(P1, aoT, W["Wo"], BR["bo"])
        ctx.op("dve", lambda v, c=c, tile=tile: v.scalar_tensor_tensor(
            tile.ap, P1.ap, 0.0, tile.ap, ALU.add, ALU.add,
            accum_out=K.stats_s_t[c].ap),
            reads=[P1.buf, tile.buf],
            writes=[tile.buf, K.stats_s_t[c].buf], tag="resid")
        ctx.op("act", lambda s, c=c, tile=tile: s.activation(
            scr.ap, tile.ap, AF.Square, accum_out=K.stats_q_t[c].ap),
            reads=[tile.buf], writes=[scr.buf, K.stats_q_t[c].buf], tag="sqdump")


def emit_ln_sweep(K, extra_per_tile=None, mean_out=None, rstd_out=None):
    ctx = K.ctx
    mean = mean_out or K.sb("sw_mean", [128, NCH], F32)
    var = K.sb("sw_var", [128, NCH], F32)
    rstd = rstd_out or K.sb("sw_rstd", [128, NCH], F32)
    sbufs = [t.buf for t in K.stats_s_t]
    qbufs = [t.buf for t in K.stats_q_t]
    ctx.op("dve", lambda v: v.tensor_scalar_mul(mean.ap, K.stats_s.ap, 1.0 / D),
           reads=sbufs + [K.stats_s.buf], writes=[mean.buf], tag="sw_mean")
    ctx.op("dve", lambda v: v.tensor_scalar_mul(var.ap, K.stats_q.ap, 1.0 / D),
           reads=qbufs + [K.stats_q.buf], writes=[var.buf], tag="sw_var")
    ctx.op("dve", lambda v: v.tensor_mul(K.stats_q.ap, mean.ap, mean.ap),
           reads=[mean.buf], writes=[K.stats_q.buf] + qbufs, tag="sw_m2")
    ctx.op("dve", lambda v: v.tensor_sub(var.ap, var.ap, K.stats_q.ap),
           reads=[var.buf, K.stats_q.buf], writes=[var.buf], tag="sw_vm")
    epsc = K.constcol(EPS_LN)
    ctx.op("act", lambda s: s.activation(var.ap, var.ap, AF.Ln, bias=epsc.ap),
           reads=[var.buf, epsc.buf], writes=[var.buf], tag="sw_ln")
    ctx.op("act", lambda s: s.activation(rstd.ap, var.ap, AF.Exp, scale=-0.5),
           reads=[var.buf], writes=[rstd.buf], tag="sw_exp")
    for c in range(NCH):
        tile = K.str_tiles[c]
        ctx.op("dve", lambda v, c=c, tile=tile: v.tensor_scalar(
            tile.ap, tile.ap, mean.ap[:, c:c + 1], rstd.ap[:, c:c + 1],
            ALU.subtract, ALU.mult),
            reads=[tile.buf, mean.buf, rstd.buf], writes=[tile.buf], tag="sw_norm")
        if extra_per_tile is not None:
            extra_per_tile(c)
    return mean, rstd


def emit_castsB(K):
    nc, ctx = K.nc, K.ctx
    wfc_bf = nc.dram_tensor("wfc_bf", [4096, 1024], BF16, kind="Internal").ap()
    wproj_bf = nc.dram_tensor("wproj_bf", [1024, 4096], BF16, kind="Internal").ap()
    bfc = Buf("wfc_bf"); bpj = Buf("wproj_bf")
    ctx.op("pool", lambda g: g.dma_start(wfc_bf, K.din["L0_ffn_W_fc"]),
           writes=[bfc], dma=True, tag="castWfc")
    ctx.op("pool", lambda g: g.dma_start(wproj_bf, K.din["L0_ffn_W_proj"]),
           writes=[bpj], dma=True, tag="castWproj")
    return (wfc_bf, bfc), (wproj_bf, bpj)


def emit_prepB(K, casts):
    nc, ctx = K.nc, K.ctx
    (wfc_bf, bfc), (wproj_bf, bpj) = casts
    wfcT_s = nc.dram_tensor("wfcT_s", [32, 128, 8, 128], BF16, kind="Internal").ap()
    wprojT_s = nc.dram_tensor("wprojT_s", [32, 128, 1024], BF16, kind="Internal").ap()
    bfcT = Buf("wfcT_s"); bpjT = Buf("wprojT_s")
    stg = [K.sb("B_stage%d" % i, [128, 4096], BF16) for i in range(2)]
    for dc in range(8):
        st = stg[dc % 2]
        ctx.op("sp", lambda s, dc=dc, st=st: s.dma_start(
            st.ap, wfc_bf[:, 128 * dc:128 * (dc + 1)], transpose=True),
            reads=[bfc], writes=[st.buf], dma=True, tag="Tfc",
            lane="B_stage%d" % (dc % 2)),
        ctx.op("act", lambda s, dc=dc, st=st: s.dma_start(
            wfcT_s[:, :, dc, :].rearrange("fc p w -> p fc w"),
            st.ap.rearrange("p (fc w) -> p fc w", w=128)),
            reads=[st.buf], writes=[bfcT], dma=True, tag="stfc", lane="wfcT_s")
    for fc in range(32):
        st = stg[fc % 2]
        ctx.op("sp", lambda s, fc=fc, st=st: s.dma_start(
            st.ap[:, 0:1024], wproj_bf[:, 128 * fc:128 * (fc + 1)],
            transpose=True),
            reads=[bpj], writes=[st.buf], dma=True, tag="Tpj",
            lane="B_stage%d" % (fc % 2)),
        ctx.op("act", lambda s, fc=fc, st=st: s.dma_start(
            wprojT_s[fc, :, :], st.ap[:, 0:1024]),
            reads=[st.buf], writes=[bpjT], dma=True, tag="stpj", lane="wprojT_s")
    return (wfcT_s, bfcT), (wprojT_s, bpjT)


def emit_phaseB(K, wfcT, wprojT):
    ctx = K.ctx
    C = K.C
    (wfcT_s, bfcT), (wprojT_s, bpjT) = wfcT, wprojT
    P0, P1, P2, P3 = K.P0, K.P1, K.P2, K.P3
    PTf = TBuf(K.PT.ap.bitcast(F32), K.PT.buf)
    TBLK, NB, GRP = 512, 8, 2
    xTb = K.sb("B_xTb", [128, 8 * TBLK], BF16)
    wfc_g = [K.sb("B_wfcg%d" % i, [128, GRP * 8 * 128], BF16) for i in range(2)]
    wpj_g = [K.sb("B_wpjg%d" % i, [128, GRP * 1024], BF16) for i in range(2)]
    hbT_full = K.sb("B_hbTf", [128, 32 * TBLK], BF16)
    hbT_t = K.carve(hbT_full, 32, TBLK)
    rb = K.sb("B_rb", [128, TBLK], BF16)
    yblk2 = [K.sb("B_yblk%d" % i, [128, 4 * 1024], F32) for i in range(2)]
    yblk_t2 = [K.carve(y, 4, 1024) for y in yblk2]
    mem = {e: K.sb("B_mem" + e, [128, 1024], F32) for e in ("dve", "pool")}
    spk = {e: K.sb("B_spk" + e, [128, 1024], F32) for e in ("dve", "pool")}
    ssum = {e: K.sb("B_ssum" + e, [128, 1024], F32) for e in ("dve", "pool")}
    scr = K.sb("B_scr", [128, 1024], F32)
    ybank = [TBuf(P0.ap[:, 0:512], P0.buf), TBuf(P0.ap[:, 512:1024], P0.buf),
             TBuf(P1.ap[:, 0:512], P1.buf), TBuf(P1.ap[:, 512:1024], P1.buf),
             TBuf(P2.ap[:, 0:512], P2.buf), TBuf(P2.ap[:, 512:1024], P2.buf),
             TBuf(P3.ap[:, 0:512], P3.buf), TBuf(PTf.ap[:, 0:512], K.PT.buf)]

    def emit_block(b):
        for g in range(4):
            tile = K.str_tiles[4 * b + g]
            K.transpose8(K.STR, 1024 * (4 * b + g), xTb,
                         dst_view_fn=lambda ap, g=g: ap.rearrange(
                             "p (dc w) -> p dc w", w=TBLK)[:, :, 128 * g:128 * (g + 1)],
                         rbuf=tile.buf)
        hrot = [TBuf(P3.ap[:, 0:512], P3.buf), TBuf(P2.ap[:, 0:512], P2.buf),
                TBuf(P2.ap[:, 512:1024], P2.buf)]
        for fcg in range(32 // GRP):
            wf = wfc_g[fcg % 2]
            ctx.op("sp", lambda s, fcg=fcg, wf=wf: s.dma_start(
                wf.ap.rearrange("p (fc q) -> p fc q", q=8 * 128),
                wfcT_s[GRP * fcg:GRP * (fcg + 1), :, :, :].rearrange(
                    "fc p dc w -> p fc (dc w)")),
                reads=[bfcT], writes=[wf.buf], dma=True, tag="ld_wfc")
            drain(4)
            for fci in range(GRP):
                fc = GRP * fcg + fci
                hps = hrot[fc % 3]
                for dc in range(8):
                    ctx.op("pe", lambda t, fci=fci, dc=dc, wf=wf, hps=hps: t.matmul(
                        hps.ap, wf.ap[:, (fci * 8 + dc) * 128:(fci * 8 + dc + 1) * 128],
                        xTb.ap[:, TBLK * dc:TBLK * (dc + 1)],
                        start=(dc == 0), stop=(dc == 7)),
                        reads=[wf.buf, xTb.buf], writes=[hps.buf], tag="mmh")
                ctx.op("act", lambda s, hps=hps: s.activation(rb.ap, hps.ap, AF.Relu),
                       reads=[hps.buf], writes=[rb.buf], tag="relu")
                ctx.op("dve", lambda v, fc=fc: v.tensor_mul(
                    hbT_t[fc].ap, rb.ap, rb.ap),
                    reads=[rb.buf], writes=[hbT_t[fc].buf], tag="sq")
        for fcg in range(32 // GRP):
            wp = wpj_g[fcg % 2]
            ctx.op("sp", lambda s, fcg=fcg, wp=wp: s.dma_start(
                wp.ap.rearrange("p (fc q) -> p fc q", q=1024),
                wprojT_s[GRP * fcg:GRP * (fcg + 1), :, :].rearrange(
                    "fc p q -> p fc q")),
                reads=[bpjT], writes=[wp.buf], dma=True, tag="ld_wpj")
            drain(4)
            for fci in range(GRP):
                fc = GRP * fcg + fci
                for g in range(4):
                    for half in range(2):
                        yb = ybank[2 * g + half]
                        ctx.op("pe", lambda t, fci=fci, fc=fc, g=g, half=half, yb=yb, wp=wp: t.matmul(
                            yb.ap, hbT_t[fc].ap[:, 128 * g:128 * (g + 1)],
                            wp.ap[:, 1024 * fci + 512 * half:1024 * fci + 512 * (half + 1)],
                            start=(fc == 0), stop=(fc == 31)),
                            reads=[hbT_t[fc].buf, wp.buf], writes=[yb.buf], tag="mmy")
        yblk_t = yblk_t2[b % 2]
        for g in range(4):
            for half in range(2):
                yb = ybank[2 * g + half]
                ctx.op("act", lambda s, g=g, half=half, yb=yb, yblk_t=yblk_t: s.copy(
                    yblk_t[g].ap[:, 512 * half:512 * (half + 1)], yb.ap),
                    reads=[yb.buf], writes=[yblk_t[g].buf], tag="yevac")
    pending = []

    def drain(n):
        for _ in range(n):
            if pending:
                pending.pop(0)()

    def scan_block(bs):
        TH = pending.append
        negone = K.constcol(-1.0)
        yblk_t = yblk_t2[bs % 2]
        for g in range(4):
            e = "dve" if g % 2 == 0 else "pool"
            y = yblk_t[g]
            me, sp_, su = mem[e], spk[e], ssum[e]
            tile = K.str_tiles[4 * bs + g]
            c = 4 * bs + g
            TH(lambda y=y, sp_=sp_: ctx.op("act", lambda s, y=y, sp_=sp_: s.activation(
                sp_.ap, y.ap, AF.Sign, bias=negone.ap),
                reads=[y.buf, negone.buf], writes=[sp_.buf], tag="scan_sg"))
            TH(lambda sp_=sp_: ctx.op("act", lambda s, sp_=sp_: s.activation(sp_.ap, sp_.ap, AF.Relu),
                   reads=[sp_.buf], writes=[sp_.buf], tag="scan_rl"))
            TH(lambda sp_=sp_, su=su: ctx.op("dve", lambda v, sp_=sp_, su=su: v.tensor_copy(su.ap, sp_.ap),
                   reads=[sp_.buf], writes=[su.buf], tag="scan_su"))
            for step in range(4):
                TH(lambda y=y, sp_=sp_: ctx.op("dve", lambda v, y=y, sp_=sp_: v.tensor_sub(
                    scr.ap, y.ap, sp_.ap),
                    reads=[y.buf, sp_.buf], writes=[scr.buf], tag="scan_t"))
                if step == 0:
                    TH(lambda me=me, y=y: ctx.op("dve", lambda v, me=me, y=y: v.scalar_tensor_tensor(
                        me.ap, y.ap, 0.9, scr.ap, ALU.mult, ALU.add),
                        reads=[y.buf, scr.buf], writes=[me.buf], tag="scan_m0"))
                else:
                    TH(lambda me=me: ctx.op("dve", lambda v, me=me: v.scalar_tensor_tensor(
                        me.ap, me.ap, 0.9, scr.ap, ALU.mult, ALU.add),
                        reads=[me.buf, scr.buf], writes=[me.buf], tag="scan_m"))
                TH(lambda me=me, sp_=sp_: ctx.op("act", lambda s, me=me, sp_=sp_: s.activation(
                    sp_.ap, me.ap, AF.Sign, bias=negone.ap),
                    reads=[me.buf, negone.buf], writes=[sp_.buf], tag="scan_sg2"))
                TH(lambda sp_=sp_: ctx.op("act", lambda s, sp_=sp_: s.activation(sp_.ap, sp_.ap, AF.Relu),
                       reads=[sp_.buf], writes=[sp_.buf], tag="scan_rl2"))
                TH(lambda su=su, sp_=sp_: ctx.op("dve", lambda v, su=su, sp_=sp_: v.tensor_add(
                    su.ap, su.ap, sp_.ap),
                    reads=[su.buf, sp_.buf], writes=[su.buf], tag="scan_su2"))
            TH(lambda su=su, tile=tile, c=c: ctx.op("dve", lambda v, su=su, tile=tile, c=c: v.scalar_tensor_tensor(
                tile.ap, su.ap, 0.2, tile.ap, ALU.mult, ALU.add,
                accum_out=K.stats_s_t[c].ap),
                reads=[su.buf, tile.buf],
                writes=[tile.buf, K.stats_s_t[c].buf], tag="residB"))
            TH(lambda tile=tile, c=c: ctx.op("act", lambda s, tile=tile, c=c: s.activation(
                scr.ap, tile.ap, AF.Square, accum_out=K.stats_q_t[c].ap),
                reads=[tile.buf], writes=[scr.buf, K.stats_q_t[c].buf], tag="sqdumpB"))

    for b in range(NB):
        emit_block(b)
        scan_block(b)
    drain(1000)


def build_launch2():
    inputs = {
        "x": ((T, D), F32),
        "L0_ffn_W_fc": ((4096, D), F32),
        "L0_ffn_W_proj": ((D, 4096), F32),
    }
    outputs = {"stream2": ((T, D), F32)}
    k = K(inputs, outputs)
    k.add_const_inputs(["ident", "ones_b", "ones_f", "onescol_b", "one1_f"])
    k.alloc_stream()
    k.load_stream_from_dram(k.din["x"], cast=True)
    wfcT, wprojT = emit_prepB(k, emit_castsB(k))
    emit_phaseB(k, wfcT, wprojT)
    emit_ln_sweep(k)
    k.store_stream_to_dram_f32(k.dout["stream2"])
    k.ctx.finalize()
    return k



JCW = [128] * 21 + [42]
JCO = [0]
for _w in JCW[:-1]:
    JCO.append(JCO[-1] + _w)
HID = 2730


def emit_batch_rstd(K, mean, var, rstd, eps):
    ctx = K.ctx
    ctx.op("dve", lambda v: v.tensor_scalar_mul(mean.ap, mean.ap, 1.0 / D),
           reads=[mean.buf], writes=[mean.buf], tag="br_mean")
    ctx.op("dve", lambda v: v.tensor_scalar_mul(var.ap, var.ap, 1.0 / D),
           reads=[var.buf], writes=[var.buf], tag="br_v")
    ctx.op("dve", lambda v: v.tensor_mul(rstd.ap, mean.ap, mean.ap),
           reads=[mean.buf], writes=[rstd.buf], tag="br_m2")
    ctx.op("dve", lambda v: v.tensor_sub(var.ap, var.ap, rstd.ap),
           reads=[var.buf, rstd.buf], writes=[var.buf], tag="br_vm")
    epsc = K.constcol(eps)
    ctx.op("act", lambda s: s.activation(var.ap, var.ap, AF.Ln, bias=epsc.ap),
           reads=[var.buf, epsc.buf], writes=[var.buf], tag="br_ln")
    ctx.op("act", lambda s: s.activation(rstd.ap, var.ap, AF.Exp, scale=-0.5),
           reads=[var.buf], writes=[rstd.buf], tag="br_exp")


def emit_stats_presweep(K, mean, var):
    ctx = K.ctx
    scr = K.sb("psw_scr", [128, 1024], F32)
    for c in range(NCH):
        tile = K.str_tiles[c]
        ctx.op("dve", lambda v, c=c, tile=tile: v.tensor_reduce(
            mean.ap[:, c:c + 1], tile.ap, AX.X, ALU.add),
            reads=[tile.buf], writes=[mean.buf], tag="psw_s")
        ctx.op("act", lambda s, c=c, tile=tile: s.activation(
            scr.ap, tile.ap, AF.Square, accum_out=var.ap[:, c:c + 1]),
            reads=[tile.buf], writes=[scr.buf, var.buf], tag="psw_q")


def emit_castsC(K):
    nc, ctx = K.nc, K.ctx
    wo_bf = nc.dram_tensor("C_wo_bf", [1024, 1024], BF16, kind="Internal").ap()
    bwo = Buf("C_wo_bf")
    ctx.op("pool", lambda g: g.dma_start(wo_bf, K.din["L1_attn_wo"]),
           writes=[bwo], dma=True, tag="castwo")
    wout_bf = nc.dram_tensor("C_wout_bf", [2730, 1024], BF16, kind="Internal").ap()
    bwout = Buf("C_wout_bf")
    ctx.op("pool", lambda g: g.dma_start(wout_bf, K.din["L1_attn_w_out"]),
           writes=[bwout], dma=True, tag="castwout")
    K._castsC = ((wo_bf, bwo), (wout_bf, bwout))


def emit_prepC(K):
    nc, ctx = K.nc, K.ctx
    res = {}
    cols = {}
    for nm in ("ln1_w", "ln1_b", "ln2_w", "ln2_b"):
        rb = K.sb("C_row_" + nm, [1, 1024], BF16)
        ctx.op("pool", lambda g, rb=rb, nm=nm: g.dma_start(
            rb.ap, K.din["L1_attn_" + nm].rearrange("(a f) -> a f", a=1)),
            writes=[rb.buf], dma=True, tag="ldrow")
        col = K.sb("C_col_" + nm, [128, 8], F32)
        for dc in range(8):
            ctx.op("pe", lambda t, rb=rb, dc=dc: t.transpose(
                K.PT.ap[:, 2 * dc:2 * dc + 1], rb.ap[:, 128 * dc:128 * (dc + 1)],
                K.C["one1_b"].ap),
                reads=[rb.buf, K.C["one1_b"].buf], writes=[K.PT.buf], tag="Trow")
        ctx.op("dve", lambda v, col=col: v.tensor_copy(
            col.ap.unsqueeze(2),
            K.PT.ap[:, 0:16].rearrange("p (a b) -> p a b", b=2)[:, :, 0:1]),
            reads=[K.PT.buf], writes=[col.buf], tag="Trowcp")
        cols[nm] = col
    wtile = K.sb("C_wtile", [128, 1024], F32)
    wtileb = K.sb("C_wtileb", [128, 1024], BF16)
    bcol = K.sb("C_bcol", [128, 1], F32)
    for nm, scale in (("wq", HD ** -0.5), ("wk", 1.0), ("wv", 1.0), ("wg", 1.0)):
        dst = nc.dram_tensor("C_%s_f" % nm, [1024, 1024], BF16, kind="Internal").ap()
        bdst = Buf("C_%s_f" % nm)
        brow = K.sb("C_brow_" + nm, [1, 1024], BF16)
        src = K.din["L1_attn_" + nm]
        for dc in range(8):
            ctx.op("sp", lambda s, src=src, dc=dc: s.dma_start(
                wtile.ap, src[128 * dc:128 * (dc + 1), :]),
                writes=[wtile.buf], dma=True, tag="ldw")
            ctx.op("dve", lambda v, dc=dc, scale=scale: v.tensor_scalar(
                wtileb.ap, wtile.ap, cols["ln1_w"].ap[:, dc:dc + 1], scale,
                ALU.mult, ALU.mult),
                reads=[wtile.buf, cols["ln1_w"].buf], writes=[wtileb.buf], tag="fold")
            ctx.op("sp", lambda s, dst=dst, dc=dc: s.dma_start(
                dst[128 * dc:128 * (dc + 1), :], wtileb.ap),
                reads=[wtileb.buf], writes=[bdst], dma=True, tag="stw")
            ctx.op("dve", lambda v, dc=dc, scale=scale: v.tensor_scalar(
                bcol.ap, cols["ln1_b"].ap[:, dc:dc + 1], scale, None,
                ALU.mult, ALU.bypass),
                reads=[cols["ln1_b"].buf], writes=[bcol.buf], tag="bcol")
            ctx.op("pe", lambda t, dc=dc: t.matmul(
                K.P3.ap[0:1, 0:512], bcol.ap, wtile.ap[:, 0:512],
                start=(dc == 0), stop=(dc == 7)),
                reads=[bcol.buf, wtile.buf], writes=[K.P3.buf], tag="mmbr0")
            ctx.op("pe", lambda t, dc=dc: t.matmul(
                K.P2.ap[0:1, 0:512], bcol.ap, wtile.ap[:, 512:1024],
                start=(dc == 0), stop=(dc == 7)),
                reads=[bcol.buf, wtile.buf], writes=[K.P2.buf], tag="mmbr1")
        ctx.op("dve", lambda v, brow=brow: v.tensor_copy(
            brow.ap[:, 0:512], K.P3.ap[0:1, 0:512]),
            reads=[K.P3.buf], writes=[brow.buf], tag="brcp0")
        ctx.op("dve", lambda v, brow=brow: v.tensor_copy(
            brow.ap[:, 512:1024], K.P2.ap[0:1, 0:512]),
            reads=[K.P2.buf], writes=[brow.buf], tag="brcp1")
        brow_d = nc.dram_tensor("C_brow_%s_d" % nm, [1, 1024], BF16,
                                kind="Internal").ap()
        bbrd = Buf("C_brow_%s_d" % nm)
        ctx.op("sp", lambda s, brow=brow, brow_d=brow_d: s.dma_start(brow_d, brow.ap),
               reads=[brow.buf], writes=[bbrd], dma=True, tag="stbrow")
        res[nm] = (dst, bdst, (brow_d, bbrd))
    win_f = nc.dram_tensor("C_win_f", [1024, 5460], BF16, kind="Internal").ap()
    bwin = Buf("C_win_f")
    wint = K.sb("C_wint", [128, 5460], F32)
    wintb = K.sb("C_wintb", [128, 5460], BF16)
    browf = K.sb("C_browf", [1, 5460], F32)
    ctx.op("dve", lambda v: v.memset(browf.ap, 0.0), writes=[browf.buf])
    NSL = 11
    for dc in range(8):
        ctx.op("sp", lambda s, dc=dc: s.dma_start(
            wint.ap, K.din["L1_attn_w_in"][128 * dc:128 * (dc + 1), :]),
            writes=[wint.buf], dma=True, tag="ldwin")
        ctx.op("dve", lambda v, dc=dc: v.tensor_scalar_mul(
            wintb.ap, wint.ap, cols["ln2_w"].ap[:, dc:dc + 1]),
            reads=[wint.buf, cols["ln2_w"].buf], writes=[wintb.buf], tag="foldin")
        ctx.op("sp", lambda s, dc=dc: s.dma_start(
            win_f[128 * dc:128 * (dc + 1), :], wintb.ap),
            reads=[wintb.buf], writes=[bwin], dma=True, tag="stwin")
        for sl in range(NSL):
            o = 512 * sl
            w = min(512, 5460 - o)
            ctx.op("pe", lambda t, dc=dc, o=o, w=w: t.matmul(
                K.P3.ap[0:1, 0:w], cols["ln2_b"].ap[:, dc:dc + 1],
                wint.ap[:, o:o + w], start=True, stop=True),
                reads=[cols["ln2_b"].buf, wint.buf], writes=[K.P3.buf], tag="mmbin")
            ctx.op("dve", lambda v, o=o, w=w: v.tensor_add(
                browf.ap[:, o:o + w], browf.ap[:, o:o + w], K.P3.ap[0:1, 0:w]),
                reads=[K.P3.buf, browf.buf], writes=[browf.buf], tag="accbin")
    binrow_f = K.sb("C_binrow_f", [1, 5460], F32)
    ctx.op("pool", lambda g: g.dma_start(
        binrow_f.ap, K.din["L1_attn_b_in"].rearrange("(a f) -> a f", a=1)),
        writes=[binrow_f.buf], dma=True, tag="ldbin")
    brow_in = K.sb("C_brow_in", [1, 5460], BF16)
    ctx.op("dve", lambda v: v.tensor_add(brow_in.ap, browf.ap, binrow_f.ap),
           reads=[browf.buf, binrow_f.buf], writes=[brow_in.buf], tag="binrow")
    brin_d = nc.dram_tensor("C_brow_in_d", [1, 5460], BF16, kind="Internal").ap()
    bbrin = Buf("C_brow_in_d")
    ctx.op("sp", lambda s: s.dma_start(brin_d, brow_in.ap),
           reads=[brow_in.buf], writes=[bbrin], dma=True, tag="stbrin")
    res["w_in"] = (win_f, bwin, (brin_d, bbrin))
    (wo_bf, bwo), (wout_bf, bwout) = K._castsC
    bout_d = nc.dram_tensor("C_bout_d", [1, 1024], BF16, kind="Internal").ap()
    bbout = Buf("C_bout_d")
    ctx.op("pool", lambda g: g.dma_start(
        bout_d, K.din["L1_attn_b_out"].rearrange("(a f) -> a f", a=1)),
        writes=[bbout], dma=True, tag="ldbout")
    res["wo"] = (wo_bf, bwo, None)
    res["w_out"] = (wout_bf, bwout, None)
    res["b_out"] = (bout_d, bbout)
    return res


def emit_phaseC2(K, prep, m1, rstd1, nch=NCH):
    ctx = K.ctx
    C = K.C
    P0, P1, P2, P3 = K.P0, K.P1, K.P2, K.P3
    W = {}
    for nm in ("wq", "wk", "wv", "wg"):
        w = K.sb("C2_" + nm, [128, 8 * 1024], BF16)
        dst, bdst, (brow_d, bbrd) = prep[nm]
        ctx.op("sp", lambda s, w=w, dst=dst: s.dma_start(
            w.ap.rearrange("p (c f) -> p c f", c=8),
            dst.rearrange("(c p) f -> p c f", p=128)),
            reads=[bdst], writes=[w.buf], dma=True, tag="ldwf")
        brow = K.sb("C2_brow_" + nm, [1, 1024], BF16)
        ctx.op("sp", lambda s, brow=brow, brow_d=brow_d: s.dma_start(
            brow.ap, brow_d),
            reads=[bbrd], writes=[brow.buf], dma=True, tag="ldbrow")
        W[nm] = (w, brow)
    wo = K.sb("C2_wo", [128, 8 * 1024], BF16)
    ctx.op("sp", lambda s: s.dma_start(
        wo.ap.rearrange("p (c f) -> p c f", c=8),
        prep["wo"][0].rearrange("(c p) f -> p c f", p=128)),
        reads=[prep["wo"][1]], writes=[wo.buf], dma=True, tag="ldwo")

    xnb = K.sb("C2_xnb", [128, 1024], BF16)
    xnT = K.sb("C2_xnT", [128, 8 * 128], BF16)
    qb = K.sb("C2_qb", [128, 1024], BF16)
    kb = K.sb("C2_kb", [128, 1024], BF16)
    vb = K.sb("C2_vb", [128, 1024], BF16)
    gb = K.sb("C2_gb", [128, 1024], BF16)
    q2b = K.sb("C2_q2b", [128, 1024], BF16)
    k3b = K.sb("C2_k3b", [128, 1024], BF16)
    gateb = K.sb("C2_gateb", [128, 1024], BF16)
    qT = K.sb("C2_qT", [64, 2048], BF16)
    q2T = K.sb("C2_q2T", [64, 2048], BF16)
    kT = K.sb("C2_kT", [64, 2048], BF16)
    scm = K.sb("C2_scm", [128, 1024], BF16)
    aob = K.sb("C2_aob", [128, 1024], BF16)
    aoT = K.sb("C2_aoT", [128, 8 * 128], BF16)
    scr = K.sb("C2_scr", [128, 1024], F32)
    Sf = K.sb("C2_Sf", [64, 1024], F32)
    Ssb = K.sb("C2_Ssb", [64, 1024], BF16)
    f16 = {nm: K.sb("C2_f_" + nm, [128, 16], F32)
           for nm in ("mu", "sq", "var", "rstd")}
    ctx.op("dve", lambda v: v.memset(Sf.ap, 0.0), writes=[Sf.buf])
    ctx.op("dve", lambda v: v.memset(Ssb.ap, 0.0), writes=[Ssb.buf])

    for c in range(nch):
        tile = K.str_tiles[c]
        ctx.op("dve", lambda v, c=c, tile=tile: v.tensor_scalar(
            xnb.ap, tile.ap, m1.ap[:, c:c + 1], rstd1.ap[:, c:c + 1],
            ALU.subtract, ALU.mult),
            reads=[tile.buf, m1.buf, rstd1.buf], writes=[xnb.buf], tag="ln1")
        K.transpose8(xnb, 0, xnT)
        K.proj_tok(P0, xnT, W["wq"][0], W["wq"][1])
        ctx.op("act", lambda s: s.copy(qb.ap, P0.ap),
               reads=[P0.buf], writes=[qb.buf], tag="evq")
        K.proj_tok(P1, xnT, W["wk"][0], W["wk"][1])
        ctx.op("act", lambda s: s.copy(kb.ap, P1.ap),
               reads=[P1.buf], writes=[kb.buf], tag="evk")
        K.proj_tok(P0, xnT, W["wv"][0], W["wv"][1])
        ctx.op("act", lambda s: s.copy(vb.ap, P0.ap),
               reads=[P0.buf], writes=[vb.buf], tag="evv")
        K.proj_tok(P1, xnT, W["wg"][0], W["wg"][1])
        ctx.op("act", lambda s: s.copy(gb.ap, P1.ap),
               reads=[P1.buf], writes=[gb.buf], tag="evg")
        if c == 0:
            K.debug_dump("xnb", xnb)
            K.debug_dump("qb", qb)
            K.debug_dump("kb", kb)
            K.debug_dump("vb", vb)
            K.debug_dump("gb", gb)
        ctx.op("act", lambda s: s.activation(scr.ap, gb.ap, AF.Exp, scale=-1.0),
               reads=[gb.buf], writes=[scr.buf], tag="gexp")
        ctx.op("dve", lambda v: v.tensor_scalar_add(scr.ap, scr.ap, 1.0),
               reads=[scr.buf], writes=[scr.buf], tag="g1")
        ctx.op("dve", lambda v: v.reciprocal(scr.ap, scr.ap),
               reads=[scr.buf], writes=[scr.buf], tag="grec")
        ctx.op("dve", lambda v: v.tensor_mul(gateb.ap, gb.ap, scr.ap),
               reads=[gb.buf, scr.buf], writes=[gateb.buf], tag="gmul")
        if c == 0:
            K.debug_dump("gateb", gateb)
        ctx.op("dve", lambda v: v.tensor_tensor(
            as3d(q2b.ap), as3d(qb.ap), bc16(C["q2s"].ap), ALU.mult),
            reads=[qb.buf, C["q2s"].buf], writes=[q2b.buf], tag="q2")
        ctx.op("dve", lambda v: v.tensor_tensor(
            as3d(k3b.ap), as3d(kb.ap), bc16(C["k3s"].ap), ALU.mult),
            reads=[kb.buf, C["k3s"].buf], writes=[k3b.buf], tag="k3")
        K.transpose_heads(qb, qT)
        K.transpose_heads(q2b, q2T)
        K.transpose_heads(kb, kT)
        for grp in range(2):
            for hh in range(8):
                h = grp * 8 + hh
                ctx.op("pe", lambda t, hh=hh, h=h: t.matmul(
                    P1.ap[:, 128 * hh:128 * (hh + 1)],
                    kT.ap[0:64, 128 * h:128 * (h + 1)],
                    qT.ap[0:64, 128 * h:128 * (h + 1)],
                    start=True, stop=True),
                    reads=[kT.buf, qT.buf], writes=[P1.buf], tag="rsc")
            ctx.op("dve", lambda v, grp=grp: v.tensor_tensor(
                scm.ap.rearrange("p (h w) -> p h w", w=128),
                P1.ap.rearrange("p (h w) -> p h w", w=128),
                C["D16"].ap[:, 1024 * grp:1024 * (grp + 1)].rearrange(
                    "p (h w) -> p h w", w=128),
                ALU.mult),
                reads=[P1.buf, C["D16"].buf], writes=[scm.buf], tag="rmask")
            for hh in range(8):
                h = grp * 8 + hh
                ctx.op("pe", lambda t, h=h, hh=hh, c=c: t.matmul(
                    P0.ap[:, 64 * h:64 * (h + 1)],
                    scm.ap[:, 128 * hh:128 * (hh + 1)],
                    vb.ap[:, 64 * h:64 * (h + 1)], start=True, stop=(c == 0)),
                    reads=[scm.buf, vb.buf], writes=[P0.buf], tag="rintra")
                if c > 0:
                    ctx.op("pe", lambda t, h=h: t.matmul(
                        P0.ap[:, 64 * h:64 * (h + 1)],
                        q2T.ap[0:64, 128 * h:128 * (h + 1)],
                        Ssb.ap[0:64, 64 * h:64 * (h + 1)], start=False, stop=True),
                        reads=[q2T.buf, Ssb.buf], writes=[P0.buf], tag="rinter")
        for h in range(H):
            ctx.op("pe", lambda t, h=h: t.matmul(
                P2.ap[0:64, 64 * h:64 * (h + 1)],
                k3b.ap[:, 64 * h:64 * (h + 1)],
                vb.ap[:, 64 * h:64 * (h + 1)], start=True, stop=True),
                reads=[k3b.buf, vb.buf], writes=[P2.buf], tag="rupd")
        ctx.op("dve", lambda v: v.tensor_tensor(
            Sf.ap.rearrange("p (h d) -> p h d", d=64),
            Sf.ap.rearrange("p (h d) -> p h d", d=64),
            C["gCt"].ap.unsqueeze(2).broadcast_to([64, H, 64]),
            ALU.mult),
            reads=[Sf.buf, C["gCt"].buf], writes=[Sf.buf], tag="sdecay")
        ctx.op("dve", lambda v: v.tensor_add(Sf.ap, Sf.ap, P2.ap[0:64, :]),
               reads=[Sf.buf, P2.buf], writes=[Sf.buf], tag="sadd")
        ctx.op("dve", lambda v: v.tensor_copy(Ssb.ap, Sf.ap),
               reads=[Sf.buf], writes=[Ssb.buf], tag="scp")
        ctx.op("dve", lambda v: v.tensor_reduce(
            f16["mu"].ap, as3d(P0.ap), AX.X, ALU.add),
            reads=[P0.buf], writes=[f16["mu"].buf], tag="gn_s")
        ctx.op("dve", lambda v: v.tensor_scalar_mul(
            f16["mu"].ap, f16["mu"].ap, 1.0 / 64.0),
            reads=[f16["mu"].buf], writes=[f16["mu"].buf])
        ctx.op("dve", lambda v: v.tensor_tensor(
            as3d(scr.ap), as3d(P0.ap), bc16(f16["mu"].ap), ALU.subtract),
            reads=[P0.buf, f16["mu"].buf], writes=[scr.buf], tag="gn_sub")
        ctx.op("act", lambda s: s.activation(scm.ap, scr.ap, AF.Square),
               reads=[scr.buf], writes=[scm.buf], tag="gn_x2")
        ctx.op("dve", lambda v: v.tensor_reduce(
            f16["var"].ap, as3d(scm.ap), AX.X, ALU.add),
            reads=[scm.buf], writes=[f16["var"].buf], tag="gn_q")
        ctx.op("dve", lambda v: v.tensor_scalar_mul(
            f16["var"].ap, f16["var"].ap, 1.0 / 64.0),
            reads=[f16["var"].buf], writes=[f16["var"].buf])
        gnc = K.constcol(EPS_GN)
        ctx.op("act", lambda s: s.activation(
            f16["var"].ap, f16["var"].ap, AF.Ln, bias=gnc.ap),
            reads=[f16["var"].buf, gnc.buf], writes=[f16["var"].buf], tag="gn_ln")
        ctx.op("act", lambda s: s.activation(
            f16["rstd"].ap, f16["var"].ap, AF.Exp, scale=-0.5),
            reads=[f16["var"].buf], writes=[f16["rstd"].buf], tag="gn_exp")
        ctx.op("dve", lambda v: v.tensor_tensor(
            as3d(scr.ap), as3d(scr.ap), bc16(f16["rstd"].ap), ALU.mult),
            reads=[scr.buf, f16["rstd"].buf], writes=[scr.buf], tag="gn_mul")
        ctx.op("dve", lambda v: v.tensor_mul(aob.ap, scr.ap, gateb.ap),
               reads=[scr.buf, gateb.buf], writes=[aob.buf], tag="gn_gate")
        if c == 0:
            K.debug_dump("aob", aob)
            K.debug_dump("rstdgn", f16["rstd"], shape=[128, 16])
            K.debug_dump("mugn", f16["mu"], shape=[128, 16])
        K.transpose8(aob, 0, aoT)
        K.proj_tok(P1, aoT, wo, None)
        ctx.op("dve", lambda v, c=c, tile=tile: v.scalar_tensor_tensor(
            tile.ap, P1.ap, 0.0, tile.ap, ALU.add, ALU.add,
            accum_out=K.stats_s_t[c].ap),
            reads=[P1.buf, tile.buf],
            writes=[tile.buf, K.stats_s_t[c].buf], tag="residC2")
        ctx.op("act", lambda s, c=c, tile=tile: s.activation(
            scr.ap, tile.ap, AF.Square, accum_out=K.stats_q_t[c].ap),
            reads=[tile.buf], writes=[scr.buf, K.stats_q_t[c].buf], tag="sqdC2")


def emit_phaseC3(K, prep, m2, rstd2, r_src_ap, r_cast, r_buf=None):
    ctx = K.ctx
    C = K.C
    P0, P1, P2, P3 = K.P0, K.P1, K.P2, K.P3
    TBLK, NB = 512, 8
    win_f, bwin, (brin_d, bbrin) = prep["w_in"]
    wout_bf, bwout, _ = prep["w_out"]
    bout_d, bbout = prep["b_out"]
    brow_in = K.sb("C3_brow_in", [1, 5460], BF16)
    ctx.op("sp", lambda s: s.dma_start(brow_in.ap, brin_d),
           reads=[bbrin], writes=[brow_in.buf], dma=True, tag="ldbrin")
    bout_row = K.sb("C3_bout", [1, 1024], BF16)
    ctx.op("sp", lambda s: s.dma_start(bout_row.ap, bout_d),
           reads=[bbout], writes=[bout_row.buf], dma=True, tag="ldbout2")
    xnb = K.sb("C3_xnb", [128, 1024], BF16)
    xnTb = K.sb("C3_xnTb", [128, 8 * TBLK], BF16)
    wina = [K.sb("C3_wina%d" % i, [128, 8 * 128], BF16) for i in range(2)]
    winb = [K.sb("C3_winb%d" % i, [128, 8 * 128], BF16) for i in range(2)]
    wout = [K.sb("C3_wout%d" % i, [128, 1024], BF16) for i in range(2)]
    gT = K.sb("C3_gT", [128, 22 * TBLK], BF16)
    gT_t = K.carve(gT, 22, TBLK)
    sb2 = [K.sb("C3_sb%d" % i, [128, TBLK], BF16) for i in range(2)]
    rt = [K.sb("C3_rt%d" % i, [128, 1024], BF16) for i in range(2)]
    yblk = K.sb("C3_yblk", [128, 4 * 1024], F32)
    yblk_t = K.carve(yblk, 4, 1024)
    scr = K.sb("C3_scr", [128, 1024], F32)
    r_view = r_src_ap.rearrange("(t p) f -> p t f", p=128)

    for b in range(NB):
        for g in range(4):
            c = 4 * b + g
            tile = K.str_tiles[c]
            ctx.op("dve", lambda v, c=c, tile=tile: v.tensor_scalar(
                xnb.ap, tile.ap, m2.ap[:, c:c + 1], rstd2.ap[:, c:c + 1],
                ALU.subtract, ALU.mult),
                reads=[tile.buf, m2.buf, rstd2.buf], writes=[xnb.buf], tag="ln2")
            K.transpose8(xnb, 0, xnTb,
                         dst_view_fn=lambda ap, g=g: ap.rearrange(
                             "p (dc w) -> p dc w", w=TBLK)[:, :, 128 * g:128 * (g + 1)])
        aslots = [TBuf(P3.ap[:, 0:512], P3.buf), TBuf(P2.ap[:, 0:512], P2.buf)]
        bslots = [TBuf(P2.ap[:, 512:1024], P2.buf), TBuf(P0.ap[:, 0:512], P0.buf)]
        for jc in range(22):
            j0, w = JCO[jc], JCW[jc]
            wa, wb_ = wina[jc % 2], winb[jc % 2]
            aps, bps = aslots[jc % 2], bslots[jc % 2]
            ctx.op("sp", lambda s, j0=j0, w=w, wa=wa: s.dma_start(
                wa.ap.rearrange("p (dc u) -> p dc u", u=128)[:, :, 0:w],
                win_f.rearrange("(dc p) f -> p dc f", p=128)[:, :, j0:j0 + w]),
                reads=[bwin], writes=[wa.buf], dma=True, tag="ldwa")
            ctx.op("sp", lambda s, j0=j0, w=w, wb_=wb_: s.dma_start(
                wb_.ap.rearrange("p (dc u) -> p dc u", u=128)[:, :, 0:w],
                win_f.rearrange("(dc p) f -> p dc f", p=128)[:, :, HID + j0:HID + j0 + w]),
                reads=[bwin], writes=[wb_.buf], dma=True, tag="ldwb")
            for dc in range(8):
                ctx.op("pe", lambda t, dc=dc, w=w, wa=wa, aps=aps: t.matmul(
                    aps.ap[0:w, :], wa.ap[:, 128 * dc:128 * dc + w],
                    xnTb.ap[:, TBLK * dc:TBLK * (dc + 1)],
                    start=(dc == 0), stop=False),
                    reads=[wa.buf, xnTb.buf], writes=[aps.buf], tag="mma")
            ctx.op("pe", lambda t, j0=j0, w=w, aps=aps: t.matmul(
                aps.ap[0:w, :], brow_in.ap[:, j0:j0 + w],
                C["ones512_b"].ap[:, 0:TBLK], start=False, stop=True),
                reads=[brow_in.buf, C["ones512_b"].buf], writes=[aps.buf], tag="mmab")
            for dc in range(8):
                ctx.op("pe", lambda t, dc=dc, w=w, wb_=wb_, bps=bps: t.matmul(
                    bps.ap[0:w, :], wb_.ap[:, 128 * dc:128 * dc + w],
                    xnTb.ap[:, TBLK * dc:TBLK * (dc + 1)],
                    start=(dc == 0), stop=False),
                    reads=[wb_.buf, xnTb.buf], writes=[bps.buf], tag="mmb")
            ctx.op("pe", lambda t, j0=j0, w=w, bps=bps: t.matmul(
                bps.ap[0:w, :], brow_in.ap[:, HID + j0:HID + j0 + w],
                C["ones512_b"].ap[:, 0:TBLK], start=False, stop=True),
                reads=[brow_in.buf, C["ones512_b"].buf], writes=[bps.buf], tag="mmbb")
            sbv = sb2[jc % 2]
            ctx.op("act", lambda s, w=w, bps=bps, sbv=sbv: s.activation(
                sbv.ap[0:w, :], bps.ap[0:w, :], AF.Silu),
                reads=[bps.buf], writes=[sbv.buf], tag="silu")
            ctx.op("dve", lambda v, jc=jc, w=w, aps=aps, sbv=sbv: v.tensor_tensor(
                gT_t[jc].ap[0:w, :], aps.ap[0:w, :], sbv.ap[0:w, :],
                ALU.mult),
                reads=[aps.buf, sbv.buf], writes=[gT_t[jc].buf], tag="gmul3")
        for half in range(2):
            for jc in range(22):
                j0, w = JCO[jc], JCW[jc]
                wu = wout[jc % 2]
                ctx.op("sp", lambda s, j0=j0, w=w, wu=wu: s.dma_start(
                    wu.ap[0:w, :], wout_bf[j0:j0 + w, :]),
                    reads=[bwout], writes=[wu.buf], dma=True, tag="ldwu")
                for g in range(4):
                    yps = P0 if g < 2 else P1
                    yo = 512 * (g % 2)
                    ctx.op("pe", lambda t, jc=jc, g=g, half=half, yps=yps, yo=yo, w=w, wu=wu: t.matmul(
                        yps.ap[:, yo:yo + 512],
                        gT_t[jc].ap[0:w, 128 * g:128 * (g + 1)],
                        wu.ap[0:w, 512 * half:512 * (half + 1)],
                        start=(jc == 0), stop=False),
                        reads=[gT_t[jc].buf, wu.buf], writes=[yps.buf], tag="mmy3")
            for g in range(4):
                yps = P0 if g < 2 else P1
                yo = 512 * (g % 2)
                ctx.op("pe", lambda t, g=g, half=half, yps=yps, yo=yo: t.matmul(
                    yps.ap[:, yo:yo + 512], C["ones_b"].ap,
                    bout_row.ap[:, 512 * half:512 * (half + 1)],
                    start=False, stop=True),
                    reads=[C["ones_b"].buf, bout_row.buf], writes=[yps.buf],
                    tag="mmy3b")
                ctx.op("act", lambda s, g=g, half=half, yps=yps, yo=yo: s.copy(
                    yblk_t[g].ap[:, 512 * half:512 * (half + 1)],
                    yps.ap[:, yo:yo + 512]),
                    reads=[yps.buf], writes=[yblk_t[g].buf], tag="yev3")
        if b == 0:
            K.debug_dump("yblk0", yblk_t[0], shape=[128, 1024])
            K.debug_dump("gT0", gT_t[0], shape=[128, 512])
            K.debug_dump("xnTb0", xnTb, shape=[128, 4096])
        for g in range(4):
            c = 4 * b + g
            tile = K.str_tiles[c]
            rtb = rt[g % 2]
            eng = "pool" if r_cast else "sp"
            ctx.op(eng, lambda s, c=c, rtb=rtb: s.dma_start(
                rtb.ap, r_view[:, c, :]),
                reads=([r_buf] if r_buf is not None else []),
                writes=[rtb.buf], dma=True, tag="ldr")
            ctx.op("dve", lambda v, g=g, tile=tile: v.scalar_tensor_tensor(
                tile.ap, yblk_t[g].ap, 0.0, tile.ap, ALU.add, ALU.add),
                reads=[yblk_t[g].buf, tile.buf], writes=[tile.buf], tag="resC3a")
            ctx.op("dve", lambda v, tile=tile, rtb=rtb, c=c: v.scalar_tensor_tensor(
                tile.ap, rtb.ap, 0.0, tile.ap, ALU.add, ALU.add,
                accum_out=K.stats_s_t[c].ap),
                reads=[rtb.buf, tile.buf],
                writes=[tile.buf, K.stats_s_t[c].buf], tag="resC3b")
            ctx.op("act", lambda s, tile=tile, c=c: s.activation(
                scr.ap, tile.ap, AF.Square, accum_out=K.stats_q_t[c].ap),
                reads=[tile.buf], writes=[scr.buf, K.stats_q_t[c].buf], tag="sqdC3")


def build_launch3():
    inputs = {
        "x": ((T, D), F32),
        "L1_attn_ln1_w": ((D,), F32), "L1_attn_ln1_b": ((D,), F32),
        "L1_attn_wq": ((D, D), F32), "L1_attn_wk": ((D, D), F32),
        "L1_attn_wv": ((D, D), F32), "L1_attn_wg": ((D, D), F32),
        "L1_attn_wo": ((D, D), F32),
        "L1_attn_ln2_w": ((D,), F32), "L1_attn_ln2_b": ((D,), F32),
        "L1_attn_w_in": ((D, 5460), F32), "L1_attn_b_in": ((5460,), F32),
        "L1_attn_w_out": ((HID, D), F32), "L1_attn_b_out": ((D,), F32),
    }
    outputs = {"stream3": ((T, D), F32)}
    k = K(inputs, outputs)
    k.add_const_inputs(["ident", "D16", "q2s", "k3s", "gCt", "ones_b", "ones_f",
                        "onescol_b", "one1_f", "one1_b", "ones512_b"])
    k.alloc_stream()
    k.mk_arena(136 * 1024 - 512)
    k.load_stream_from_dram(k.din["x"], cast=True)
    m1 = k.sb("m1", [128, NCH], F32, perm=True)
    v1 = k.sb("v1", [128, NCH], F32, perm=True)
    rstd1 = k.sb("rstd1", [128, NCH], F32, perm=True)
    m2 = k.sb("m2", [128, NCH], F32, perm=True)
    rstd2 = k.sb("rstd2", [128, NCH], F32, perm=True)
    k.push_region()
    emit_castsC(k)
    prep = emit_prepC(k)
    emit_stats_presweep(k, m1, v1)
    k.pop_region()
    emit_batch_rstd(k, m1, v1, rstd1, EPS_LN)
    import os as _os
    _st3 = int(_os.environ.get("STAGE3", "9"))
    k.push_region()
    emit_phaseC2(k, prep, m1, rstd1, nch=min(NCH, 32 if _st3 >= 2 else int(_os.environ.get("NCH3", "32"))))
    k.pop_region()
    if _st3 < 2:
        k.store_stream_to_dram_f32(k.dout["stream3"])
        k.ctx.finalize()
        return k
    k.ctx.op("dve", lambda v: v.tensor_copy(m2.ap, k.stats_s.ap),
             reads=[b.buf for b in k.stats_s_t], writes=[m2.buf])
    v2 = k.stats_q
    emit_batch_rstd_from(k, m2, v2, rstd2, EPS_LN)
    k.debug_dump("m2", m2, shape=[128, NCH])
    k.debug_dump("rstd2", rstd2, shape=[128, NCH])
    k.push_region()
    emit_phaseC3(k, prep, m2, rstd2, k.din["x"], r_cast=True)
    k.pop_region()
    emit_ln_sweep(k)
    k.store_stream_to_dram_f32(k.dout["stream3"])
    k.ctx.finalize()
    return k


def emit_batch_rstd_from(K, mean, var_tb, rstd, eps):
    ctx = K.ctx
    qbufs = [b.buf for b in K.stats_q_t]
    ctx.op("dve", lambda v: v.tensor_scalar_mul(mean.ap, mean.ap, 1.0 / D),
           reads=[mean.buf], writes=[mean.buf], tag="b2_mean")
    ctx.op("dve", lambda v: v.tensor_scalar_mul(var_tb.ap, var_tb.ap, 1.0 / D),
           reads=qbufs + [var_tb.buf], writes=[var_tb.buf] + qbufs, tag="b2_v")
    ctx.op("dve", lambda v: v.tensor_mul(rstd.ap, mean.ap, mean.ap),
           reads=[mean.buf], writes=[rstd.buf], tag="b2_m2")
    ctx.op("dve", lambda v: v.tensor_sub(var_tb.ap, var_tb.ap, rstd.ap),
           reads=[var_tb.buf, rstd.buf], writes=[var_tb.buf] + qbufs, tag="b2_vm")
    epsc = K.constcol(eps)
    ctx.op("act", lambda s: s.activation(var_tb.ap, var_tb.ap, AF.Ln, bias=epsc.ap),
           reads=[var_tb.buf, epsc.buf], writes=[var_tb.buf] + qbufs, tag="b2_ln")
    ctx.op("act", lambda s: s.activation(rstd.ap, var_tb.ap, AF.Exp, scale=-0.5),
           reads=[var_tb.buf], writes=[rstd.buf], tag="b2_exp")


def emit_castsD(K):
    nc, ctx = K.nc, K.ctx
    out = {}
    for nm in ("W_r", "W_k", "W_v"):
        w_bf = nc.dram_tensor("D_%s_bf" % nm, [1024, 1024], BF16, kind="Internal").ap()
        bw = Buf("D_%s_bf" % nm)
        ctx.op("pool", lambda g, nm=nm, w_bf=w_bf: g.dma_start(
            w_bf, K.din["L1_ffn_" + nm]),
            writes=[bw], dma=True, tag="castD")
        out[nm] = (w_bf, bw)
    return out


def emit_prepD(K, castsD):
    nc, ctx = K.nc, K.ctx
    res = {}
    stage = K.sb("D_stage", [128, 1024], BF16)
    for nm in ("W_r", "W_k", "W_v"):
        w_bf, bw = castsD[nm]
        wT = nc.dram_tensor("D_%sT" % nm, [1024, 1024], BF16, kind="Internal").ap()
        bT = Buf("D_%sT" % nm)
        for dc in range(8):
            ctx.op("sp", lambda s, dc=dc, w_bf=w_bf: s.dma_start(
                stage.ap, w_bf[:, 128 * dc:128 * (dc + 1)], transpose=True),
                reads=[bw], writes=[stage.buf], dma=True, tag="TD", lane="D_stage")
            ctx.op("sp", lambda s, dc=dc, wT=wT: s.dma_start(
                wT[128 * dc:128 * (dc + 1), :], stage.ap),
                reads=[stage.buf], writes=[bT], dma=True, tag="stTD",
                lane="D_%sT" % nm)
        res[nm] = (wT, bT)
    for nm in ("mu_r", "mu_k"):
        rb = K.sb("D_row_" + nm, [1, 1024], BF16)
        ctx.op("pool", lambda g, rb=rb, nm=nm: g.dma_start(
            rb.ap, K.din["L1_ffn_" + nm].rearrange("(a f) -> a f", a=1)),
            writes=[rb.buf], dma=True, tag="ldmu")
        col = K.sb("D_col_" + nm, [128, 8], F32, perm=True)
        om = K.sb("D_om_" + nm, [128, 8], F32, perm=True)
        for dc in range(8):
            ctx.op("pe", lambda t, rb=rb, dc=dc: t.transpose(
                K.PT.ap[:, 2 * dc:2 * dc + 1], rb.ap[:, 128 * dc:128 * (dc + 1)],
                K.C["one1_b"].ap),
                reads=[rb.buf, K.C["one1_b"].buf], writes=[K.PT.buf], tag="Tmu")
        ctx.op("dve", lambda v, col=col: v.tensor_copy(
            col.ap.unsqueeze(2),
            K.PT.ap[:, 0:16].rearrange("p (a b) -> p a b", b=2)[:, :, 0:1]),
            reads=[K.PT.buf], writes=[col.buf], tag="Tmucp")
        ctx.op("dve", lambda v, col=col, om=om: v.tensor_scalar(
            om.ap, col.ap, -1.0, 1.0, ALU.mult, ALU.add),
            reads=[col.buf], writes=[om.buf], tag="ommu")
        res[nm] = (col, om)
    return res


def emit_phaseD(K, prepd):
    ctx = K.ctx
    C = K.C
    P0, P1, P2, P3 = K.P0, K.P1, K.P2, K.P3
    TBLK, NB = 512, 8
    W = {}
    for nm in ("W_r", "W_k", "W_v"):
        wT, bT = prepd[nm]
        w = K.sb("D_" + nm + "T", [128, 8 * 1024], BF16)
        ctx.op("sp", lambda s, w=w, wT=wT: s.dma_start(
            w.ap.rearrange("p (c f) -> p c f", c=8),
            wT.rearrange("(c p) f -> p c f", p=128)),
            reads=[bT], writes=[w.buf], dma=True, tag="ldWD")
        W[nm] = w
    murc, omrc = prepd["mu_r"]
    mukc, omkc = prepd["mu_k"]
    xTb = K.sb("D_xTb", [128, 8 * TBLK], BF16)
    xsh = K.sb("D_xsh", [128, 8 * TBLK], BF16)
    tdif = K.sb("D_tdif", [128, 8 * TBLK], BF16)
    xrT = K.sb("D_xrT", [128, 8 * TBLK], BF16)
    xkT = K.sb("D_xkT", [128, 8 * TBLK], BF16)
    kaT = K.sb("D_kaT", [128, 8 * TBLK], BF16)
    rb2 = K.sb("D_rb2", [128, TBLK], BF16)
    rblk = K.sb("D_rblk", [128, 4 * 1024], F32)
    rblk_t = K.carve(rblk, 4, 1024)
    scr = K.sb("D_scr", [128, 1024], F32)
    pv = K.sb("D_pv", [128, 8], BF16)
    v3 = lambda ap: ap.rearrange("p (dc w) -> p dc w", w=TBLK)
    ctx.op("dve", lambda v: v.memset(pv.ap, 0.0), writes=[pv.buf])

    for b in range(NB):
        if b > 0:
            ctx.op("dve", lambda v: v.tensor_copy(
                pv.ap.unsqueeze(2), v3(xTb.ap)[:, :, TBLK - 1:TBLK]),
                reads=[xTb.buf], writes=[pv.buf], tag="pvcp")
        for g in range(4):
            tile = K.str_tiles[4 * b + g]
            K.transpose8(K.STR, 1024 * (4 * b + g), xTb,
                         dst_view_fn=lambda ap, g=g: v3(ap)[:, :, 128 * g:128 * (g + 1)],
                         rbuf=tile.buf)
        ctx.op("dve", lambda v: v.tensor_copy(
            v3(xsh.ap)[:, :, 1:TBLK], v3(xTb.ap)[:, :, 0:TBLK - 1]),
            reads=[xTb.buf], writes=[xsh.buf], tag="shcp")
        ctx.op("dve", lambda v: v.tensor_copy(
            v3(xsh.ap)[:, :, 0:1], pv.ap.unsqueeze(2)),
            reads=[pv.buf], writes=[xsh.buf], tag="shcp0")
        ctx.op("dve", lambda v: v.tensor_sub(tdif.ap, xTb.ap, xsh.ap),
               reads=[xTb.buf, xsh.buf], writes=[tdif.buf], tag="tdif")
        for dst, mu in ((xrT, murc), (xkT, mukc)):
            ctx.op("dve", lambda v, dst=dst, mu=mu: v.tensor_tensor(
                v3(dst.ap), v3(tdif.ap),
                mu.ap.unsqueeze(2).broadcast_to([128, 8, TBLK]), ALU.mult),
                reads=[tdif.buf, mu.buf], writes=[dst.buf], tag="xmu")
            ctx.op("dve", lambda v, dst=dst: v.tensor_add(dst.ap, dst.ap, xsh.ap),
                   reads=[dst.buf, xsh.buf], writes=[dst.buf], tag="xmu2")
        for g in range(4):
            rps = P0 if g % 2 == 0 else P1
            for half in range(2):
                o = 512 * half
                for cc in range(8):
                    ctx.op("pe", lambda t, g=g, cc=cc, o=o, rps=rps: t.matmul(
                        rps.ap[:, o:o + 512],
                        v3(xrT.ap)[:, cc, 128 * g:128 * (g + 1)],
                        W["W_r"].ap[:, 1024 * cc + o:1024 * cc + o + 512],
                        start=(cc == 0), stop=(cc == 7)),
                        reads=[xrT.buf, W["W_r"].buf], writes=[rps.buf], tag="mmr")
            ctx.op("act", lambda s, rps=rps: s.activation(
                scr.ap, rps.ap, AF.Tanh, scale=0.5),
                reads=[rps.buf], writes=[scr.buf], tag="rtanh")
            ctx.op("dve", lambda v, g=g: v.tensor_scalar(
                rblk_t[g].ap, scr.ap, 0.5, 0.5, ALU.mult, ALU.add),
                reads=[scr.buf], writes=[rblk_t[g].buf], tag="rsig")
        for dc2 in range(8):
            kps = P3 if dc2 % 2 == 0 else P2
            kap = kps.ap[:, 0:TBLK]
            for cc in range(8):
                ctx.op("pe", lambda t, dc2=dc2, cc=cc, kap=kap: t.matmul(
                    kap, W["W_k"].ap[:, 1024 * cc + 128 * dc2:1024 * cc + 128 * dc2 + 128],
                    v3(xkT.ap)[:, cc, :],
                    start=(cc == 0), stop=(cc == 7)),
                    reads=[xkT.buf, W["W_k"].buf], writes=[kps.buf], tag="mmk")
            ctx.op("act", lambda s, kap=kap, kps=kps: s.activation(
                rb2.ap, kap, AF.Relu),
                reads=[kps.buf], writes=[rb2.buf], tag="krelu")
            ctx.op("dve", lambda v, dc2=dc2: v.tensor_mul(
                v3(kaT.ap)[:, dc2, :], rb2.ap, rb2.ap),
                reads=[rb2.buf], writes=[kaT.buf], tag="ksq")
        for g in range(4):
            c = 4 * b + g
            tile = K.str_tiles[c]
            kvps = P0 if g % 2 == 0 else P1
            for half in range(2):
                o = 512 * half
                for cc in range(8):
                    ctx.op("pe", lambda t, g=g, cc=cc, o=o, kvps=kvps: t.matmul(
                        kvps.ap[:, o:o + 512],
                        v3(kaT.ap)[:, cc, 128 * g:128 * (g + 1)],
                        W["W_v"].ap[:, 1024 * cc + o:1024 * cc + o + 512],
                        start=(cc == 0), stop=(cc == 7)),
                        reads=[kaT.buf, W["W_v"].buf], writes=[kvps.buf], tag="mmkv")
            ctx.op("dve", lambda v, g=g, kvps=kvps: v.tensor_mul(
                scr.ap, kvps.ap, rblk_t[g].ap),
                reads=[kvps.buf, rblk_t[g].buf], writes=[scr.buf], tag="kvr")
            ctx.op("dve", lambda v, tile=tile, c=c: v.scalar_tensor_tensor(
                tile.ap, scr.ap, 0.0, tile.ap, ALU.add, ALU.add,
                accum_out=K.stats_s_t[c].ap),
                reads=[scr.buf, tile.buf],
                writes=[tile.buf, K.stats_s_t[c].buf], tag="residD")
            ctx.op("act", lambda s, tile=tile, c=c: s.activation(
                scr.ap, tile.ap, AF.Square, accum_out=K.stats_q_t[c].ap),
                reads=[tile.buf], writes=[scr.buf, K.stats_q_t[c].buf], tag="sqdD")


def emit_final_sweep(K, out_ap):
    ctx = K.ctx
    mean = K.sb("fs_mean", [128, NCH], F32)
    var = K.sb("fs_var", [128, NCH], F32)
    rstd = K.sb("fs_rstd", [128, NCH], F32)
    ctx.op("dve", lambda v: v.tensor_copy(mean.ap, K.stats_s.ap),
           reads=[b.buf for b in K.stats_s_t], writes=[mean.buf], tag="fs_cp")
    emit_batch_rstd_from(K, mean, K.stats_q, rstd, EPS_LN)
    stage = [K.sb("fs_stage%d" % i, [128, 1024], F32) for i in range(2)]
    ov = out_ap.rearrange("(t p) f -> p t f", p=128)
    for c in range(NCH):
        tile = K.str_tiles[c]
        st = stage[c % 2]
        ctx.op("dve", lambda v, c=c, tile=tile, st=st: v.tensor_scalar(
            st.ap, tile.ap, mean.ap[:, c:c + 1], rstd.ap[:, c:c + 1],
            ALU.subtract, ALU.mult),
            reads=[tile.buf, mean.buf, rstd.buf], writes=[st.buf], tag="fs_norm")
        ctx.op("sp", lambda s, c=c, st=st: s.dma_start(ov[:, c, :], st.ap),
               reads=[st.buf], dma=True, tag="fs_st", lane="out" + str(c % 2))


def build_launch4():
    inputs = {
        "x": ((T, D), F32),
        "L1_ffn_W_r": ((D, D), F32), "L1_ffn_W_k": ((D, D), F32),
        "L1_ffn_W_v": ((D, D), F32),
        "L1_ffn_mu_r": ((D,), F32), "L1_ffn_mu_k": ((D,), F32),
    }
    outputs = {"out": ((T, D), F32)}
    k = K(inputs, outputs)
    k.add_const_inputs(["ident", "ones_b", "ones_f", "onescol_b", "one1_f", "one1_b"])
    k.alloc_stream()
    k.mk_arena(136 * 1024 - 512)
    k.load_stream_from_dram(k.din["x"], cast=True)
    k.push_region()
    prepd = emit_prepD(k, castsD)
    emit_phaseD(k, prepd)
    k.pop_region()
    k.push_region()
    emit_final_sweep(k, k.dout["out"])
    k.pop_region()
    k.ctx.finalize()
    return k




def build_launch1(nch=NCH, sweep=True):
    inputs = {
        "x": ((T, D), F32),
        "L0_attn_Wq": ((D, D), F32), "L0_attn_bq": ((D,), F32),
        "L0_attn_Wk": ((D, D), F32), "L0_attn_bk": ((D,), F32),
        "L0_attn_Wv": ((D, D), F32), "L0_attn_bv": ((D,), F32),
        "L0_attn_Wo": ((D, D), F32), "L0_attn_bo": ((D,), F32),
    }
    outputs = {"stream1": ((T, D), F32)}
    k = K(inputs, outputs)
    k.add_const_inputs(["ident", "triU", "normal", "inv_normal",
                        "ones_b", "ones_f", "onescol_b", "one1_f"])
    k.alloc_stream()
    k.load_stream_from_dram(k.din["x"], cast=True)
    emit_phaseA(k, nch=nch)
    if sweep:
        emit_ln_sweep(k)
    k.store_stream_to_dram_f32(k.dout["stream1"])
    k.ctx.finalize()
    return k


def const_inmap():
    c = host_consts()
    return {"c_" + n: np.ascontiguousarray(v) for n, v in c.items()}




ALL_INPUTS = {
    "L0_attn_x": ((T, D), F32),
    "L0_attn_Wq": ((D, D), F32), "L0_attn_bq": ((D,), F32),
    "L0_attn_Wk": ((D, D), F32), "L0_attn_bk": ((D,), F32),
    "L0_attn_Wv": ((D, D), F32), "L0_attn_bv": ((D,), F32),
    "L0_attn_Wo": ((D, D), F32), "L0_attn_bo": ((D,), F32),
    "L0_ffn_W_fc": ((4096, D), F32), "L0_ffn_W_proj": ((D, 4096), F32),
    "L1_attn_ln1_w": ((D,), F32), "L1_attn_ln1_b": ((D,), F32),
    "L1_attn_wq": ((D, D), F32), "L1_attn_wk": ((D, D), F32),
    "L1_attn_wv": ((D, D), F32), "L1_attn_wg": ((D, D), F32),
    "L1_attn_wo": ((D, D), F32),
    "L1_attn_ln2_w": ((D,), F32), "L1_attn_ln2_b": ((D,), F32),
    "L1_attn_w_in": ((D, 5460), F32), "L1_attn_b_in": ((5460,), F32),
    "L1_attn_w_out": ((HID, D), F32), "L1_attn_b_out": ((D,), F32),
    "L1_ffn_W_r": ((D, D), F32), "L1_ffn_W_k": ((D, D), F32),
    "L1_ffn_W_v": ((D, D), F32),
    "L1_ffn_mu_r": ((D,), F32), "L1_ffn_mu_k": ((D,), F32),
}


def build_fused():
    k = K(ALL_INPUTS, {"out": ((T, D), F32)})
    k.add_const_inputs()
    k.alloc_stream()
    k.mk_arena(136 * 1024 - 512)
    m1 = k.sb("m1", [128, NCH], F32, perm=True)
    v1 = k.sb("v1", [128, NCH], F32, perm=True)
    rstd1 = k.sb("rstd1", [128, NCH], F32, perm=True)
    m2 = k.sb("m2", [128, NCH], F32, perm=True)
    rstd2 = k.sb("rstd2", [128, NCH], F32, perm=True)
    r_spill = k.nc.dram_tensor("r_spill", [T, D], BF16, kind="Internal").ap()
    b_rspill = Buf("r_spill")

    k.load_stream_from_dram(k.din["L0_attn_x"], cast=True)
    castsB = emit_castsB(k)
    emit_castsC(k)
    castsD = emit_castsD(k)
    k.push_region()
    emit_phaseA(k)
    k.pop_region()
    k.push_region()
    emit_ln_sweep(k)
    k.pop_region()
    k.push_region()
    wfcT, wprojT = emit_prepB(k, castsB)
    emit_phaseB(k, wfcT, wprojT)
    k.pop_region()
    k.push_region()
    scrS = k.sb("S2_scr", [128, 1024], F32)
    def extra(c):
        tile = k.str_tiles[c]
        k.ctx.op("dve", lambda v, c=c, tile=tile: v.tensor_reduce(
            m1.ap[:, c:c + 1], tile.ap, AX.X, ALU.add),
            reads=[tile.buf], writes=[m1.buf], tag="s2_m1")
        k.ctx.op("act", lambda s, c=c, tile=tile: s.activation(
            scrS.ap, tile.ap, AF.Square, accum_out=v1.ap[:, c:c + 1]),
            reads=[tile.buf], writes=[scrS.buf, v1.buf], tag="s2_v1")
    emit_ln_sweep(k, extra_per_tile=extra)
    k.ctx.op("sp", lambda s: s.dma_start(
        r_spill.rearrange("(t p) f -> p t f", p=128),
        k.STR.ap.rearrange("p (t f) -> p t f", t=NCH)),
        reads=[t.buf for t in k.str_tiles], writes=[b_rspill],
        dma=True, tag="spill_r", lane="r_spill")
    k.pop_region()
    emit_batch_rstd(k, m1, v1, rstd1, EPS_LN)
    k.push_region()
    prep = emit_prepC(k)
    k.pop_region()
    k.push_region()
    emit_phaseC2(k, prep, m1, rstd1)
    k.pop_region()
    k.ctx.op("dve", lambda v: v.tensor_copy(m2.ap, k.stats_s.ap),
             reads=[b.buf for b in k.stats_s_t], writes=[m2.buf], tag="s3cp")
    emit_batch_rstd_from(k, m2, k.stats_q, rstd2, EPS_LN)
    k.push_region()
    emit_phaseC3(k, prep, m2, rstd2, r_spill, r_cast=False, r_buf=b_rspill)
    k.pop_region()
    k.push_region()
    emit_ln_sweep(k)
    k.pop_region()
    k.push_region()
    prepd = emit_prepD(k, castsD)
    emit_phaseD(k, prepd)
    k.pop_region()
    k.push_region()
    emit_final_sweep(k, k.dout["out"])
    k.pop_region()
    k.ctx.finalize()
    return k



_BUILD_CACHE = {}


def _get_fused():
    if "kf" not in _BUILD_CACHE:
        _BUILD_CACHE["kf"] = build_fused()
        _BUILD_CACHE["c"] = const_inmap()
    return _BUILD_CACHE["kf"], _BUILD_CACHE["c"]


def kernel(**inputs):
    k, cm = _get_fused()
    NCORES = 8
    x = np.asarray(inputs["L0_attn_x"], dtype=np.float32)
    base = {("c_" + n): v for n, v in cm.items() if False}
    base = {n: np.ascontiguousarray(v) for n, v in cm.items()}
    for nm in ALL_INPUTS:
        if nm == "L0_attn_x":
            continue
        base[nm] = np.ascontiguousarray(np.asarray(inputs[nm], dtype=np.float32))
    in_maps = [dict(base, L0_attn_x=np.ascontiguousarray(x[c]))
               for c in range(NCORES)]
    res = run_bass_kernel_spmd(k.nc, in_maps, core_ids=list(range(NCORES)))
    return np.stack([np.asarray(res.results[c]["out"]) for c in range(NCORES)],
                    axis=0)


# version 07 (final)
# speedup vs baseline: 1.5397x; 1.0548x over previous
import numpy as np
import ml_dtypes
from contextlib import ExitStack

import concourse.bass as bass
import concourse.mybir as mybir
from concourse.bass_utils import run_bass_kernel_spmd

F32 = mybir.dt.float32
BF16 = mybir.dt.bfloat16
AF = mybir.ActivationFunctionType
ALU = mybir.AluOpType
AX = mybir.AxisListType
BF16NP = ml_dtypes.bfloat16

T, D, H, HD = 4096, 1024, 16, 64
NCH = 32
EPS_A = 1e-6
EPS_LN = 1e-5
EPS_GN = 1e-6

ENGS = ("pe", "dve", "act", "pool", "sp")
SECT = {"pe": "tensor", "dve": "vector", "act": "scalar", "pool": "gpsimd", "sp": "sync"}


class Buf:
    __slots__ = ("name", "writer", "readers")
    def __init__(self, name):
        self.name = name
        self.writer = None
        self.readers = []


class TBuf:
    __slots__ = ("ap", "buf")
    def __init__(self, ap, buf):
        self.ap = ap
        self.buf = buf


class Ctx:
    def __init__(self, nc):
        self.nc = nc
        self.stream = {e: [] for e in ENGS}
        self.ccount = {e: 0 for e in ENGS}
        self.qcount = {}
        self.know = {e: {} for e in ENGS}
        self.same_engine_wait = {"pe": False, "dve": True, "act": True,
                                 "pool": True, "sp": False}
        self.nops = 0
        self.pending_barrier = {e: [] for e in ENGS}

    def barrier(self):
        toks = self.barrier_token_all()
        for e in ENGS:
            self.pending_barrier[e] = list(toks)

    def op(self, eng, fn, reads=(), writes=(), dma=False, tag="", lane=None):
        if dma:
            lane = lane or (writes[0].name if writes else "misc")
            key = ("q", lane)
        else:
            key = ("c", eng)
        need = {}
        if self.pending_barrier[eng]:
            for tok in self.pending_barrier[eng]:
                k, v = tok
                if v > need.get(k, 0):
                    need[k] = v
            self.pending_barrier[eng] = []
        def _need(tok):
            k, v = tok
            if v > need.get(k, 0):
                need[k] = v
        for b in reads:
            if b.writer:
                _need(b.writer)
        for b in writes:
            if b.writer:
                _need(b.writer)
            for tok in b.readers:
                _need(tok)
        kn = self.know[eng]
        for k in sorted(need, key=repr):
            v = need[k]
            if k == ("c", eng) and not dma and not self.same_engine_wait[eng]:
                continue
            if kn.get(k, 0) >= v:
                continue
            self.stream[eng].append(("w", k, v))
            kn[k] = v
        if dma:
            self.qcount[lane] = self.qcount.get(lane, 0) + 1
            val = self.qcount[lane] * 16
        else:
            self.ccount[eng] += 1
            val = self.ccount[eng]
        self.stream[eng].append(("o", fn, key, 16 if dma else 1, tag))
        self.nops += 1
        tok = (key, val)
        for b in reads:
            b.readers.append(tok)
        for b in writes:
            b.writer = tok
            b.readers = []
        return tok

    def barrier_token_all(self):
        toks = []
        for e in ENGS:
            if self.ccount[e]:
                toks.append((("c", e), self.ccount[e]))
        for lane, n in self.qcount.items():
            if n:
                toks.append((("q", lane), n * 16))
        return toks

    def finalize(self):
        nc = self.nc
        es = ExitStack()
        sems = {}
        for e in ENGS:
            sems[("c", e)] = es.enter_context(nc.semaphore(f"c_{e}"))
        for i, lane in enumerate(self.qcount):
            sems[("q", lane)] = es.enter_context(nc.semaphore(f"q{i}"))
        for e in ENGS:
            if self.ccount[e]:
                self.stream["sp"].append(("w", ("c", e), self.ccount[e]))
        for lane, n in self.qcount.items():
            if n:
                self.stream["sp"].append(("w", ("q", lane), n * 16))
        with nc.Block() as block:
            for e in ENGS:
                entries = self.stream[e]
                if not entries:
                    continue
                def mk(entries=entries):
                    def run(engobj):
                        for ent in entries:
                            if ent[0] == "w":
                                engobj.wait_ge(sems[ent[1]], ent[2])
                            else:
                                _, fn, key, inc, tag = ent
                                inst = fn(engobj)
                                inst.then_inc(sems[key], inc)
                    return run
                getattr(block, SECT[e])(mk())
        es.close()


def gamma_vec():
    return 1.0 - np.exp2(-5.0 - np.arange(H, dtype=np.float64))


def host_consts():
    c = {}
    c["ident"] = np.eye(128, dtype=BF16NP)
    tri = (np.arange(128)[:, None] <= np.arange(128)[None, :])
    c["triU"] = tri.astype(BF16NP)
    g = gamma_vec()
    tp = np.arange(128)
    dm = np.where(tp[None, :, None] >= tp[:, None, None],
                  g[None, None, :] ** (tp[None, :, None] - tp[:, None, None]), 0.0)
    c["D16"] = dm.transpose(0, 2, 1).reshape(128, H * 128).astype(BF16NP)
    tok = np.arange(T, dtype=np.float64).reshape(NCH, 128).T
    c["normal"] = (tok + 1.0).astype(np.float32)
    c["inv_normal"] = (1.0 / (tok + 1.0)).astype(np.float32)
    c["q2s"] = (g[None, :] ** (tp[:, None] + 1.0)).astype(np.float32)
    c["k3s"] = (g[None, :] ** (127.0 - tp[:, None])).astype(np.float32)
    c["gCt"] = np.broadcast_to((g ** 128.0)[None, :], (64, H)).astype(np.float32).copy()
    c["ones_b"] = np.ones((1, 128), dtype=BF16NP)
    c["ones_f"] = np.ones((1, 128), dtype=np.float32)
    c["onescol_b"] = np.ones((128, 1), dtype=BF16NP)
    c["one1_f"] = np.ones((1, 1), dtype=np.float32)
    c["one1_b"] = np.ones((1, 1), dtype=BF16NP)
    c["ones512_b"] = np.ones((1, 512), dtype=BF16NP)
    return c

CONST_DTYPES = {"ident": BF16, "triU": BF16, "D16": BF16, "normal": F32,
                "inv_normal": F32, "q2s": F32, "k3s": F32, "gCt": F32,
                "ones_b": BF16, "ones_f": F32, "onescol_b": BF16, "one1_f": F32, "one1_b": BF16,
                "ones512_b": BF16}


class K:
    def __init__(self, inputs, outputs, consts=True):
        nc = bass.Bass("TRN2", target_bir_lowering=False)
        self.nc = nc
        self.ctx = Ctx(nc)
        self.din = {}
        self.dout = {}
        for n, (shape, dt) in inputs.items():
            self.din[n] = nc.dram_tensor(n, list(shape), dt, kind="ExternalInput").ap()
        for n, (shape, dt) in outputs.items():
            self.dout[n] = nc.dram_tensor(n, list(shape), dt, kind="ExternalOutput").ap()
        self.sb_bytes = 0
        self._uniq = 0
        self._arena = None
        self._arena_bytes = 0
        self._region = None
        if consts:
            for n, dt in CONST_DTYPES.items():
                pass
        self.P0 = self.psum("P0", [128, 1024], F32)
        self.P1 = self.psum("P1", [128, 1024], F32)
        self.P2 = self.psum("P2", [128, 1024], F32)
        self.PT = self.psum("PTT", [128, 1024], BF16)
        self.P3 = self.psum("P3", [128, 512], F32)

    def debug_dump(self, name, tb, shape=None, rows=128):
        import os as _os
        if not _os.environ.get("MKDBG"):
            return
        ap = tb.ap
        shape = list(ap.shape) if shape is None else shape
        o = self.nc.dram_tensor("dbg_" + name, shape, F32, kind="ExternalOutput").ap()
        self.ctx.op("pool", lambda s: s.dma_start(o, ap),
                    reads=[tb.buf], dma=True, tag="dbgst")

    def constcol(self, value):
        if not hasattr(self, "_ccols"):
            self._ccols = {}
        if value not in self._ccols:
            tb = self.sb("ccol", [128, 1], F32, perm=True)
            self.ctx.op("dve", lambda v, tb=tb, value=value: v.memset(tb.ap, value),
                        writes=[tb.buf], tag="ccol")
            self._ccols[value] = tb
        return self._ccols[value]

    def uniq(self):
        self._uniq += 1
        return self._uniq

    def sb(self, name, shape, dtype, perm=None):
        per_part = int(np.prod(shape[1:])) * (2 if dtype == BF16 else 4)
        if self._region is None or perm:
            h = self.nc.alloc_sbuf_tensor(f"{name}_{self.uniq()}", list(shape), dtype)
            self.sb_bytes += per_part
            return TBuf(h.ap(), Buf(name))
        off = (self._region + 3) & ~3
        assert off + per_part <= self._arena_bytes, \
            f"arena overflow at {name}: {off}+{per_part} > {self._arena_bytes}"
        self._region = off + per_part
        self.region_peak = max(self.region_peak, self._region)
        base = self._arena.ap()
        if dtype == BF16:
            ap = base[0:shape[0], off // 2: off // 2 + per_part // 2]
        else:
            ap = base.bitcast(F32)[0:shape[0], off // 4: off // 4 + per_part // 4]
        if len(shape) == 2 and shape[1] != ap.shape[1]:
            ap = ap[:, 0:shape[1]]
        return TBuf(ap, Buf(name))

    def mk_arena(self, nbytes):
        self._arena = self.nc.alloc_sbuf_tensor("ARENA", [128, nbytes // 2], BF16)
        self._arena_bytes = nbytes
        self.sb_bytes += nbytes
        self.region_peak = 0

    def push_region(self):
        assert self._region is None
        self._region = 0
        self.ctx.barrier()

    def pop_region(self):
        self._region = None
        self.ctx.barrier()

    def psum(self, name, shape, dtype):
        h = self.nc.alloc_psum_tensor(name, list(shape), dtype)
        return TBuf(h.ap(), Buf(name))

    def carve(self, tb, n, width):
        return [TBuf(tb.ap[:, i * width:(i + 1) * width], Buf(f"{tb.buf.name}[{i}]"))
                for i in range(n)]

    def add_const_inputs(self, names=None):
        names = list(CONST_DTYPES) if names is None else names
        for n in names:
            dt = CONST_DTYPES[n]
            shp = {"ident": (128, 128), "triU": (128, 128), "D16": (128, H * 128),
                   "normal": (128, NCH), "inv_normal": (128, NCH), "q2s": (128, H),
                   "k3s": (128, H), "gCt": (64, H), "ones_b": (1, 128),
                   "ones_f": (1, 128), "onescol_b": (128, 1), "one1_f": (1, 1),
                   "one1_b": (1, 1), "ones512_b": (1, 512)}[n]
            self.din["c_" + n] = self.nc.dram_tensor("c_" + n, list(shp), dt,
                                                     kind="ExternalInput").ap()
        self.C = {}
        cbuf = Buf("consts")
        for n in names:
            dt = CONST_DTYPES[n]
            src = self.din["c_" + n]
            tb = self.sb("c" + n, list(src.shape), dt, perm=True)
            self.ctx.op("sp", lambda s, tb=tb, src=src: s.dma_start(tb.ap, src),
                        writes=[cbuf], dma=True, tag=f"ld_const_{n}", lane="consts")
            tb.buf = cbuf
            self.C[n] = tb

    def load_stream_from_dram(self, src_ap, cast):
        eng = "pool" if cast else "sp"
        view = src_ap.rearrange("(t p) f -> p t f", p=128)
        self.ctx.op(eng, lambda g: g.dma_start(
            self.STR.ap.rearrange("p (t f) -> p t f", t=NCH), view),
            writes=[self.STR.buf] + [t.buf for t in self.str_tiles],
            dma=True, tag="ld_stream")

    def store_stream_to_dram(self, dst_ap, cast=False):
        eng = "pool" if cast else "sp"
        view = dst_ap.rearrange("(t p) f -> p t f", p=128)
        self.ctx.op(eng, lambda g: g.dma_start(
            view, self.STR.ap.rearrange("p (t f) -> p t f", t=NCH)),
            reads=[t.buf for t in self.str_tiles] + [self.STR.buf],
            dma=True, tag="st_stream")

    def store_stream_to_dram_f32(self, dst_ap):
        view = dst_ap.rearrange("(t p) f -> p t f", p=128)
        self.ctx.op("pool", lambda g: g.dma_start(
            view, self.STR.ap.rearrange("p (t f) -> p t f", t=NCH)),
            reads=[t.buf for t in self.str_tiles] + [self.STR.buf],
            dma=True, tag="st_stream_f32")

    def alloc_stream(self):
        self.STR = self.sb("STR", [128, NCH * 1024], BF16, perm=True)
        self.str_tiles = self.carve(self.STR, NCH, 1024)
        self.stats_s = self.sb("stats_s", [128, NCH], F32, perm=True)
        self.stats_q = self.sb("stats_q", [128, NCH], F32, perm=True)
        self.stats_s_t = self.carve(self.stats_s, NCH, 1)
        self.stats_q_t = self.carve(self.stats_q, NCH, 1)

    def transpose8(self, src_tb, src_off, dst_tb, dst_view_fn=None, n=8, w=128,
                   rbuf=None):
        ctx = self.ctx
        rb = rbuf if rbuf is not None else src_tb.buf
        for i in range(n):
            ctx.op("pe", lambda t, i=i: t.transpose(
                self.PT.ap[:, i * w:(i + 1) * w],
                src_tb.ap[:, src_off + i * w: src_off + (i + 1) * w],
                self.C["ident"].ap),
                reads=[rb, self.C["ident"].buf], writes=[self.PT.buf],
                tag="T")
        dst_ap = dst_tb.ap if dst_view_fn is None else dst_view_fn(dst_tb.ap)
        src_ap = self.PT.ap[:, :n * w]
        ctx.op("act", lambda s: s.copy(dst_ap, src_ap),
               reads=[self.PT.buf], writes=[dst_tb.buf], tag="Tcopy")

    def transpose_heads(self, src_tb, dst_tb, rbuf=None):
        ctx = self.ctx
        rb = rbuf if rbuf is not None else src_tb.buf
        for b in range(2):
            for hh in range(8):
                h = 8 * b + hh
                ctx.op("pe", lambda t, h=h, hh=hh: t.transpose(
                    self.PT.ap[0:64, 128 * hh:128 * (hh + 1)],
                    src_tb.ap[:, 64 * h:64 * (h + 1)],
                    self.C["ident"].ap),
                    reads=[rb, self.C["ident"].buf], writes=[self.PT.buf],
                    tag="Th")
            ctx.op("act", lambda s, b=b: s.copy(
                dst_tb.ap[0:64, 1024 * b:1024 * (b + 1)], self.PT.ap[0:64, 0:1024]),
                reads=[self.PT.buf], writes=[dst_tb.buf], tag="Thcopy")

    def proj_tok(self, dst_ps, xT_tb, w_tb, bias_row=None, eng_bias_sb=None):
        ctx = self.ctx
        for half in range(2):
            o = 512 * half
            for dc in range(8):
                last = (dc == 7 and bias_row is None)
                ctx.op("pe", lambda t, dc=dc, o=o, last=last: t.matmul(
                    dst_ps.ap[:, o:o + 512],
                    xT_tb.ap[:, 128 * dc:128 * (dc + 1)],
                    w_tb.ap[:, 1024 * dc + o: 1024 * dc + o + 512],
                    start=(dc == 0), stop=last),
                    reads=[xT_tb.buf, w_tb.buf], writes=[dst_ps.buf], tag="mmproj")
            if bias_row is not None:
                ctx.op("pe", lambda t, o=o: t.matmul(
                    dst_ps.ap[:, o:o + 512], self.C["ones_b"].ap,
                    bias_row.ap[:, o:o + 512], start=False, stop=True),
                    reads=[bias_row.buf, self.C["ones_b"].buf],
                    writes=[dst_ps.buf], tag="mmbias")


def as3d(ap, d=64):
    return ap.rearrange("p (h d) -> p h d", d=d)


def bc16(ap2d, d=64, n=128):
    return ap2d.unsqueeze(2).broadcast_to([n, ap2d.shape[1], d])


STAGE = 99
def emit_phaseA(K, nch=NCH):
    ctx = K.ctx
    C = K.C
    P0, P1, P2, P3 = K.P0, K.P1, K.P2, K.P3
    pairs = [P0, P1, P2]
    W = {}
    for nm in ("Wq", "Wk", "Wv", "Wo"):
        w = K.sb("A_" + nm, [128, 8 * 1024], BF16)
        src = K.din["L0_attn_" + nm]
        ctx.op("pool", lambda g, w=w, src=src: g.dma_start(
            w.ap.rearrange("p (c f) -> p c f", c=8),
            src.rearrange("(c p) f -> p c f", p=128)),
            writes=[w.buf], dma=True, tag="ldw")
        W[nm] = w
    BR = {}
    for nm in ("bq", "bk", "bv", "bo"):
        b = K.sb("A_" + nm, [1, 1024], BF16)
        src = K.din["L0_attn_" + nm]
        ctx.op("pool", lambda g, b=b, src=src: g.dma_start(
            b.ap, src.rearrange("(a f) -> a f", a=1)),
            writes=[b.buf], dma=True, tag="ldb")
        BR[nm] = b

    xT2 = [K.sb("A_xT%d" % i, [128, 8 * 128], BF16) for i in range(2)]
    qb2 = [K.sb("A_qb%d" % i, [128, 1024], BF16) for i in range(2)]
    kb2 = [K.sb("A_kb%d" % i, [128, 1024], BF16) for i in range(2)]
    vb2 = [K.sb("A_vb%d" % i, [128, 1024], BF16) for i in range(2)]
    ksob = K.sb("A_ksob", [128, 1024], BF16)
    qsib = K.sb("A_qsib", [128, 1024], BF16)
    qpb = K.sb("A_qpb", [128, 1024], BF16)
    vpb = K.sb("A_vpb", [128, 1024], BF16)
    scr = K.sb("A_scr", [128, 1024], F32)
    qT = K.sb("A_qT", [64, 16 * 128], BF16)
    kT = K.sb("A_kT", [64, 16 * 128], BF16)
    scm = K.sb("A_scm", [128, 1024], BF16)
    aob = K.sb("A_aob", [128, 1024], BF16)
    aoT = K.sb("A_aoT", [128, 8 * 128], BF16)
    KVf = K.sb("A_KVf", [64, 1024], F32)
    KVsb = K.sb("A_KV", [64, 1024], BF16)
    cars = {nm: K.sb("A_car_" + nm, [1, 1024], BF16) for nm in ("k", "q", "3", "4")}
    carCS = K.sb("A_carCS", [1, 16], BF16)
    f16 = {nm: K.sb("A_f_" + nm, [128, 16], F32)
           for nm in ("siv", "qsc", "si", "sov", "so", "csk", "alo", "csr",
                      "sg", "cs", "ccs", "cmp", "t0", "cor")}
    csb = K.sb("A_csb", [128, 16], BF16)

    ctx.op("dve", lambda v: v.memset(KVf.ap, 0.0), writes=[KVf.buf], tag="kv0")
    ctx.op("dve", lambda v: v.memset(KVsb.ap, 0.0), writes=[KVsb.buf], tag="kv0b")
    for nm in cars:
        ctx.op("dve", lambda v, nm=nm: v.memset(cars[nm].ap, 0.0),
               writes=[cars[nm].buf], tag="car0")
    ctx.op("dve", lambda v: v.memset(carCS.ap, 0.0), writes=[carCS.buf])

    def front(c):
        par = c % 2
        pp = pairs[c % 3]
        tile = K.str_tiles[c]
        K.transpose8(K.STR, 1024 * c, xT2[par], rbuf=tile.buf)
        K.proj_tok(pp, xT2[par], W["Wq"], BR["bq"])
        ctx.op("act", lambda s, pp=pp, par=par: s.activation(
            qb2[par].ap, pp.ap, AF.Sigmoid),
            reads=[pp.buf], writes=[qb2[par].buf], tag="sigq")
        K.proj_tok(pp, xT2[par], W["Wk"], BR["bk"])
        ctx.op("act", lambda s, pp=pp, par=par: s.activation(
            kb2[par].ap, pp.ap, AF.Sigmoid),
            reads=[pp.buf], writes=[kb2[par].buf], tag="sigk")
        K.proj_tok(pp, xT2[par], W["Wv"], BR["bv"])
        ctx.op("act", lambda s, pp=pp, par=par: s.copy(vb2[par].ap, pp.ap),
               reads=[pp.buf], writes=[vb2[par].buf], tag="evacv")

    def cumsum_mm(ps, src_b, car, tag):
        for half in range(2):
            o = 512 * half
            ctx.op("pe", lambda t, o=o, ps=ps, src_b=src_b: t.matmul(
                ps.ap[:, o:o + 512], C["triU"].ap, src_b.ap[:, o:o + 512],
                start=True, stop=False),
                reads=[C["triU"].buf, src_b.buf], writes=[ps.buf], tag="cum" + tag)
            ctx.op("pe", lambda t, o=o, ps=ps, car=car: t.matmul(
                ps.ap[:, o:o + 512], C["ones_b"].ap, car.ap[:, o:o + 512],
                start=False, stop=True),
                reads=[C["ones_b"].buf, car.buf], writes=[ps.buf], tag="cumcar")
        for half in range(2):
            o = 512 * half
            ctx.op("pe", lambda t, o=o, src_b=src_b: t.matmul(
                P3.ap[0:1, 0:512], C["onescol_b"].ap, src_b.ap[:, o:o + 512],
                start=True, stop=False),
                reads=[C["onescol_b"].buf, src_b.buf], writes=[P3.buf], tag="colsum")
            ctx.op("pe", lambda t, o=o, car=car: t.matmul(
                P3.ap[0:1, 0:512], C["one1_b"].ap, car.ap[:, o:o + 512],
                start=False, stop=True),
                reads=[C["one1_b"].buf, car.buf], writes=[P3.buf], tag="colcar")
            ctx.op("dve", lambda v, o=o, car=car: v.tensor_copy(
                car.ap[:, o:o + 512], P3.ap[0:1, 0:512]),
                reads=[P3.buf], writes=[car.buf], tag="carcopy")

    def dot16(ps, qk, dst, tag):
        ctx.op("dve", lambda v, ps=ps, qk=qk: v.scalar_tensor_tensor(
            scr.ap, ps.ap, EPS_A, qk.ap, ALU.add, ALU.mult),
            reads=[ps.buf, qk.buf], writes=[scr.buf], tag=tag)
        ctx.op("dve", lambda v, dst=dst: v.tensor_reduce(
            f16[dst].ap, as3d(scr.ap), AX.X, ALU.add),
            reads=[scr.buf], writes=[f16[dst].buf], tag="red_" + tag)
        ctx.op("dve", lambda v, ps=ps: v.tensor_reduce(
            f16["cor"].ap, as3d(ps.ap), AX.X, ALU.add),
            reads=[ps.buf], writes=[f16["cor"].buf], tag="redc_" + tag)
        ctx.op("dve", lambda v: v.tensor_scalar(
            f16["cor"].ap, f16["cor"].ap, EPS_A, 64.0 * EPS_A * EPS_A,
            ALU.mult, ALU.add),
            reads=[f16["cor"].buf], writes=[f16["cor"].buf], tag="corsc")
        ctx.op("dve", lambda v, dst=dst: v.tensor_add(
            f16[dst].ap, f16[dst].ap, f16["cor"].ap),
            reads=[f16[dst].buf, f16["cor"].buf], writes=[f16[dst].buf],
            tag="coradd")

    front(0)
    for c in range(nch):
        par = c % 2
        qb, kb, vb, xT = qb2[par], kb2[par], vb2[par], xT2[par]
        PC = pairs[c % 3]
        PA = pairs[(c + 2) % 3]
        tile = K.str_tiles[c]
        ncol = C["normal"].ap[:, c:c + 1]
        incol = C["inv_normal"].ap[:, c:c + 1]
        if c + 1 < nch:
            front(c + 1)
        cumsum_mm(PC, qb, cars["q"], "q")
        dot16(PC, kb, "sov", "t2")
        ctx.op("dve", lambda v: v.reciprocal(f16["t0"].ap, f16["sov"].ap),
               reads=[f16["sov"].buf], writes=[f16["t0"].buf])
        ctx.op("dve", lambda v, ncol=ncol: v.tensor_scalar_mul(
            f16["so"].ap, f16["t0"].ap, ncol),
            reads=[f16["t0"].buf], writes=[f16["so"].buf])
        ctx.op("dve", lambda v, kb=kb: v.tensor_tensor(
            as3d(ksob.ap), as3d(kb.ap), bc16(f16["so"].ap), ALU.mult),
            reads=[kb.buf, f16["so"].buf], writes=[ksob.buf], tag="kso")
        cumsum_mm(PC, kb, cars["k"], "k")
        dot16(PC, qb, "siv", "t1")
        ctx.op("dve", lambda v: v.reciprocal(f16["qsc"].ap, f16["siv"].ap),
               reads=[f16["siv"].buf], writes=[f16["qsc"].buf])
        ctx.op("dve", lambda v, ncol=ncol: v.tensor_scalar_mul(
            f16["si"].ap, f16["qsc"].ap, ncol),
            reads=[f16["qsc"].buf], writes=[f16["si"].buf])
        ctx.op("dve", lambda v, qb=qb: v.tensor_tensor(
            as3d(qsib.ap), as3d(qb.ap), bc16(f16["si"].ap), ALU.mult),
            reads=[qb.buf, f16["si"].buf], writes=[qsib.buf], tag="qsi")
        cumsum_mm(PC, ksob, cars["3"], "3")
        dot16(PC, qb, "csk", "t3")
        ctx.op("dve", lambda v, incol=incol: v.tensor_scalar_mul(
            f16["csk"].ap, f16["csk"].ap, incol),
            reads=[f16["csk"].buf], writes=[f16["csk"].buf])
        ctx.op("act", lambda s: s.activation(f16["alo"].ap, f16["csk"].ap, AF.Sigmoid),
               reads=[f16["csk"].buf], writes=[f16["alo"].buf], tag="sig_alloc")
        cumsum_mm(PC, qsib, cars["4"], "4")
        dot16(PC, kb, "csr", "t4")
        ctx.op("dve", lambda v, incol=incol: v.tensor_scalar_mul(
            f16["csr"].ap, f16["csr"].ap, incol),
            reads=[f16["csr"].buf], writes=[f16["csr"].buf])
        ctx.op("act", lambda s: s.activation(f16["sg"].ap, f16["csr"].ap, AF.Sigmoid),
               reads=[f16["csr"].buf], writes=[f16["sg"].buf], tag="sig_cs")
        ctx.op("dve", lambda v: v.tensor_scalar(
            f16["t0"].ap, f16["sg"].ap, -1.0, 1.0, ALU.mult, ALU.add),
            reads=[f16["sg"].buf], writes=[f16["t0"].buf])
        ctx.op("dve", lambda v: v.reciprocal(f16["t0"].ap, f16["t0"].ap),
               reads=[f16["t0"].buf], writes=[f16["t0"].buf])
        ctx.op("dve", lambda v: v.tensor_mul(f16["cs"].ap, f16["sg"].ap, f16["t0"].ap),
               reads=[f16["sg"].buf, f16["t0"].buf], writes=[f16["cs"].buf])
        ctx.op("dve", lambda v: v.tensor_copy(csb.ap, f16["cs"].ap),
               reads=[f16["cs"].buf], writes=[csb.buf])
        ctx.op("pe", lambda t, PC=PC: t.matmul(PC.ap[:, 0:16], C["triU"].ap, csb.ap,
                                               start=True, stop=False),
               reads=[C["triU"].buf, csb.buf], writes=[PC.buf], tag="cumcs")
        ctx.op("pe", lambda t, PC=PC: t.matmul(PC.ap[:, 0:16], C["ones_b"].ap,
                                               carCS.ap, start=False, stop=True),
               reads=[C["ones_b"].buf, carCS.buf], writes=[PC.buf])
        ctx.op("dve", lambda v, PC=PC: v.tensor_copy(f16["ccs"].ap, PC.ap[:, 0:16]),
               reads=[PC.buf], writes=[f16["ccs"].buf])
        ctx.op("pe", lambda t: t.matmul(P3.ap[0:1, 0:16], C["onescol_b"].ap, csb.ap,
                                        start=True, stop=False),
               reads=[C["onescol_b"].buf, csb.buf], writes=[P3.buf])
        ctx.op("pe", lambda t: t.matmul(P3.ap[0:1, 0:16], C["one1_b"].ap,
                                        carCS.ap, start=False, stop=True),
               reads=[C["one1_b"].buf, carCS.buf], writes=[P3.buf])
        ctx.op("dve", lambda v: v.tensor_copy(carCS.ap, P3.ap[0:1, 0:16]),
               reads=[P3.buf], writes=[carCS.buf])
        ctx.op("dve", lambda v: v.reciprocal(f16["ccs"].ap, f16["ccs"].ap),
               reads=[f16["ccs"].buf], writes=[f16["ccs"].buf])
        ctx.op("dve", lambda v: v.tensor_mul(f16["cmp"].ap, f16["cs"].ap, f16["ccs"].ap),
               reads=[f16["cs"].buf, f16["ccs"].buf], writes=[f16["cmp"].buf])
        ctx.op("dve", lambda v, ncol=ncol: v.tensor_scalar_mul(
            f16["cmp"].ap, f16["cmp"].ap, ncol),
            reads=[f16["cmp"].buf], writes=[f16["cmp"].buf])
        ctx.op("dve", lambda v, qb=qb: v.tensor_tensor(
            as3d(qpb.ap), as3d(qb.ap), bc16(f16["qsc"].ap), ALU.mult),
            reads=[qb.buf, f16["qsc"].buf], writes=[qpb.buf], tag="qp")
        ctx.op("dve", lambda v, vb=vb: v.tensor_tensor(
            as3d(vpb.ap), as3d(vb.ap), bc16(f16["cmp"].ap), ALU.mult),
            reads=[vb.buf, f16["cmp"].buf], writes=[vpb.buf], tag="vp")
        for h in range(H):
            ctx.op("pe", lambda t, h=h, PC=PC: t.matmul(
                PC.ap[0:64, 64 * h:64 * (h + 1)],
                kb.ap[:, 64 * h:64 * (h + 1)],
                vpb.ap[:, 64 * h:64 * (h + 1)],
                start=True, stop=True),
                reads=[kb.buf, vpb.buf], writes=[PC.buf], tag="kvupd")
        K.transpose_heads(qpb, qT)
        K.transpose_heads(kb, kT)
        for sg in range(4):
            slot = 512 * (sg % 2)
            for hh in range(4):
                h = 4 * sg + hh
                ctx.op("pe", lambda t, hh=hh, h=h: t.matmul(
                    P3.ap[:, 128 * hh:128 * (hh + 1)],
                    kT.ap[0:64, 128 * h:128 * (h + 1)],
                    qT.ap[0:64, 128 * h:128 * (h + 1)],
                    start=True, stop=True),
                    reads=[kT.buf, qT.buf], writes=[P3.buf], tag="scores")
            ctx.op("dve", lambda v, slot=slot: v.tensor_tensor(
                scm.ap[:, slot:slot + 512].rearrange("p (h w) -> p h w", w=128),
                P3.ap.rearrange("p (h w) -> p h w", w=128),
                C["triU"].ap.unsqueeze(1).broadcast_to([128, 4, 128]),
                ALU.mult),
                reads=[P3.buf, C["triU"].buf], writes=[scm.buf], tag="mask")
            for hh in range(4):
                h = 4 * sg + hh
                ctx.op("pe", lambda t, h=h, hh=hh, c=c, slot=slot, PA=PA: t.matmul(
                    PA.ap[:, 64 * h:64 * (h + 1)],
                    scm.ap[:, slot + 128 * hh:slot + 128 * (hh + 1)],
                    vpb.ap[:, 64 * h:64 * (h + 1)], start=True, stop=(c == 0)),
                    reads=[scm.buf, vpb.buf], writes=[PA.buf], tag="intra")
                if c > 0:
                    ctx.op("pe", lambda t, h=h, PA=PA: t.matmul(
                        PA.ap[:, 64 * h:64 * (h + 1)],
                        qT.ap[0:64, 128 * h:128 * (h + 1)],
                        KVsb.ap[0:64, 64 * h:64 * (h + 1)], start=False, stop=True),
                        reads=[qT.buf, KVsb.buf], writes=[PA.buf], tag="inter")
        ctx.op("dve", lambda v, PC=PC: v.tensor_add(KVf.ap, KVf.ap, PC.ap[0:64, :]),
               reads=[KVf.buf, PC.buf], writes=[KVf.buf], tag="kvadd")
        ctx.op("dve", lambda v: v.tensor_copy(KVsb.ap, KVf.ap),
               reads=[KVf.buf], writes=[KVsb.buf], tag="kvevac")
        ctx.op("dve", lambda v, PA=PA: v.tensor_tensor(
            as3d(aob.ap), as3d(PA.ap), bc16(f16["alo"].ap), ALU.mult),
            reads=[PA.buf, f16["alo"].buf], writes=[aob.buf], tag="gate_alloc")
        K.transpose8(aob, 0, aoT)
        K.proj_tok(PA, aoT, W["Wo"], BR["bo"])
        ctx.op("dve", lambda v, c=c, tile=tile, PA=PA: v.scalar_tensor_tensor(
            tile.ap, PA.ap, 0.0, tile.ap, ALU.add, ALU.add,
            accum_out=K.stats_s_t[c].ap),
            reads=[PA.buf, tile.buf],
            writes=[tile.buf, K.stats_s_t[c].buf], tag="resid")
        ctx.op("act", lambda s, c=c, tile=tile: s.activation(
            scr.ap, tile.ap, AF.Square, accum_out=K.stats_q_t[c].ap),
            reads=[tile.buf], writes=[scr.buf, K.stats_q_t[c].buf], tag="sqdump")


def emit_ln_sweep(K, extra_per_tile=None, mean_out=None, rstd_out=None):
    ctx = K.ctx
    mean = mean_out or K.sb("sw_mean", [128, NCH], F32)
    var = K.sb("sw_var", [128, NCH], F32)
    rstd = rstd_out or K.sb("sw_rstd", [128, NCH], F32)
    sbufs = [t.buf for t in K.stats_s_t]
    qbufs = [t.buf for t in K.stats_q_t]
    ctx.op("dve", lambda v: v.tensor_scalar_mul(mean.ap, K.stats_s.ap, 1.0 / D),
           reads=sbufs + [K.stats_s.buf], writes=[mean.buf], tag="sw_mean")
    ctx.op("dve", lambda v: v.tensor_scalar_mul(var.ap, K.stats_q.ap, 1.0 / D),
           reads=qbufs + [K.stats_q.buf], writes=[var.buf], tag="sw_var")
    ctx.op("dve", lambda v: v.tensor_mul(K.stats_q.ap, mean.ap, mean.ap),
           reads=[mean.buf], writes=[K.stats_q.buf] + qbufs, tag="sw_m2")
    ctx.op("dve", lambda v: v.tensor_sub(var.ap, var.ap, K.stats_q.ap),
           reads=[var.buf, K.stats_q.buf], writes=[var.buf], tag="sw_vm")
    epsc = K.constcol(EPS_LN)
    ctx.op("act", lambda s: s.activation(var.ap, var.ap, AF.Ln, bias=epsc.ap),
           reads=[var.buf, epsc.buf], writes=[var.buf], tag="sw_ln")
    ctx.op("act", lambda s: s.activation(rstd.ap, var.ap, AF.Exp, scale=-0.5),
           reads=[var.buf], writes=[rstd.buf], tag="sw_exp")
    for c in range(NCH):
        tile = K.str_tiles[c]
        ctx.op("dve", lambda v, c=c, tile=tile: v.tensor_scalar(
            tile.ap, tile.ap, mean.ap[:, c:c + 1], rstd.ap[:, c:c + 1],
            ALU.subtract, ALU.mult),
            reads=[tile.buf, mean.buf, rstd.buf], writes=[tile.buf], tag="sw_norm")
        if extra_per_tile is not None:
            extra_per_tile(c)
    return mean, rstd


def emit_castsB(K):
    nc, ctx = K.nc, K.ctx
    wfc_bf = nc.dram_tensor("wfc_bf", [4096, 1024], BF16, kind="Internal").ap()
    wproj_bf = nc.dram_tensor("wproj_bf", [1024, 4096], BF16, kind="Internal").ap()
    bfc = Buf("wfc_bf"); bpj = Buf("wproj_bf")
    ctx.op("pool", lambda g: g.dma_start(wfc_bf, K.din["L0_ffn_W_fc"]),
           writes=[bfc], dma=True, tag="castWfc")
    ctx.op("pool", lambda g: g.dma_start(wproj_bf, K.din["L0_ffn_W_proj"]),
           writes=[bpj], dma=True, tag="castWproj")
    return (wfc_bf, bfc), (wproj_bf, bpj)


def emit_prepB(K, casts):
    nc, ctx = K.nc, K.ctx
    (wfc_bf, bfc), (wproj_bf, bpj) = casts
    wfcT_s = nc.dram_tensor("wfcT_s", [32, 128, 8, 128], BF16, kind="Internal").ap()
    wprojT_s = nc.dram_tensor("wprojT_s", [32, 128, 1024], BF16, kind="Internal").ap()
    bfcT = Buf("wfcT_s"); bpjT = Buf("wprojT_s")
    stg = [K.sb("B_stage%d" % i, [128, 4096], BF16) for i in range(2)]
    for dc in range(8):
        st = stg[dc % 2]
        ctx.op("sp", lambda s, dc=dc, st=st: s.dma_start(
            st.ap, wfc_bf[:, 128 * dc:128 * (dc + 1)], transpose=True),
            reads=[bfc], writes=[st.buf], dma=True, tag="Tfc",
            lane="B_stage%d" % (dc % 2)),
        ctx.op("act", lambda s, dc=dc, st=st: s.dma_start(
            wfcT_s[:, :, dc, :].rearrange("fc p w -> p fc w"),
            st.ap.rearrange("p (fc w) -> p fc w", w=128)),
            reads=[st.buf], writes=[bfcT], dma=True, tag="stfc", lane="wfcT_s")
    for fc in range(32):
        st = stg[fc % 2]
        ctx.op("sp", lambda s, fc=fc, st=st: s.dma_start(
            st.ap[:, 0:1024], wproj_bf[:, 128 * fc:128 * (fc + 1)],
            transpose=True),
            reads=[bpj], writes=[st.buf], dma=True, tag="Tpj",
            lane="B_stage%d" % (fc % 2)),
        ctx.op("act", lambda s, fc=fc, st=st: s.dma_start(
            wprojT_s[fc, :, :], st.ap[:, 0:1024]),
            reads=[st.buf], writes=[bpjT], dma=True, tag="stpj", lane="wprojT_s")
    return (wfcT_s, bfcT), (wprojT_s, bpjT)


def emit_phaseB(K, wfcT, wprojT):
    ctx = K.ctx
    C = K.C
    (wfcT_s, bfcT), (wprojT_s, bpjT) = wfcT, wprojT
    P0, P1, P2, P3 = K.P0, K.P1, K.P2, K.P3
    PTf = TBuf(K.PT.ap.bitcast(F32), K.PT.buf)
    TBLK, NB, GRP = 512, 8, 2
    xTb = K.sb("B_xTb", [128, 8 * TBLK], BF16)
    wfc_g = [K.sb("B_wfcg%d" % i, [128, GRP * 8 * 128], BF16) for i in range(2)]
    wpj_g = [K.sb("B_wpjg%d" % i, [128, GRP * 1024], BF16) for i in range(2)]
    hbT_full = K.sb("B_hbTf", [128, 32 * TBLK], BF16)
    hbT_t = K.carve(hbT_full, 32, TBLK)
    rb = K.sb("B_rb", [128, TBLK], BF16)
    yblk2 = [K.sb("B_yblk%d" % i, [128, 4 * 1024], F32) for i in range(2)]
    yblk_t2 = [K.carve(y, 4, 1024) for y in yblk2]
    mem = {e: K.sb("B_mem" + e, [128, 1024], F32) for e in ("dve", "pool")}
    spk = {e: K.sb("B_spk" + e, [128, 1024], F32) for e in ("dve", "pool")}
    ssum = {e: K.sb("B_ssum" + e, [128, 1024], F32) for e in ("dve", "pool")}
    scr = K.sb("B_scr", [128, 1024], F32)
    ybank = [TBuf(P0.ap[:, 0:512], P0.buf), TBuf(P0.ap[:, 512:1024], P0.buf),
             TBuf(P1.ap[:, 0:512], P1.buf), TBuf(P1.ap[:, 512:1024], P1.buf),
             TBuf(P2.ap[:, 0:512], P2.buf), TBuf(P2.ap[:, 512:1024], P2.buf),
             TBuf(P3.ap[:, 0:512], P3.buf), TBuf(PTf.ap[:, 0:512], K.PT.buf)]

    def emit_block(b):
        for g in range(4):
            tile = K.str_tiles[4 * b + g]
            K.transpose8(K.STR, 1024 * (4 * b + g), xTb,
                         dst_view_fn=lambda ap, g=g: ap.rearrange(
                             "p (dc w) -> p dc w", w=TBLK)[:, :, 128 * g:128 * (g + 1)],
                         rbuf=tile.buf)
        hrot = [TBuf(P3.ap[:, 0:512], P3.buf), TBuf(P2.ap[:, 0:512], P2.buf),
                TBuf(P2.ap[:, 512:1024], P2.buf)]
        for fcg in range(32 // GRP):
            wf = wfc_g[fcg % 2]
            ctx.op("sp", lambda s, fcg=fcg, wf=wf: s.dma_start(
                wf.ap.rearrange("p (fc q) -> p fc q", q=8 * 128),
                wfcT_s[GRP * fcg:GRP * (fcg + 1), :, :, :].rearrange(
                    "fc p dc w -> p fc (dc w)")),
                reads=[bfcT], writes=[wf.buf], dma=True, tag="ld_wfc")
            drain(4)
            for fci in range(GRP):
                fc = GRP * fcg + fci
                hps = hrot[fc % 3]
                for dc in range(8):
                    ctx.op("pe", lambda t, fci=fci, dc=dc, wf=wf, hps=hps: t.matmul(
                        hps.ap, wf.ap[:, (fci * 8 + dc) * 128:(fci * 8 + dc + 1) * 128],
                        xTb.ap[:, TBLK * dc:TBLK * (dc + 1)],
                        start=(dc == 0), stop=(dc == 7)),
                        reads=[wf.buf, xTb.buf], writes=[hps.buf], tag="mmh")
                ctx.op("act", lambda s, hps=hps: s.activation(rb.ap, hps.ap, AF.Relu),
                       reads=[hps.buf], writes=[rb.buf], tag="relu")
                ctx.op("dve", lambda v, fc=fc: v.tensor_mul(
                    hbT_t[fc].ap, rb.ap, rb.ap),
                    reads=[rb.buf], writes=[hbT_t[fc].buf], tag="sq")
        for fcg in range(32 // GRP):
            wp = wpj_g[fcg % 2]
            ctx.op("sp", lambda s, fcg=fcg, wp=wp: s.dma_start(
                wp.ap.rearrange("p (fc q) -> p fc q", q=1024),
                wprojT_s[GRP * fcg:GRP * (fcg + 1), :, :].rearrange(
                    "fc p q -> p fc q")),
                reads=[bpjT], writes=[wp.buf], dma=True, tag="ld_wpj")
            drain(4)
            for fci in range(GRP):
                fc = GRP * fcg + fci
                for g in range(4):
                    for half in range(2):
                        yb = ybank[2 * g + half]
                        ctx.op("pe", lambda t, fci=fci, fc=fc, g=g, half=half, yb=yb, wp=wp: t.matmul(
                            yb.ap, hbT_t[fc].ap[:, 128 * g:128 * (g + 1)],
                            wp.ap[:, 1024 * fci + 512 * half:1024 * fci + 512 * (half + 1)],
                            start=(fc == 0), stop=(fc == 31)),
                            reads=[hbT_t[fc].buf, wp.buf], writes=[yb.buf], tag="mmy")
        yblk_t = yblk_t2[b % 2]
        for g in range(4):
            for half in range(2):
                yb = ybank[2 * g + half]
                ctx.op("act", lambda s, g=g, half=half, yb=yb, yblk_t=yblk_t: s.copy(
                    yblk_t[g].ap[:, 512 * half:512 * (half + 1)], yb.ap),
                    reads=[yb.buf], writes=[yblk_t[g].buf], tag="yevac")
    pending = []

    def drain(n):
        for _ in range(n):
            if pending:
                pending.pop(0)()

    def scan_block(bs):
        TH = pending.append
        negone = K.constcol(-1.0)
        yblk_t = yblk_t2[bs % 2]
        for g in range(4):
            e = "dve" if g % 2 == 0 else "pool"
            y = yblk_t[g]
            me, sp_, su = mem[e], spk[e], ssum[e]
            tile = K.str_tiles[4 * bs + g]
            c = 4 * bs + g
            TH(lambda y=y, sp_=sp_: ctx.op("act", lambda s, y=y, sp_=sp_: s.activation(
                sp_.ap, y.ap, AF.Sign, bias=negone.ap),
                reads=[y.buf, negone.buf], writes=[sp_.buf], tag="scan_sg"))
            TH(lambda sp_=sp_: ctx.op("act", lambda s, sp_=sp_: s.activation(sp_.ap, sp_.ap, AF.Relu),
                   reads=[sp_.buf], writes=[sp_.buf], tag="scan_rl"))
            TH(lambda sp_=sp_, su=su: ctx.op("dve", lambda v, sp_=sp_, su=su: v.tensor_copy(su.ap, sp_.ap),
                   reads=[sp_.buf], writes=[su.buf], tag="scan_su"))
            for step in range(4):
                TH(lambda y=y, sp_=sp_: ctx.op("dve", lambda v, y=y, sp_=sp_: v.tensor_sub(
                    scr.ap, y.ap, sp_.ap),
                    reads=[y.buf, sp_.buf], writes=[scr.buf], tag="scan_t"))
                if step == 0:
                    TH(lambda me=me, y=y: ctx.op("dve", lambda v, me=me, y=y: v.scalar_tensor_tensor(
                        me.ap, y.ap, 0.9, scr.ap, ALU.mult, ALU.add),
                        reads=[y.buf, scr.buf], writes=[me.buf], tag="scan_m0"))
                else:
                    TH(lambda me=me: ctx.op("dve", lambda v, me=me: v.scalar_tensor_tensor(
                        me.ap, me.ap, 0.9, scr.ap, ALU.mult, ALU.add),
                        reads=[me.buf, scr.buf], writes=[me.buf], tag="scan_m"))
                TH(lambda me=me, sp_=sp_: ctx.op("act", lambda s, me=me, sp_=sp_: s.activation(
                    sp_.ap, me.ap, AF.Sign, bias=negone.ap),
                    reads=[me.buf, negone.buf], writes=[sp_.buf], tag="scan_sg2"))
                TH(lambda sp_=sp_: ctx.op("act", lambda s, sp_=sp_: s.activation(sp_.ap, sp_.ap, AF.Relu),
                       reads=[sp_.buf], writes=[sp_.buf], tag="scan_rl2"))
                TH(lambda su=su, sp_=sp_: ctx.op("dve", lambda v, su=su, sp_=sp_: v.tensor_add(
                    su.ap, su.ap, sp_.ap),
                    reads=[su.buf, sp_.buf], writes=[su.buf], tag="scan_su2"))
            TH(lambda su=su, tile=tile, c=c: ctx.op("dve", lambda v, su=su, tile=tile, c=c: v.scalar_tensor_tensor(
                tile.ap, su.ap, 0.2, tile.ap, ALU.mult, ALU.add,
                accum_out=K.stats_s_t[c].ap),
                reads=[su.buf, tile.buf],
                writes=[tile.buf, K.stats_s_t[c].buf], tag="residB"))
            TH(lambda tile=tile, c=c: ctx.op("act", lambda s, tile=tile, c=c: s.activation(
                scr.ap, tile.ap, AF.Square, accum_out=K.stats_q_t[c].ap),
                reads=[tile.buf], writes=[scr.buf, K.stats_q_t[c].buf], tag="sqdumpB"))

    for b in range(NB):
        emit_block(b)
        scan_block(b)
    drain(1000)


def build_launch2():
    inputs = {
        "x": ((T, D), F32),
        "L0_ffn_W_fc": ((4096, D), F32),
        "L0_ffn_W_proj": ((D, 4096), F32),
    }
    outputs = {"stream2": ((T, D), F32)}
    k = K(inputs, outputs)
    k.add_const_inputs(["ident", "ones_b", "ones_f", "onescol_b", "one1_f"])
    k.alloc_stream()
    k.load_stream_from_dram(k.din["x"], cast=True)
    wfcT, wprojT = emit_prepB(k, emit_castsB(k))
    emit_phaseB(k, wfcT, wprojT)
    emit_ln_sweep(k)
    k.store_stream_to_dram_f32(k.dout["stream2"])
    k.ctx.finalize()
    return k



JCW = [128] * 21 + [42]
JCO = [0]
for _w in JCW[:-1]:
    JCO.append(JCO[-1] + _w)
HID = 2730


def emit_batch_rstd(K, mean, var, rstd, eps):
    ctx = K.ctx
    ctx.op("dve", lambda v: v.tensor_scalar_mul(mean.ap, mean.ap, 1.0 / D),
           reads=[mean.buf], writes=[mean.buf], tag="br_mean")
    ctx.op("dve", lambda v: v.tensor_scalar_mul(var.ap, var.ap, 1.0 / D),
           reads=[var.buf], writes=[var.buf], tag="br_v")
    ctx.op("dve", lambda v: v.tensor_mul(rstd.ap, mean.ap, mean.ap),
           reads=[mean.buf], writes=[rstd.buf], tag="br_m2")
    ctx.op("dve", lambda v: v.tensor_sub(var.ap, var.ap, rstd.ap),
           reads=[var.buf, rstd.buf], writes=[var.buf], tag="br_vm")
    epsc = K.constcol(eps)
    ctx.op("act", lambda s: s.activation(var.ap, var.ap, AF.Ln, bias=epsc.ap),
           reads=[var.buf, epsc.buf], writes=[var.buf], tag="br_ln")
    ctx.op("act", lambda s: s.activation(rstd.ap, var.ap, AF.Exp, scale=-0.5),
           reads=[var.buf], writes=[rstd.buf], tag="br_exp")


def emit_stats_presweep(K, mean, var):
    ctx = K.ctx
    scr = K.sb("psw_scr", [128, 1024], F32)
    for c in range(NCH):
        tile = K.str_tiles[c]
        ctx.op("dve", lambda v, c=c, tile=tile: v.tensor_reduce(
            mean.ap[:, c:c + 1], tile.ap, AX.X, ALU.add),
            reads=[tile.buf], writes=[mean.buf], tag="psw_s")
        ctx.op("act", lambda s, c=c, tile=tile: s.activation(
            scr.ap, tile.ap, AF.Square, accum_out=var.ap[:, c:c + 1]),
            reads=[tile.buf], writes=[scr.buf, var.buf], tag="psw_q")


def emit_castsC(K):
    nc, ctx = K.nc, K.ctx
    wo_bf = nc.dram_tensor("C_wo_bf", [1024, 1024], BF16, kind="Internal").ap()
    bwo = Buf("C_wo_bf")
    ctx.op("pool", lambda g: g.dma_start(wo_bf, K.din["L1_attn_wo"]),
           writes=[bwo], dma=True, tag="castwo")
    wout_bf = nc.dram_tensor("C_wout_bf", [2730, 1024], BF16, kind="Internal").ap()
    bwout = Buf("C_wout_bf")
    ctx.op("pool", lambda g: g.dma_start(wout_bf, K.din["L1_attn_w_out"]),
           writes=[bwout], dma=True, tag="castwout")
    K._castsC = ((wo_bf, bwo), (wout_bf, bwout))


def emit_prepC(K):
    nc, ctx = K.nc, K.ctx
    res = {}
    cols = {}
    for nm in ("ln1_w", "ln1_b", "ln2_w", "ln2_b"):
        rb = K.sb("C_row_" + nm, [1, 1024], BF16)
        ctx.op("pool", lambda g, rb=rb, nm=nm: g.dma_start(
            rb.ap, K.din["L1_attn_" + nm].rearrange("(a f) -> a f", a=1)),
            writes=[rb.buf], dma=True, tag="ldrow")
        col = K.sb("C_col_" + nm, [128, 8], F32)
        for dc in range(8):
            ctx.op("pe", lambda t, rb=rb, dc=dc: t.transpose(
                K.PT.ap[:, 2 * dc:2 * dc + 1], rb.ap[:, 128 * dc:128 * (dc + 1)],
                K.C["one1_b"].ap),
                reads=[rb.buf, K.C["one1_b"].buf], writes=[K.PT.buf], tag="Trow")
        ctx.op("dve", lambda v, col=col: v.tensor_copy(
            col.ap.unsqueeze(2),
            K.PT.ap[:, 0:16].rearrange("p (a b) -> p a b", b=2)[:, :, 0:1]),
            reads=[K.PT.buf], writes=[col.buf], tag="Trowcp")
        cols[nm] = col
    wtile = K.sb("C_wtile", [128, 1024], F32)
    wtileb = K.sb("C_wtileb", [128, 1024], BF16)
    bcol = K.sb("C_bcol", [128, 1], F32)
    for nm, scale in (("wq", HD ** -0.5), ("wk", 1.0), ("wv", 1.0), ("wg", 1.0)):
        dst = nc.dram_tensor("C_%s_f" % nm, [1024, 1024], BF16, kind="Internal").ap()
        bdst = Buf("C_%s_f" % nm)
        brow = K.sb("C_brow_" + nm, [1, 1024], BF16)
        src = K.din["L1_attn_" + nm]
        for dc in range(8):
            ctx.op("sp", lambda s, src=src, dc=dc: s.dma_start(
                wtile.ap, src[128 * dc:128 * (dc + 1), :]),
                writes=[wtile.buf], dma=True, tag="ldw")
            ctx.op("dve", lambda v, dc=dc, scale=scale: v.tensor_scalar(
                wtileb.ap, wtile.ap, cols["ln1_w"].ap[:, dc:dc + 1], scale,
                ALU.mult, ALU.mult),
                reads=[wtile.buf, cols["ln1_w"].buf], writes=[wtileb.buf], tag="fold")
            ctx.op("sp", lambda s, dst=dst, dc=dc: s.dma_start(
                dst[128 * dc:128 * (dc + 1), :], wtileb.ap),
                reads=[wtileb.buf], writes=[bdst], dma=True, tag="stw")
            ctx.op("dve", lambda v, dc=dc, scale=scale: v.tensor_scalar(
                bcol.ap, cols["ln1_b"].ap[:, dc:dc + 1], scale, None,
                ALU.mult, ALU.bypass),
                reads=[cols["ln1_b"].buf], writes=[bcol.buf], tag="bcol")
            ctx.op("pe", lambda t, dc=dc: t.matmul(
                K.P3.ap[0:1, 0:512], bcol.ap, wtile.ap[:, 0:512],
                start=(dc == 0), stop=(dc == 7)),
                reads=[bcol.buf, wtile.buf], writes=[K.P3.buf], tag="mmbr0")
            ctx.op("pe", lambda t, dc=dc: t.matmul(
                K.P2.ap[0:1, 0:512], bcol.ap, wtile.ap[:, 512:1024],
                start=(dc == 0), stop=(dc == 7)),
                reads=[bcol.buf, wtile.buf], writes=[K.P2.buf], tag="mmbr1")
        ctx.op("dve", lambda v, brow=brow: v.tensor_copy(
            brow.ap[:, 0:512], K.P3.ap[0:1, 0:512]),
            reads=[K.P3.buf], writes=[brow.buf], tag="brcp0")
        ctx.op("dve", lambda v, brow=brow: v.tensor_copy(
            brow.ap[:, 512:1024], K.P2.ap[0:1, 0:512]),
            reads=[K.P2.buf], writes=[brow.buf], tag="brcp1")
        brow_d = nc.dram_tensor("C_brow_%s_d" % nm, [1, 1024], BF16,
                                kind="Internal").ap()
        bbrd = Buf("C_brow_%s_d" % nm)
        ctx.op("sp", lambda s, brow=brow, brow_d=brow_d: s.dma_start(brow_d, brow.ap),
               reads=[brow.buf], writes=[bbrd], dma=True, tag="stbrow")
        res[nm] = (dst, bdst, (brow_d, bbrd))
    win_f = nc.dram_tensor("C_win_f", [1024, 5460], BF16, kind="Internal").ap()
    bwin = Buf("C_win_f")
    wint = K.sb("C_wint", [128, 5460], F32)
    wintb = K.sb("C_wintb", [128, 5460], BF16)
    browf = K.sb("C_browf", [1, 5460], F32)
    ctx.op("dve", lambda v: v.memset(browf.ap, 0.0), writes=[browf.buf])
    NSL = 11
    for dc in range(8):
        ctx.op("sp", lambda s, dc=dc: s.dma_start(
            wint.ap, K.din["L1_attn_w_in"][128 * dc:128 * (dc + 1), :]),
            writes=[wint.buf], dma=True, tag="ldwin")
        ctx.op("dve", lambda v, dc=dc: v.tensor_scalar_mul(
            wintb.ap, wint.ap, cols["ln2_w"].ap[:, dc:dc + 1]),
            reads=[wint.buf, cols["ln2_w"].buf], writes=[wintb.buf], tag="foldin")
        ctx.op("sp", lambda s, dc=dc: s.dma_start(
            win_f[128 * dc:128 * (dc + 1), :], wintb.ap),
            reads=[wintb.buf], writes=[bwin], dma=True, tag="stwin")
        for sl in range(NSL):
            o = 512 * sl
            w = min(512, 5460 - o)
            ctx.op("pe", lambda t, dc=dc, o=o, w=w: t.matmul(
                K.P3.ap[0:1, 0:w], cols["ln2_b"].ap[:, dc:dc + 1],
                wint.ap[:, o:o + w], start=True, stop=True),
                reads=[cols["ln2_b"].buf, wint.buf], writes=[K.P3.buf], tag="mmbin")
            ctx.op("dve", lambda v, o=o, w=w: v.tensor_add(
                browf.ap[:, o:o + w], browf.ap[:, o:o + w], K.P3.ap[0:1, 0:w]),
                reads=[K.P3.buf, browf.buf], writes=[browf.buf], tag="accbin")
    binrow_f = K.sb("C_binrow_f", [1, 5460], F32)
    ctx.op("pool", lambda g: g.dma_start(
        binrow_f.ap, K.din["L1_attn_b_in"].rearrange("(a f) -> a f", a=1)),
        writes=[binrow_f.buf], dma=True, tag="ldbin")
    brow_in = K.sb("C_brow_in", [1, 5460], BF16)
    ctx.op("dve", lambda v: v.tensor_add(brow_in.ap, browf.ap, binrow_f.ap),
           reads=[browf.buf, binrow_f.buf], writes=[brow_in.buf], tag="binrow")
    brin_d = nc.dram_tensor("C_brow_in_d", [1, 5460], BF16, kind="Internal").ap()
    bbrin = Buf("C_brow_in_d")
    ctx.op("sp", lambda s: s.dma_start(brin_d, brow_in.ap),
           reads=[brow_in.buf], writes=[bbrin], dma=True, tag="stbrin")
    res["w_in"] = (win_f, bwin, (brin_d, bbrin))
    (wo_bf, bwo), (wout_bf, bwout) = K._castsC
    bout_d = nc.dram_tensor("C_bout_d", [1, 1024], BF16, kind="Internal").ap()
    bbout = Buf("C_bout_d")
    ctx.op("pool", lambda g: g.dma_start(
        bout_d, K.din["L1_attn_b_out"].rearrange("(a f) -> a f", a=1)),
        writes=[bbout], dma=True, tag="ldbout")
    res["wo"] = (wo_bf, bwo, None)
    res["w_out"] = (wout_bf, bwout, None)
    res["b_out"] = (bout_d, bbout)
    return res


def emit_phaseC2(K, prep, m1, rstd1, nch=NCH):
    ctx = K.ctx
    C = K.C
    P0, P1, P2, P3 = K.P0, K.P1, K.P2, K.P3
    W = {}
    for nm in ("wq", "wk", "wv", "wg"):
        w = K.sb("C2_" + nm, [128, 8 * 1024], BF16)
        dst, bdst, (brow_d, bbrd) = prep[nm]
        ctx.op("sp", lambda s, w=w, dst=dst: s.dma_start(
            w.ap.rearrange("p (c f) -> p c f", c=8),
            dst.rearrange("(c p) f -> p c f", p=128)),
            reads=[bdst], writes=[w.buf], dma=True, tag="ldwf")
        brow = K.sb("C2_brow_" + nm, [1, 1024], BF16)
        ctx.op("sp", lambda s, brow=brow, brow_d=brow_d: s.dma_start(
            brow.ap, brow_d),
            reads=[bbrd], writes=[brow.buf], dma=True, tag="ldbrow")
        W[nm] = (w, brow)
    wo = K.sb("C2_wo", [128, 8 * 1024], BF16)
    ctx.op("sp", lambda s: s.dma_start(
        wo.ap.rearrange("p (c f) -> p c f", c=8),
        prep["wo"][0].rearrange("(c p) f -> p c f", p=128)),
        reads=[prep["wo"][1]], writes=[wo.buf], dma=True, tag="ldwo")

    xnb = K.sb("C2_xnb", [128, 1024], BF16)
    xnT = K.sb("C2_xnT", [128, 8 * 128], BF16)
    qb = K.sb("C2_qb", [128, 1024], BF16)
    kb = K.sb("C2_kb", [128, 1024], BF16)
    vb = K.sb("C2_vb", [128, 1024], BF16)
    gb = K.sb("C2_gb", [128, 1024], BF16)
    q2b = K.sb("C2_q2b", [128, 1024], BF16)
    k3b = K.sb("C2_k3b", [128, 1024], BF16)
    gateb = K.sb("C2_gateb", [128, 1024], BF16)
    qT = K.sb("C2_qT", [64, 2048], BF16)
    q2T = K.sb("C2_q2T", [64, 2048], BF16)
    kT = K.sb("C2_kT", [64, 2048], BF16)
    scm = K.sb("C2_scm", [128, 1024], BF16)
    aob = K.sb("C2_aob", [128, 1024], BF16)
    aoT = K.sb("C2_aoT", [128, 8 * 128], BF16)
    scr = K.sb("C2_scr", [128, 1024], F32)
    Sf = K.sb("C2_Sf", [64, 1024], F32)
    Ssb = K.sb("C2_Ssb", [64, 1024], BF16)
    f16 = {nm: K.sb("C2_f_" + nm, [128, 16], F32)
           for nm in ("mu", "sq", "var", "rstd")}
    ctx.op("dve", lambda v: v.memset(Sf.ap, 0.0), writes=[Sf.buf])
    ctx.op("dve", lambda v: v.memset(Ssb.ap, 0.0), writes=[Ssb.buf])

    for c in range(nch):
        tile = K.str_tiles[c]
        ctx.op("dve", lambda v, c=c, tile=tile: v.tensor_scalar(
            xnb.ap, tile.ap, m1.ap[:, c:c + 1], rstd1.ap[:, c:c + 1],
            ALU.subtract, ALU.mult),
            reads=[tile.buf, m1.buf, rstd1.buf], writes=[xnb.buf], tag="ln1")
        K.transpose8(xnb, 0, xnT)
        K.proj_tok(P0, xnT, W["wq"][0], W["wq"][1])
        ctx.op("act", lambda s: s.copy(qb.ap, P0.ap),
               reads=[P0.buf], writes=[qb.buf], tag="evq")
        K.proj_tok(P1, xnT, W["wk"][0], W["wk"][1])
        ctx.op("act", lambda s: s.copy(kb.ap, P1.ap),
               reads=[P1.buf], writes=[kb.buf], tag="evk")
        K.proj_tok(P0, xnT, W["wv"][0], W["wv"][1])
        ctx.op("act", lambda s: s.copy(vb.ap, P0.ap),
               reads=[P0.buf], writes=[vb.buf], tag="evv")
        K.proj_tok(P1, xnT, W["wg"][0], W["wg"][1])
        ctx.op("act", lambda s: s.copy(gb.ap, P1.ap),
               reads=[P1.buf], writes=[gb.buf], tag="evg")
        if c == 0:
            K.debug_dump("xnb", xnb)
            K.debug_dump("qb", qb)
            K.debug_dump("kb", kb)
            K.debug_dump("vb", vb)
            K.debug_dump("gb", gb)
        ctx.op("act", lambda s: s.activation(scr.ap, gb.ap, AF.Exp, scale=-1.0),
               reads=[gb.buf], writes=[scr.buf], tag="gexp")
        ctx.op("dve", lambda v: v.tensor_scalar_add(scr.ap, scr.ap, 1.0),
               reads=[scr.buf], writes=[scr.buf], tag="g1")
        ctx.op("dve", lambda v: v.reciprocal(scr.ap, scr.ap),
               reads=[scr.buf], writes=[scr.buf], tag="grec")
        ctx.op("dve", lambda v: v.tensor_mul(gateb.ap, gb.ap, scr.ap),
               reads=[gb.buf, scr.buf], writes=[gateb.buf], tag="gmul")
        if c == 0:
            K.debug_dump("gateb", gateb)
        ctx.op("dve", lambda v: v.tensor_tensor(
            as3d(q2b.ap), as3d(qb.ap), bc16(C["q2s"].ap), ALU.mult),
            reads=[qb.buf, C["q2s"].buf], writes=[q2b.buf], tag="q2")
        ctx.op("dve", lambda v: v.tensor_tensor(
            as3d(k3b.ap), as3d(kb.ap), bc16(C["k3s"].ap), ALU.mult),
            reads=[kb.buf, C["k3s"].buf], writes=[k3b.buf], tag="k3")
        K.transpose_heads(qb, qT)
        K.transpose_heads(q2b, q2T)
        K.transpose_heads(kb, kT)
        for grp in range(2):
            for hh in range(8):
                h = grp * 8 + hh
                ctx.op("pe", lambda t, hh=hh, h=h: t.matmul(
                    P1.ap[:, 128 * hh:128 * (hh + 1)],
                    kT.ap[0:64, 128 * h:128 * (h + 1)],
                    qT.ap[0:64, 128 * h:128 * (h + 1)],
                    start=True, stop=True),
                    reads=[kT.buf, qT.buf], writes=[P1.buf], tag="rsc")
            ctx.op("dve", lambda v, grp=grp: v.tensor_tensor(
                scm.ap.rearrange("p (h w) -> p h w", w=128),
                P1.ap.rearrange("p (h w) -> p h w", w=128),
                C["D16"].ap[:, 1024 * grp:1024 * (grp + 1)].rearrange(
                    "p (h w) -> p h w", w=128),
                ALU.mult),
                reads=[P1.buf, C["D16"].buf], writes=[scm.buf], tag="rmask")
            for hh in range(8):
                h = grp * 8 + hh
                ctx.op("pe", lambda t, h=h, hh=hh, c=c: t.matmul(
                    P0.ap[:, 64 * h:64 * (h + 1)],
                    scm.ap[:, 128 * hh:128 * (hh + 1)],
                    vb.ap[:, 64 * h:64 * (h + 1)], start=True, stop=(c == 0)),
                    reads=[scm.buf, vb.buf], writes=[P0.buf], tag="rintra")
                if c > 0:
                    ctx.op("pe", lambda t, h=h: t.matmul(
                        P0.ap[:, 64 * h:64 * (h + 1)],
                        q2T.ap[0:64, 128 * h:128 * (h + 1)],
                        Ssb.ap[0:64, 64 * h:64 * (h + 1)], start=False, stop=True),
                        reads=[q2T.buf, Ssb.buf], writes=[P0.buf], tag="rinter")
        for h in range(H):
            ctx.op("pe", lambda t, h=h: t.matmul(
                P2.ap[0:64, 64 * h:64 * (h + 1)],
                k3b.ap[:, 64 * h:64 * (h + 1)],
                vb.ap[:, 64 * h:64 * (h + 1)], start=True, stop=True),
                reads=[k3b.buf, vb.buf], writes=[P2.buf], tag="rupd")
        ctx.op("dve", lambda v: v.tensor_tensor(
            Sf.ap.rearrange("p (h d) -> p h d", d=64),
            Sf.ap.rearrange("p (h d) -> p h d", d=64),
            C["gCt"].ap.unsqueeze(2).broadcast_to([64, H, 64]),
            ALU.mult),
            reads=[Sf.buf, C["gCt"].buf], writes=[Sf.buf], tag="sdecay")
        ctx.op("dve", lambda v: v.tensor_add(Sf.ap, Sf.ap, P2.ap[0:64, :]),
               reads=[Sf.buf, P2.buf], writes=[Sf.buf], tag="sadd")
        ctx.op("dve", lambda v: v.tensor_copy(Ssb.ap, Sf.ap),
               reads=[Sf.buf], writes=[Ssb.buf], tag="scp")
        ctx.op("dve", lambda v: v.tensor_reduce(
            f16["mu"].ap, as3d(P0.ap), AX.X, ALU.add),
            reads=[P0.buf], writes=[f16["mu"].buf], tag="gn_s")
        ctx.op("dve", lambda v: v.tensor_scalar_mul(
            f16["mu"].ap, f16["mu"].ap, 1.0 / 64.0),
            reads=[f16["mu"].buf], writes=[f16["mu"].buf])
        ctx.op("dve", lambda v: v.tensor_tensor(
            as3d(scr.ap), as3d(P0.ap), bc16(f16["mu"].ap), ALU.subtract),
            reads=[P0.buf, f16["mu"].buf], writes=[scr.buf], tag="gn_sub")
        ctx.op("act", lambda s: s.activation(scm.ap, scr.ap, AF.Square),
               reads=[scr.buf], writes=[scm.buf], tag="gn_x2")
        ctx.op("dve", lambda v: v.tensor_reduce(
            f16["var"].ap, as3d(scm.ap), AX.X, ALU.add),
            reads=[scm.buf], writes=[f16["var"].buf], tag="gn_q")
        ctx.op("dve", lambda v: v.tensor_scalar_mul(
            f16["var"].ap, f16["var"].ap, 1.0 / 64.0),
            reads=[f16["var"].buf], writes=[f16["var"].buf])
        gnc = K.constcol(EPS_GN)
        ctx.op("act", lambda s: s.activation(
            f16["var"].ap, f16["var"].ap, AF.Ln, bias=gnc.ap),
            reads=[f16["var"].buf, gnc.buf], writes=[f16["var"].buf], tag="gn_ln")
        ctx.op("act", lambda s: s.activation(
            f16["rstd"].ap, f16["var"].ap, AF.Exp, scale=-0.5),
            reads=[f16["var"].buf], writes=[f16["rstd"].buf], tag="gn_exp")
        ctx.op("dve", lambda v: v.tensor_tensor(
            as3d(scr.ap), as3d(scr.ap), bc16(f16["rstd"].ap), ALU.mult),
            reads=[scr.buf, f16["rstd"].buf], writes=[scr.buf], tag="gn_mul")
        ctx.op("dve", lambda v: v.tensor_mul(aob.ap, scr.ap, gateb.ap),
               reads=[scr.buf, gateb.buf], writes=[aob.buf], tag="gn_gate")
        if c == 0:
            K.debug_dump("aob", aob)
            K.debug_dump("rstdgn", f16["rstd"], shape=[128, 16])
            K.debug_dump("mugn", f16["mu"], shape=[128, 16])
        K.transpose8(aob, 0, aoT)
        K.proj_tok(P1, aoT, wo, None)
        ctx.op("dve", lambda v, c=c, tile=tile: v.scalar_tensor_tensor(
            tile.ap, P1.ap, 0.0, tile.ap, ALU.add, ALU.add,
            accum_out=K.stats_s_t[c].ap),
            reads=[P1.buf, tile.buf],
            writes=[tile.buf, K.stats_s_t[c].buf], tag="residC2")
        ctx.op("act", lambda s, c=c, tile=tile: s.activation(
            scr.ap, tile.ap, AF.Square, accum_out=K.stats_q_t[c].ap),
            reads=[tile.buf], writes=[scr.buf, K.stats_q_t[c].buf], tag="sqdC2")


def emit_phaseC3(K, prep, m2, rstd2, r_src_ap, r_cast, r_buf=None):
    ctx = K.ctx
    C = K.C
    P0, P1, P2, P3 = K.P0, K.P1, K.P2, K.P3
    TBLK, NB = 512, 8
    win_f, bwin, (brin_d, bbrin) = prep["w_in"]
    wout_bf, bwout, _ = prep["w_out"]
    bout_d, bbout = prep["b_out"]
    brow_in = K.sb("C3_brow_in", [1, 5460], BF16)
    ctx.op("sp", lambda s: s.dma_start(brow_in.ap, brin_d),
           reads=[bbrin], writes=[brow_in.buf], dma=True, tag="ldbrin")
    bout_row = K.sb("C3_bout", [1, 1024], BF16)
    ctx.op("sp", lambda s: s.dma_start(bout_row.ap, bout_d),
           reads=[bbout], writes=[bout_row.buf], dma=True, tag="ldbout2")
    xnb = K.sb("C3_xnb", [128, 1024], BF16)
    xnTb = K.sb("C3_xnTb", [128, 8 * TBLK], BF16)
    wina = [K.sb("C3_wina%d" % i, [128, 8 * 128], BF16) for i in range(2)]
    winb = [K.sb("C3_winb%d" % i, [128, 8 * 128], BF16) for i in range(2)]
    wout = [K.sb("C3_wout%d" % i, [128, 1024], BF16) for i in range(2)]
    gT = K.sb("C3_gT", [128, 22 * TBLK], BF16)
    gT_t = K.carve(gT, 22, TBLK)
    sb2 = [K.sb("C3_sb%d" % i, [128, TBLK], BF16) for i in range(2)]
    rt = [K.sb("C3_rt%d" % i, [128, 1024], BF16) for i in range(2)]
    yblk = K.sb("C3_yblk", [128, 4 * 1024], F32)
    yblk_t = K.carve(yblk, 4, 1024)
    scr = K.sb("C3_scr", [128, 1024], F32)
    r_view = r_src_ap.rearrange("(t p) f -> p t f", p=128)

    for b in range(NB):
        for g in range(4):
            c = 4 * b + g
            tile = K.str_tiles[c]
            ctx.op("dve", lambda v, c=c, tile=tile: v.tensor_scalar(
                xnb.ap, tile.ap, m2.ap[:, c:c + 1], rstd2.ap[:, c:c + 1],
                ALU.subtract, ALU.mult),
                reads=[tile.buf, m2.buf, rstd2.buf], writes=[xnb.buf], tag="ln2")
            K.transpose8(xnb, 0, xnTb,
                         dst_view_fn=lambda ap, g=g: ap.rearrange(
                             "p (dc w) -> p dc w", w=TBLK)[:, :, 128 * g:128 * (g + 1)])
        aslots = [TBuf(P3.ap[:, 0:512], P3.buf), TBuf(P2.ap[:, 0:512], P2.buf)]
        bslots = [TBuf(P2.ap[:, 512:1024], P2.buf), TBuf(P0.ap[:, 0:512], P0.buf)]
        for jc in range(22):
            j0, w = JCO[jc], JCW[jc]
            wa, wb_ = wina[jc % 2], winb[jc % 2]
            aps, bps = aslots[jc % 2], bslots[jc % 2]
            ctx.op("sp", lambda s, j0=j0, w=w, wa=wa: s.dma_start(
                wa.ap.rearrange("p (dc u) -> p dc u", u=128)[:, :, 0:w],
                win_f.rearrange("(dc p) f -> p dc f", p=128)[:, :, j0:j0 + w]),
                reads=[bwin], writes=[wa.buf], dma=True, tag="ldwa")
            ctx.op("sp", lambda s, j0=j0, w=w, wb_=wb_: s.dma_start(
                wb_.ap.rearrange("p (dc u) -> p dc u", u=128)[:, :, 0:w],
                win_f.rearrange("(dc p) f -> p dc f", p=128)[:, :, HID + j0:HID + j0 + w]),
                reads=[bwin], writes=[wb_.buf], dma=True, tag="ldwb")
            for dc in range(8):
                ctx.op("pe", lambda t, dc=dc, w=w, wa=wa, aps=aps: t.matmul(
                    aps.ap[0:w, :], wa.ap[:, 128 * dc:128 * dc + w],
                    xnTb.ap[:, TBLK * dc:TBLK * (dc + 1)],
                    start=(dc == 0), stop=False),
                    reads=[wa.buf, xnTb.buf], writes=[aps.buf], tag="mma")
            ctx.op("pe", lambda t, j0=j0, w=w, aps=aps: t.matmul(
                aps.ap[0:w, :], brow_in.ap[:, j0:j0 + w],
                C["ones512_b"].ap[:, 0:TBLK], start=False, stop=True),
                reads=[brow_in.buf, C["ones512_b"].buf], writes=[aps.buf], tag="mmab")
            for dc in range(8):
                ctx.op("pe", lambda t, dc=dc, w=w, wb_=wb_, bps=bps: t.matmul(
                    bps.ap[0:w, :], wb_.ap[:, 128 * dc:128 * dc + w],
                    xnTb.ap[:, TBLK * dc:TBLK * (dc + 1)],
                    start=(dc == 0), stop=False),
                    reads=[wb_.buf, xnTb.buf], writes=[bps.buf], tag="mmb")
            ctx.op("pe", lambda t, j0=j0, w=w, bps=bps: t.matmul(
                bps.ap[0:w, :], brow_in.ap[:, HID + j0:HID + j0 + w],
                C["ones512_b"].ap[:, 0:TBLK], start=False, stop=True),
                reads=[brow_in.buf, C["ones512_b"].buf], writes=[bps.buf], tag="mmbb")
            sbv = sb2[jc % 2]
            ctx.op("act", lambda s, w=w, bps=bps, sbv=sbv: s.activation(
                sbv.ap[0:w, :], bps.ap[0:w, :], AF.Silu),
                reads=[bps.buf], writes=[sbv.buf], tag="silu")
            ctx.op("dve", lambda v, jc=jc, w=w, aps=aps, sbv=sbv: v.tensor_tensor(
                gT_t[jc].ap[0:w, :], aps.ap[0:w, :], sbv.ap[0:w, :],
                ALU.mult),
                reads=[aps.buf, sbv.buf], writes=[gT_t[jc].buf], tag="gmul3")
        for half in range(2):
            for jc in range(22):
                j0, w = JCO[jc], JCW[jc]
                wu = wout[jc % 2]
                ctx.op("sp", lambda s, j0=j0, w=w, wu=wu: s.dma_start(
                    wu.ap[0:w, :], wout_bf[j0:j0 + w, :]),
                    reads=[bwout], writes=[wu.buf], dma=True, tag="ldwu")
                for g in range(4):
                    yps = P0 if g < 2 else P1
                    yo = 512 * (g % 2)
                    ctx.op("pe", lambda t, jc=jc, g=g, half=half, yps=yps, yo=yo, w=w, wu=wu: t.matmul(
                        yps.ap[:, yo:yo + 512],
                        gT_t[jc].ap[0:w, 128 * g:128 * (g + 1)],
                        wu.ap[0:w, 512 * half:512 * (half + 1)],
                        start=(jc == 0), stop=False),
                        reads=[gT_t[jc].buf, wu.buf], writes=[yps.buf], tag="mmy3")
            for g in range(4):
                yps = P0 if g < 2 else P1
                yo = 512 * (g % 2)
                ctx.op("pe", lambda t, g=g, half=half, yps=yps, yo=yo: t.matmul(
                    yps.ap[:, yo:yo + 512], C["ones_b"].ap,
                    bout_row.ap[:, 512 * half:512 * (half + 1)],
                    start=False, stop=True),
                    reads=[C["ones_b"].buf, bout_row.buf], writes=[yps.buf],
                    tag="mmy3b")
                ctx.op("act", lambda s, g=g, half=half, yps=yps, yo=yo: s.copy(
                    yblk_t[g].ap[:, 512 * half:512 * (half + 1)],
                    yps.ap[:, yo:yo + 512]),
                    reads=[yps.buf], writes=[yblk_t[g].buf], tag="yev3")
        if b == 0:
            K.debug_dump("yblk0", yblk_t[0], shape=[128, 1024])
            K.debug_dump("gT0", gT_t[0], shape=[128, 512])
            K.debug_dump("xnTb0", xnTb, shape=[128, 4096])
        for g in range(4):
            c = 4 * b + g
            tile = K.str_tiles[c]
            rtb = rt[g % 2]
            eng = "pool" if r_cast else "sp"
            ctx.op(eng, lambda s, c=c, rtb=rtb: s.dma_start(
                rtb.ap, r_view[:, c, :]),
                reads=([r_buf] if r_buf is not None else []),
                writes=[rtb.buf], dma=True, tag="ldr")
            ctx.op("dve", lambda v, g=g, tile=tile: v.scalar_tensor_tensor(
                tile.ap, yblk_t[g].ap, 0.0, tile.ap, ALU.add, ALU.add),
                reads=[yblk_t[g].buf, tile.buf], writes=[tile.buf], tag="resC3a")
            ctx.op("dve", lambda v, tile=tile, rtb=rtb, c=c: v.scalar_tensor_tensor(
                tile.ap, rtb.ap, 0.0, tile.ap, ALU.add, ALU.add,
                accum_out=K.stats_s_t[c].ap),
                reads=[rtb.buf, tile.buf],
                writes=[tile.buf, K.stats_s_t[c].buf], tag="resC3b")
            ctx.op("act", lambda s, tile=tile, c=c: s.activation(
                scr.ap, tile.ap, AF.Square, accum_out=K.stats_q_t[c].ap),
                reads=[tile.buf], writes=[scr.buf, K.stats_q_t[c].buf], tag="sqdC3")


def build_launch3():
    inputs = {
        "x": ((T, D), F32),
        "L1_attn_ln1_w": ((D,), F32), "L1_attn_ln1_b": ((D,), F32),
        "L1_attn_wq": ((D, D), F32), "L1_attn_wk": ((D, D), F32),
        "L1_attn_wv": ((D, D), F32), "L1_attn_wg": ((D, D), F32),
        "L1_attn_wo": ((D, D), F32),
        "L1_attn_ln2_w": ((D,), F32), "L1_attn_ln2_b": ((D,), F32),
        "L1_attn_w_in": ((D, 5460), F32), "L1_attn_b_in": ((5460,), F32),
        "L1_attn_w_out": ((HID, D), F32), "L1_attn_b_out": ((D,), F32),
    }
    outputs = {"stream3": ((T, D), F32)}
    k = K(inputs, outputs)
    k.add_const_inputs(["ident", "D16", "q2s", "k3s", "gCt", "ones_b", "ones_f",
                        "onescol_b", "one1_f", "one1_b", "ones512_b"])
    k.alloc_stream()
    k.mk_arena(136 * 1024 - 512)
    k.load_stream_from_dram(k.din["x"], cast=True)
    m1 = k.sb("m1", [128, NCH], F32, perm=True)
    v1 = k.sb("v1", [128, NCH], F32, perm=True)
    rstd1 = k.sb("rstd1", [128, NCH], F32, perm=True)
    m2 = k.sb("m2", [128, NCH], F32, perm=True)
    rstd2 = k.sb("rstd2", [128, NCH], F32, perm=True)
    k.push_region()
    emit_castsC(k)
    prep = emit_prepC(k)
    emit_stats_presweep(k, m1, v1)
    k.pop_region()
    emit_batch_rstd(k, m1, v1, rstd1, EPS_LN)
    import os as _os
    _st3 = int(_os.environ.get("STAGE3", "9"))
    k.push_region()
    emit_phaseC2(k, prep, m1, rstd1, nch=min(NCH, 32 if _st3 >= 2 else int(_os.environ.get("NCH3", "32"))))
    k.pop_region()
    if _st3 < 2:
        k.store_stream_to_dram_f32(k.dout["stream3"])
        k.ctx.finalize()
        return k
    k.ctx.op("dve", lambda v: v.tensor_copy(m2.ap, k.stats_s.ap),
             reads=[b.buf for b in k.stats_s_t], writes=[m2.buf])
    v2 = k.stats_q
    emit_batch_rstd_from(k, m2, v2, rstd2, EPS_LN)
    k.debug_dump("m2", m2, shape=[128, NCH])
    k.debug_dump("rstd2", rstd2, shape=[128, NCH])
    k.push_region()
    emit_phaseC3(k, prep, m2, rstd2, k.din["x"], r_cast=True)
    k.pop_region()
    emit_ln_sweep(k)
    k.store_stream_to_dram_f32(k.dout["stream3"])
    k.ctx.finalize()
    return k


def emit_batch_rstd_from(K, mean, var_tb, rstd, eps):
    ctx = K.ctx
    qbufs = [b.buf for b in K.stats_q_t]
    ctx.op("dve", lambda v: v.tensor_scalar_mul(mean.ap, mean.ap, 1.0 / D),
           reads=[mean.buf], writes=[mean.buf], tag="b2_mean")
    ctx.op("dve", lambda v: v.tensor_scalar_mul(var_tb.ap, var_tb.ap, 1.0 / D),
           reads=qbufs + [var_tb.buf], writes=[var_tb.buf] + qbufs, tag="b2_v")
    ctx.op("dve", lambda v: v.tensor_mul(rstd.ap, mean.ap, mean.ap),
           reads=[mean.buf], writes=[rstd.buf], tag="b2_m2")
    ctx.op("dve", lambda v: v.tensor_sub(var_tb.ap, var_tb.ap, rstd.ap),
           reads=[var_tb.buf, rstd.buf], writes=[var_tb.buf] + qbufs, tag="b2_vm")
    epsc = K.constcol(eps)
    ctx.op("act", lambda s: s.activation(var_tb.ap, var_tb.ap, AF.Ln, bias=epsc.ap),
           reads=[var_tb.buf, epsc.buf], writes=[var_tb.buf] + qbufs, tag="b2_ln")
    ctx.op("act", lambda s: s.activation(rstd.ap, var_tb.ap, AF.Exp, scale=-0.5),
           reads=[var_tb.buf], writes=[rstd.buf], tag="b2_exp")


def emit_castsD(K):
    nc, ctx = K.nc, K.ctx
    out = {}
    for nm in ("W_r", "W_k", "W_v"):
        w_bf = nc.dram_tensor("D_%s_bf" % nm, [1024, 1024], BF16, kind="Internal").ap()
        bw = Buf("D_%s_bf" % nm)
        ctx.op("pool", lambda g, nm=nm, w_bf=w_bf: g.dma_start(
            w_bf, K.din["L1_ffn_" + nm]),
            writes=[bw], dma=True, tag="castD")
        out[nm] = (w_bf, bw)
    return out


def emit_prepD(K, castsD):
    nc, ctx = K.nc, K.ctx
    res = {}
    stage = K.sb("D_stage", [128, 1024], BF16)
    for nm in ("W_r", "W_k", "W_v"):
        w_bf, bw = castsD[nm]
        wT = nc.dram_tensor("D_%sT" % nm, [1024, 1024], BF16, kind="Internal").ap()
        bT = Buf("D_%sT" % nm)
        for dc in range(8):
            ctx.op("sp", lambda s, dc=dc, w_bf=w_bf: s.dma_start(
                stage.ap, w_bf[:, 128 * dc:128 * (dc + 1)], transpose=True),
                reads=[bw], writes=[stage.buf], dma=True, tag="TD", lane="D_stage")
            ctx.op("sp", lambda s, dc=dc, wT=wT: s.dma_start(
                wT[128 * dc:128 * (dc + 1), :], stage.ap),
                reads=[stage.buf], writes=[bT], dma=True, tag="stTD",
                lane="D_%sT" % nm)
        res[nm] = (wT, bT)
    for nm in ("mu_r", "mu_k"):
        rb = K.sb("D_row_" + nm, [1, 1024], BF16)
        ctx.op("pool", lambda g, rb=rb, nm=nm: g.dma_start(
            rb.ap, K.din["L1_ffn_" + nm].rearrange("(a f) -> a f", a=1)),
            writes=[rb.buf], dma=True, tag="ldmu")
        col = K.sb("D_col_" + nm, [128, 8], F32, perm=True)
        om = K.sb("D_om_" + nm, [128, 8], F32, perm=True)
        for dc in range(8):
            ctx.op("pe", lambda t, rb=rb, dc=dc: t.transpose(
                K.PT.ap[:, 2 * dc:2 * dc + 1], rb.ap[:, 128 * dc:128 * (dc + 1)],
                K.C["one1_b"].ap),
                reads=[rb.buf, K.C["one1_b"].buf], writes=[K.PT.buf], tag="Tmu")
        ctx.op("dve", lambda v, col=col: v.tensor_copy(
            col.ap.unsqueeze(2),
            K.PT.ap[:, 0:16].rearrange("p (a b) -> p a b", b=2)[:, :, 0:1]),
            reads=[K.PT.buf], writes=[col.buf], tag="Tmucp")
        ctx.op("dve", lambda v, col=col, om=om: v.tensor_scalar(
            om.ap, col.ap, -1.0, 1.0, ALU.mult, ALU.add),
            reads=[col.buf], writes=[om.buf], tag="ommu")
        res[nm] = (col, om)
    return res


def emit_phaseD(K, prepd):
    ctx = K.ctx
    C = K.C
    P0, P1, P2, P3 = K.P0, K.P1, K.P2, K.P3
    TBLK, NB = 512, 8
    W = {}
    for nm in ("W_r", "W_k", "W_v"):
        wT, bT = prepd[nm]
        w = K.sb("D_" + nm + "T", [128, 8 * 1024], BF16)
        ctx.op("sp", lambda s, w=w, wT=wT: s.dma_start(
            w.ap.rearrange("p (c f) -> p c f", c=8),
            wT.rearrange("(c p) f -> p c f", p=128)),
            reads=[bT], writes=[w.buf], dma=True, tag="ldWD")
        W[nm] = w
    murc, omrc = prepd["mu_r"]
    mukc, omkc = prepd["mu_k"]
    xTb = K.sb("D_xTb", [128, 8 * TBLK], BF16)
    xsh = K.sb("D_xsh", [128, 8 * TBLK], BF16)
    tdif = K.sb("D_tdif", [128, 8 * TBLK], BF16)
    xrT = K.sb("D_xrT", [128, 8 * TBLK], BF16)
    xkT = K.sb("D_xkT", [128, 8 * TBLK], BF16)
    kaT = K.sb("D_kaT", [128, 8 * TBLK], BF16)
    rb2 = K.sb("D_rb2", [128, TBLK], BF16)
    rblk = K.sb("D_rblk", [128, 4 * 1024], F32)
    rblk_t = K.carve(rblk, 4, 1024)
    scr = K.sb("D_scr", [128, 1024], F32)
    pv = K.sb("D_pv", [128, 8], BF16)
    v3 = lambda ap: ap.rearrange("p (dc w) -> p dc w", w=TBLK)
    ctx.op("dve", lambda v: v.memset(pv.ap, 0.0), writes=[pv.buf])

    for b in range(NB):
        if b > 0:
            ctx.op("dve", lambda v: v.tensor_copy(
                pv.ap.unsqueeze(2), v3(xTb.ap)[:, :, TBLK - 1:TBLK]),
                reads=[xTb.buf], writes=[pv.buf], tag="pvcp")
        for g in range(4):
            tile = K.str_tiles[4 * b + g]
            K.transpose8(K.STR, 1024 * (4 * b + g), xTb,
                         dst_view_fn=lambda ap, g=g: v3(ap)[:, :, 128 * g:128 * (g + 1)],
                         rbuf=tile.buf)
        ctx.op("dve", lambda v: v.tensor_copy(
            v3(xsh.ap)[:, :, 1:TBLK], v3(xTb.ap)[:, :, 0:TBLK - 1]),
            reads=[xTb.buf], writes=[xsh.buf], tag="shcp")
        ctx.op("dve", lambda v: v.tensor_copy(
            v3(xsh.ap)[:, :, 0:1], pv.ap.unsqueeze(2)),
            reads=[pv.buf], writes=[xsh.buf], tag="shcp0")
        ctx.op("dve", lambda v: v.tensor_sub(tdif.ap, xTb.ap, xsh.ap),
               reads=[xTb.buf, xsh.buf], writes=[tdif.buf], tag="tdif")
        for dst, mu in ((xrT, murc), (xkT, mukc)):
            ctx.op("dve", lambda v, dst=dst, mu=mu: v.tensor_tensor(
                v3(dst.ap), v3(tdif.ap),
                mu.ap.unsqueeze(2).broadcast_to([128, 8, TBLK]), ALU.mult),
                reads=[tdif.buf, mu.buf], writes=[dst.buf], tag="xmu")
            ctx.op("dve", lambda v, dst=dst: v.tensor_add(dst.ap, dst.ap, xsh.ap),
                   reads=[dst.buf, xsh.buf], writes=[dst.buf], tag="xmu2")
        for g in range(4):
            rps = P0 if g % 2 == 0 else P1
            for half in range(2):
                o = 512 * half
                for cc in range(8):
                    ctx.op("pe", lambda t, g=g, cc=cc, o=o, rps=rps: t.matmul(
                        rps.ap[:, o:o + 512],
                        v3(xrT.ap)[:, cc, 128 * g:128 * (g + 1)],
                        W["W_r"].ap[:, 1024 * cc + o:1024 * cc + o + 512],
                        start=(cc == 0), stop=(cc == 7)),
                        reads=[xrT.buf, W["W_r"].buf], writes=[rps.buf], tag="mmr")
            ctx.op("act", lambda s, rps=rps: s.activation(
                scr.ap, rps.ap, AF.Tanh, scale=0.5),
                reads=[rps.buf], writes=[scr.buf], tag="rtanh")
            ctx.op("dve", lambda v, g=g: v.tensor_scalar(
                rblk_t[g].ap, scr.ap, 0.5, 0.5, ALU.mult, ALU.add),
                reads=[scr.buf], writes=[rblk_t[g].buf], tag="rsig")
        for dc2 in range(8):
            kps = P3 if dc2 % 2 == 0 else P2
            kap = kps.ap[:, 0:TBLK]
            for cc in range(8):
                ctx.op("pe", lambda t, dc2=dc2, cc=cc, kap=kap: t.matmul(
                    kap, W["W_k"].ap[:, 1024 * cc + 128 * dc2:1024 * cc + 128 * dc2 + 128],
                    v3(xkT.ap)[:, cc, :],
                    start=(cc == 0), stop=(cc == 7)),
                    reads=[xkT.buf, W["W_k"].buf], writes=[kps.buf], tag="mmk")
            ctx.op("act", lambda s, kap=kap, kps=kps: s.activation(
                rb2.ap, kap, AF.Relu),
                reads=[kps.buf], writes=[rb2.buf], tag="krelu")
            ctx.op("dve", lambda v, dc2=dc2: v.tensor_mul(
                v3(kaT.ap)[:, dc2, :], rb2.ap, rb2.ap),
                reads=[rb2.buf], writes=[kaT.buf], tag="ksq")
        for g in range(4):
            c = 4 * b + g
            tile = K.str_tiles[c]
            kvps = P0 if g % 2 == 0 else P1
            for half in range(2):
                o = 512 * half
                for cc in range(8):
                    ctx.op("pe", lambda t, g=g, cc=cc, o=o, kvps=kvps: t.matmul(
                        kvps.ap[:, o:o + 512],
                        v3(kaT.ap)[:, cc, 128 * g:128 * (g + 1)],
                        W["W_v"].ap[:, 1024 * cc + o:1024 * cc + o + 512],
                        start=(cc == 0), stop=(cc == 7)),
                        reads=[kaT.buf, W["W_v"].buf], writes=[kvps.buf], tag="mmkv")
            ctx.op("dve", lambda v, g=g, kvps=kvps: v.tensor_mul(
                scr.ap, kvps.ap, rblk_t[g].ap),
                reads=[kvps.buf, rblk_t[g].buf], writes=[scr.buf], tag="kvr")
            ctx.op("dve", lambda v, tile=tile, c=c: v.scalar_tensor_tensor(
                tile.ap, scr.ap, 0.0, tile.ap, ALU.add, ALU.add,
                accum_out=K.stats_s_t[c].ap),
                reads=[scr.buf, tile.buf],
                writes=[tile.buf, K.stats_s_t[c].buf], tag="residD")
            ctx.op("act", lambda s, tile=tile, c=c: s.activation(
                scr.ap, tile.ap, AF.Square, accum_out=K.stats_q_t[c].ap),
                reads=[tile.buf], writes=[scr.buf, K.stats_q_t[c].buf], tag="sqdD")


def emit_final_sweep(K, out_ap):
    ctx = K.ctx
    mean = K.sb("fs_mean", [128, NCH], F32)
    var = K.sb("fs_var", [128, NCH], F32)
    rstd = K.sb("fs_rstd", [128, NCH], F32)
    ctx.op("dve", lambda v: v.tensor_copy(mean.ap, K.stats_s.ap),
           reads=[b.buf for b in K.stats_s_t], writes=[mean.buf], tag="fs_cp")
    emit_batch_rstd_from(K, mean, K.stats_q, rstd, EPS_LN)
    stage = [K.sb("fs_stage%d" % i, [128, 1024], F32) for i in range(2)]
    ov = out_ap.rearrange("(t p) f -> p t f", p=128)
    for c in range(NCH):
        tile = K.str_tiles[c]
        st = stage[c % 2]
        ctx.op("dve", lambda v, c=c, tile=tile, st=st: v.tensor_scalar(
            st.ap, tile.ap, mean.ap[:, c:c + 1], rstd.ap[:, c:c + 1],
            ALU.subtract, ALU.mult),
            reads=[tile.buf, mean.buf, rstd.buf], writes=[st.buf], tag="fs_norm")
        ctx.op("sp", lambda s, c=c, st=st: s.dma_start(ov[:, c, :], st.ap),
               reads=[st.buf], dma=True, tag="fs_st", lane="out" + str(c % 2))


def build_launch4():
    inputs = {
        "x": ((T, D), F32),
        "L1_ffn_W_r": ((D, D), F32), "L1_ffn_W_k": ((D, D), F32),
        "L1_ffn_W_v": ((D, D), F32),
        "L1_ffn_mu_r": ((D,), F32), "L1_ffn_mu_k": ((D,), F32),
    }
    outputs = {"out": ((T, D), F32)}
    k = K(inputs, outputs)
    k.add_const_inputs(["ident", "ones_b", "ones_f", "onescol_b", "one1_f", "one1_b"])
    k.alloc_stream()
    k.mk_arena(136 * 1024 - 512)
    k.load_stream_from_dram(k.din["x"], cast=True)
    k.push_region()
    prepd = emit_prepD(k, castsD)
    emit_phaseD(k, prepd)
    k.pop_region()
    k.push_region()
    emit_final_sweep(k, k.dout["out"])
    k.pop_region()
    k.ctx.finalize()
    return k




def build_launch1(nch=NCH, sweep=True):
    inputs = {
        "x": ((T, D), F32),
        "L0_attn_Wq": ((D, D), F32), "L0_attn_bq": ((D,), F32),
        "L0_attn_Wk": ((D, D), F32), "L0_attn_bk": ((D,), F32),
        "L0_attn_Wv": ((D, D), F32), "L0_attn_bv": ((D,), F32),
        "L0_attn_Wo": ((D, D), F32), "L0_attn_bo": ((D,), F32),
    }
    outputs = {"stream1": ((T, D), F32)}
    k = K(inputs, outputs)
    k.add_const_inputs(["ident", "triU", "normal", "inv_normal",
                        "ones_b", "ones_f", "onescol_b", "one1_f"])
    k.alloc_stream()
    k.load_stream_from_dram(k.din["x"], cast=True)
    emit_phaseA(k, nch=nch)
    if sweep:
        emit_ln_sweep(k)
    k.store_stream_to_dram_f32(k.dout["stream1"])
    k.ctx.finalize()
    return k


def const_inmap():
    c = host_consts()
    return {"c_" + n: np.ascontiguousarray(v) for n, v in c.items()}




ALL_INPUTS = {
    "L0_attn_x": ((T, D), F32),
    "L0_attn_Wq": ((D, D), F32), "L0_attn_bq": ((D,), F32),
    "L0_attn_Wk": ((D, D), F32), "L0_attn_bk": ((D,), F32),
    "L0_attn_Wv": ((D, D), F32), "L0_attn_bv": ((D,), F32),
    "L0_attn_Wo": ((D, D), F32), "L0_attn_bo": ((D,), F32),
    "L0_ffn_W_fc": ((4096, D), F32), "L0_ffn_W_proj": ((D, 4096), F32),
    "L1_attn_ln1_w": ((D,), F32), "L1_attn_ln1_b": ((D,), F32),
    "L1_attn_wq": ((D, D), F32), "L1_attn_wk": ((D, D), F32),
    "L1_attn_wv": ((D, D), F32), "L1_attn_wg": ((D, D), F32),
    "L1_attn_wo": ((D, D), F32),
    "L1_attn_ln2_w": ((D,), F32), "L1_attn_ln2_b": ((D,), F32),
    "L1_attn_w_in": ((D, 5460), F32), "L1_attn_b_in": ((5460,), F32),
    "L1_attn_w_out": ((HID, D), F32), "L1_attn_b_out": ((D,), F32),
    "L1_ffn_W_r": ((D, D), F32), "L1_ffn_W_k": ((D, D), F32),
    "L1_ffn_W_v": ((D, D), F32),
    "L1_ffn_mu_r": ((D,), F32), "L1_ffn_mu_k": ((D,), F32),
}


def build_fused():
    k = K(ALL_INPUTS, {"out": ((T, D), F32)})
    k.add_const_inputs()
    k.alloc_stream()
    k.mk_arena(136 * 1024 - 512)
    m1 = k.sb("m1", [128, NCH], F32, perm=True)
    v1 = k.sb("v1", [128, NCH], F32, perm=True)
    rstd1 = k.sb("rstd1", [128, NCH], F32, perm=True)
    m2 = k.sb("m2", [128, NCH], F32, perm=True)
    rstd2 = k.sb("rstd2", [128, NCH], F32, perm=True)
    r_spill = k.nc.dram_tensor("r_spill", [T, D], BF16, kind="Internal").ap()
    b_rspill = Buf("r_spill")

    k.load_stream_from_dram(k.din["L0_attn_x"], cast=True)
    castsB = emit_castsB(k)
    emit_castsC(k)
    castsD = emit_castsD(k)
    k.push_region()
    emit_phaseA(k)
    k.pop_region()
    k.push_region()
    emit_ln_sweep(k)
    k.pop_region()
    k.push_region()
    wfcT, wprojT = emit_prepB(k, castsB)
    emit_phaseB(k, wfcT, wprojT)
    k.pop_region()
    k.push_region()
    scrS = k.sb("S2_scr", [128, 1024], F32)
    def extra(c):
        tile = k.str_tiles[c]
        k.ctx.op("dve", lambda v, c=c, tile=tile: v.tensor_reduce(
            m1.ap[:, c:c + 1], tile.ap, AX.X, ALU.add),
            reads=[tile.buf], writes=[m1.buf], tag="s2_m1")
        k.ctx.op("act", lambda s, c=c, tile=tile: s.activation(
            scrS.ap, tile.ap, AF.Square, accum_out=v1.ap[:, c:c + 1]),
            reads=[tile.buf], writes=[scrS.buf, v1.buf], tag="s2_v1")
    emit_ln_sweep(k, extra_per_tile=extra)
    k.ctx.op("sp", lambda s: s.dma_start(
        r_spill.rearrange("(t p) f -> p t f", p=128),
        k.STR.ap.rearrange("p (t f) -> p t f", t=NCH)),
        reads=[t.buf for t in k.str_tiles], writes=[b_rspill],
        dma=True, tag="spill_r", lane="r_spill")
    k.pop_region()
    emit_batch_rstd(k, m1, v1, rstd1, EPS_LN)
    k.push_region()
    prep = emit_prepC(k)
    k.pop_region()
    k.push_region()
    emit_phaseC2(k, prep, m1, rstd1)
    k.pop_region()
    k.ctx.op("dve", lambda v: v.tensor_copy(m2.ap, k.stats_s.ap),
             reads=[b.buf for b in k.stats_s_t], writes=[m2.buf], tag="s3cp")
    emit_batch_rstd_from(k, m2, k.stats_q, rstd2, EPS_LN)
    k.push_region()
    emit_phaseC3(k, prep, m2, rstd2, r_spill, r_cast=False, r_buf=b_rspill)
    k.pop_region()
    k.push_region()
    emit_ln_sweep(k)
    k.pop_region()
    k.push_region()
    prepd = emit_prepD(k, castsD)
    emit_phaseD(k, prepd)
    k.pop_region()
    k.push_region()
    emit_final_sweep(k, k.dout["out"])
    k.pop_region()
    k.ctx.finalize()
    return k



_BUILD_CACHE = {}


def _get_fused():
    if "kf" not in _BUILD_CACHE:
        _BUILD_CACHE["kf"] = build_fused()
        _BUILD_CACHE["c"] = const_inmap()
    return _BUILD_CACHE["kf"], _BUILD_CACHE["c"]


def kernel(**inputs):
    k, cm = _get_fused()
    NCORES = 8
    x = np.asarray(inputs["L0_attn_x"], dtype=np.float32)
    base = {("c_" + n): v for n, v in cm.items() if False}
    base = {n: np.ascontiguousarray(v) for n, v in cm.items()}
    for nm in ALL_INPUTS:
        if nm == "L0_attn_x":
            continue
        base[nm] = np.ascontiguousarray(np.asarray(inputs[nm], dtype=np.float32))
    in_maps = [dict(base, L0_attn_x=np.ascontiguousarray(x[c]))
               for c in range(NCORES)]
    res = run_bass_kernel_spmd(k.nc, in_maps, core_ids=list(range(NCORES)))
    return np.stack([np.asarray(res.results[c]["out"]) for c in range(NCORES)],
                    axis=0)


# version 12
# speedup vs baseline: 1.5563x; 1.0108x over previous
import numpy as np
import ml_dtypes
from contextlib import ExitStack

import concourse.bass as bass
import concourse.mybir as mybir
from concourse.bass_utils import run_bass_kernel_spmd

F32 = mybir.dt.float32
BF16 = mybir.dt.bfloat16
AF = mybir.ActivationFunctionType
ALU = mybir.AluOpType
AX = mybir.AxisListType
BF16NP = ml_dtypes.bfloat16

T, D, H, HD = 4096, 1024, 16, 64
NCH = 32
EPS_A = 1e-6
EPS_LN = 1e-5
EPS_GN = 1e-6

ENGS = ("pe", "dve", "act", "pool", "sp")
SECT = {"pe": "tensor", "dve": "vector", "act": "scalar", "pool": "gpsimd", "sp": "sync"}


class Buf:
    __slots__ = ("name", "writer", "readers")
    def __init__(self, name):
        self.name = name
        self.writer = None
        self.readers = []


class TBuf:
    __slots__ = ("ap", "buf")
    def __init__(self, ap, buf):
        self.ap = ap
        self.buf = buf


class Ctx:
    def __init__(self, nc):
        self.nc = nc
        self.stream = {e: [] for e in ENGS}
        self.ccount = {e: 0 for e in ENGS}
        self.qcount = {}
        self.know = {e: {} for e in ENGS}
        self.same_engine_wait = {"pe": False, "dve": True, "act": True,
                                 "pool": True, "sp": False}
        self.nops = 0
        self.pending_barrier = {e: [] for e in ENGS}

    def barrier(self):
        toks = self.barrier_token_all()
        for e in ENGS:
            self.pending_barrier[e] = list(toks)

    def op(self, eng, fn, reads=(), writes=(), dma=False, tag="", lane=None):
        if dma:
            lane = lane or (writes[0].name if writes else "misc")
            key = ("q", lane)
        else:
            key = ("c", eng)
        need = {}
        if self.pending_barrier[eng]:
            for tok in self.pending_barrier[eng]:
                k, v = tok
                if v > need.get(k, 0):
                    need[k] = v
            self.pending_barrier[eng] = []
        def _need(tok):
            k, v = tok
            if v > need.get(k, 0):
                need[k] = v
        for b in reads:
            if b.writer:
                _need(b.writer)
        for b in writes:
            if b.writer:
                _need(b.writer)
            for tok in b.readers:
                _need(tok)
        kn = self.know[eng]
        for k in sorted(need, key=repr):
            v = need[k]
            if k == ("c", eng) and not dma and not self.same_engine_wait[eng]:
                continue
            if kn.get(k, 0) >= v:
                continue
            self.stream[eng].append(("w", k, v))
            kn[k] = v
        if dma:
            self.qcount[lane] = self.qcount.get(lane, 0) + 1
            val = self.qcount[lane] * 16
        else:
            self.ccount[eng] += 1
            val = self.ccount[eng]
        self.stream[eng].append(("o", fn, key, 16 if dma else 1, tag))
        self.nops += 1
        tok = (key, val)
        for b in reads:
            b.readers.append(tok)
        for b in writes:
            b.writer = tok
            b.readers = []
        return tok

    def barrier_token_all(self):
        toks = []
        for e in ENGS:
            if self.ccount[e]:
                toks.append((("c", e), self.ccount[e]))
        for lane, n in self.qcount.items():
            if n:
                toks.append((("q", lane), n * 16))
        return toks

    def finalize(self):
        nc = self.nc
        es = ExitStack()
        sems = {}
        for e in ENGS:
            sems[("c", e)] = es.enter_context(nc.semaphore(f"c_{e}"))
        for i, lane in enumerate(self.qcount):
            sems[("q", lane)] = es.enter_context(nc.semaphore(f"q{i}"))
        for e in ENGS:
            if self.ccount[e]:
                self.stream["sp"].append(("w", ("c", e), self.ccount[e]))
        for lane, n in self.qcount.items():
            if n:
                self.stream["sp"].append(("w", ("q", lane), n * 16))
        with nc.Block() as block:
            for e in ENGS:
                entries = self.stream[e]
                if not entries:
                    continue
                def mk(entries=entries):
                    def run(engobj):
                        for ent in entries:
                            if ent[0] == "w":
                                engobj.wait_ge(sems[ent[1]], ent[2])
                            else:
                                _, fn, key, inc, tag = ent
                                inst = fn(engobj)
                                inst.then_inc(sems[key], inc)
                    return run
                getattr(block, SECT[e])(mk())
        es.close()


def gamma_vec():
    return 1.0 - np.exp2(-5.0 - np.arange(H, dtype=np.float64))


def host_consts():
    c = {}
    c["ident"] = np.eye(128, dtype=BF16NP)
    tri = (np.arange(128)[:, None] <= np.arange(128)[None, :])
    c["triU"] = tri.astype(BF16NP)
    g = gamma_vec()
    tp = np.arange(128)
    dm = np.where(tp[None, :, None] >= tp[:, None, None],
                  (g[None, None, :] ** (-(tp[:, None, None] + 1.0))), 0.0)
    c["DM2"] = dm.transpose(0, 2, 1).reshape(128, H * 128).astype(BF16NP)
    tok = np.arange(T, dtype=np.float64).reshape(NCH, 128).T
    c["normal"] = (tok + 1.0).astype(np.float32)
    c["inv_normal"] = (1.0 / (tok + 1.0)).astype(np.float32)
    c["q2s"] = (g[None, :] ** (tp[:, None] + 1.0)).astype(np.float32)
    c["k3s"] = (g[None, :] ** (127.0 - tp[:, None])).astype(np.float32)
    c["gCt"] = np.broadcast_to((g ** 128.0)[None, :], (64, H)).astype(np.float32).copy()
    c["ones_b"] = np.ones((1, 128), dtype=BF16NP)
    c["ones_f"] = np.ones((1, 128), dtype=np.float32)
    c["onescol_b"] = np.ones((128, 1), dtype=BF16NP)
    c["one1_f"] = np.ones((1, 1), dtype=np.float32)
    c["one1_b"] = np.ones((1, 1), dtype=BF16NP)
    c["ones512_b"] = np.ones((1, 512), dtype=BF16NP)
    return c

CONST_DTYPES = {"ident": BF16, "triU": BF16, "DM2": BF16, "normal": F32,
                "inv_normal": F32, "q2s": F32, "k3s": F32, "gCt": F32,
                "ones_b": BF16, "ones_f": F32, "onescol_b": BF16, "one1_f": F32, "one1_b": BF16,
                "ones512_b": BF16}


class K:
    def __init__(self, inputs, outputs, consts=True):
        nc = bass.Bass("TRN2", target_bir_lowering=False)
        self.nc = nc
        self.ctx = Ctx(nc)
        self.din = {}
        self.dout = {}
        for n, (shape, dt) in inputs.items():
            self.din[n] = nc.dram_tensor(n, list(shape), dt, kind="ExternalInput").ap()
        for n, (shape, dt) in outputs.items():
            self.dout[n] = nc.dram_tensor(n, list(shape), dt, kind="ExternalOutput").ap()
        self.sb_bytes = 0
        self._uniq = 0
        self._arena = None
        self._arena_bytes = 0
        self._region = None
        if consts:
            for n, dt in CONST_DTYPES.items():
                pass
        self.P0 = self.psum("P0", [128, 1024], F32)
        self.P1 = self.psum("P1", [128, 1024], F32)
        self.P2 = self.psum("P2", [128, 1024], F32)
        self.PT = self.psum("PTT", [128, 1024], BF16)
        self.P3 = self.psum("P3", [128, 512], F32)

    def debug_dump(self, name, tb, shape=None, rows=128):
        import os as _os
        if not _os.environ.get("MKDBG"):
            return
        ap = tb.ap
        shape = list(ap.shape) if shape is None else shape
        o = self.nc.dram_tensor("dbg_" + name, shape, F32, kind="ExternalOutput").ap()
        self.ctx.op("pool", lambda s: s.dma_start(o, ap),
                    reads=[tb.buf], dma=True, tag="dbgst")

    def constcol(self, value):
        if not hasattr(self, "_ccols"):
            self._ccols = {}
        if value not in self._ccols:
            tb = self.sb("ccol", [128, 1], F32, perm=True)
            self.ctx.op("dve", lambda v, tb=tb, value=value: v.memset(tb.ap, value),
                        writes=[tb.buf], tag="ccol")
            self._ccols[value] = tb
        return self._ccols[value]

    def uniq(self):
        self._uniq += 1
        return self._uniq

    def sb(self, name, shape, dtype, perm=None):
        per_part = int(np.prod(shape[1:])) * (2 if dtype == BF16 else 4)
        if self._region is None or perm:
            h = self.nc.alloc_sbuf_tensor(f"{name}_{self.uniq()}", list(shape), dtype)
            self.sb_bytes += per_part
            return TBuf(h.ap(), Buf(name))
        off = (self._region + 3) & ~3
        assert off + per_part <= self._arena_bytes, \
            f"arena overflow at {name}: {off}+{per_part} > {self._arena_bytes}"
        self._region = off + per_part
        self.region_peak = max(self.region_peak, self._region)
        base = self._arena.ap()
        if dtype == BF16:
            ap = base[0:shape[0], off // 2: off // 2 + per_part // 2]
        else:
            ap = base.bitcast(F32)[0:shape[0], off // 4: off // 4 + per_part // 4]
        if len(shape) == 2 and shape[1] != ap.shape[1]:
            ap = ap[:, 0:shape[1]]
        return TBuf(ap, Buf(name))

    def mk_arena(self, nbytes):
        self._arena = self.nc.alloc_sbuf_tensor("ARENA", [128, nbytes // 2], BF16)
        self._arena_bytes = nbytes
        self.sb_bytes += nbytes
        self.region_peak = 0

    def push_region(self):
        assert self._region is None
        self._region = 0
        self.ctx.barrier()

    def pop_region(self):
        self._region = None
        self.ctx.barrier()

    def psum(self, name, shape, dtype):
        h = self.nc.alloc_psum_tensor(name, list(shape), dtype)
        return TBuf(h.ap(), Buf(name))

    def carve(self, tb, n, width):
        return [TBuf(tb.ap[:, i * width:(i + 1) * width], Buf(f"{tb.buf.name}[{i}]"))
                for i in range(n)]

    def add_const_inputs(self, names=None):
        names = list(CONST_DTYPES) if names is None else names
        for n in names:
            dt = CONST_DTYPES[n]
            shp = {"ident": (128, 128), "triU": (128, 128), "DM2": (128, H * 128),
                   "normal": (128, NCH), "inv_normal": (128, NCH), "q2s": (128, H),
                   "k3s": (128, H), "gCt": (64, H), "ones_b": (1, 128),
                   "ones_f": (1, 128), "onescol_b": (128, 1), "one1_f": (1, 1),
                   "one1_b": (1, 1), "ones512_b": (1, 512)}[n]
            self.din["c_" + n] = self.nc.dram_tensor("c_" + n, list(shp), dt,
                                                     kind="ExternalInput").ap()
        self.C = {}
        cbuf = Buf("consts")
        for n in names:
            dt = CONST_DTYPES[n]
            src = self.din["c_" + n]
            tb = self.sb("c" + n, list(src.shape), dt, perm=True)
            self.ctx.op("sp", lambda s, tb=tb, src=src: s.dma_start(tb.ap, src),
                        writes=[cbuf], dma=True, tag=f"ld_const_{n}", lane="consts")
            tb.buf = cbuf
            self.C[n] = tb

    def load_stream_from_dram(self, src_ap, cast):
        eng = "pool" if cast else "sp"
        view = src_ap.rearrange("(t p) f -> p t f", p=128)
        self.ctx.op(eng, lambda g: g.dma_start(
            self.STR.ap.rearrange("p (t f) -> p t f", t=NCH), view),
            writes=[self.STR.buf] + [t.buf for t in self.str_tiles],
            dma=True, tag="ld_stream")

    def store_stream_to_dram(self, dst_ap, cast=False):
        eng = "pool" if cast else "sp"
        view = dst_ap.rearrange("(t p) f -> p t f", p=128)
        self.ctx.op(eng, lambda g: g.dma_start(
            view, self.STR.ap.rearrange("p (t f) -> p t f", t=NCH)),
            reads=[t.buf for t in self.str_tiles] + [self.STR.buf],
            dma=True, tag="st_stream")

    def store_stream_to_dram_f32(self, dst_ap):
        view = dst_ap.rearrange("(t p) f -> p t f", p=128)
        self.ctx.op("pool", lambda g: g.dma_start(
            view, self.STR.ap.rearrange("p (t f) -> p t f", t=NCH)),
            reads=[t.buf for t in self.str_tiles] + [self.STR.buf],
            dma=True, tag="st_stream_f32")

    def alloc_stream(self):
        self.STR = self.sb("STR", [128, NCH * 1024], BF16, perm=True)
        self.str_tiles = self.carve(self.STR, NCH, 1024)
        self.stats_s = self.sb("stats_s", [128, NCH], F32, perm=True)
        self.stats_q = self.sb("stats_q", [128, NCH], F32, perm=True)
        self.stats_s_t = self.carve(self.stats_s, NCH, 1)
        self.stats_q_t = self.carve(self.stats_q, NCH, 1)

    def transpose8(self, src_tb, src_off, dst_tb, dst_view_fn=None, n=8, w=128,
                   rbuf=None):
        ctx = self.ctx
        rb = rbuf if rbuf is not None else src_tb.buf
        for i in range(n):
            ctx.op("pe", lambda t, i=i: t.transpose(
                self.PT.ap[:, i * w:(i + 1) * w],
                src_tb.ap[:, src_off + i * w: src_off + (i + 1) * w],
                self.C["ident"].ap),
                reads=[rb, self.C["ident"].buf], writes=[self.PT.buf],
                tag="T")
        dst_ap = dst_tb.ap if dst_view_fn is None else dst_view_fn(dst_tb.ap)
        src_ap = self.PT.ap[:, :n * w]
        ctx.op("act", lambda s: s.copy(dst_ap, src_ap),
               reads=[self.PT.buf], writes=[dst_tb.buf], tag="Tcopy")

    def transpose_heads(self, src_tb, dst_tb, rbuf=None):
        ctx = self.ctx
        rb = rbuf if rbuf is not None else src_tb.buf
        for b in range(2):
            for hh in range(8):
                h = 8 * b + hh
                ctx.op("pe", lambda t, h=h, hh=hh: t.transpose(
                    self.PT.ap[0:64, 128 * hh:128 * (hh + 1)],
                    src_tb.ap[:, 64 * h:64 * (h + 1)],
                    self.C["ident"].ap),
                    reads=[rb, self.C["ident"].buf], writes=[self.PT.buf],
                    tag="Th")
            ctx.op("act", lambda s, b=b: s.copy(
                dst_tb.ap[0:64, 1024 * b:1024 * (b + 1)], self.PT.ap[0:64, 0:1024]),
                reads=[self.PT.buf], writes=[dst_tb.buf], tag="Thcopy")

    def proj_tok(self, dst_ps, xT_tb, w_tb, bias_row=None, eng_bias_sb=None):
        ctx = self.ctx
        for half in range(2):
            o = 512 * half
            for dc in range(8):
                last = (dc == 7 and bias_row is None)
                ctx.op("pe", lambda t, dc=dc, o=o, last=last: t.matmul(
                    dst_ps.ap[:, o:o + 512],
                    xT_tb.ap[:, 128 * dc:128 * (dc + 1)],
                    w_tb.ap[:, 1024 * dc + o: 1024 * dc + o + 512],
                    start=(dc == 0), stop=last),
                    reads=[xT_tb.buf, w_tb.buf], writes=[dst_ps.buf], tag="mmproj")
            if bias_row is not None:
                ctx.op("pe", lambda t, o=o: t.matmul(
                    dst_ps.ap[:, o:o + 512], self.C["ones_b"].ap,
                    bias_row.ap[:, o:o + 512], start=False, stop=True),
                    reads=[bias_row.buf, self.C["ones_b"].buf],
                    writes=[dst_ps.buf], tag="mmbias")


def as3d(ap, d=64):
    return ap.rearrange("p (h d) -> p h d", d=d)


def bc16(ap2d, d=64, n=128):
    return ap2d.unsqueeze(2).broadcast_to([n, ap2d.shape[1], d])


STAGE = 99
def emit_phaseA(K, nch=NCH):
    ctx = K.ctx
    C = K.C
    P0, P1, P2, P3 = K.P0, K.P1, K.P2, K.P3
    pairs = [P0, P1, P2]
    W = {}
    for nm in ("Wq", "Wk", "Wv", "Wo"):
        w = K.sb("A_" + nm, [128, 8 * 1024], BF16)
        src = K.din["L0_attn_" + nm]
        ctx.op("pool", lambda g, w=w, src=src: g.dma_start(
            w.ap.rearrange("p (c f) -> p c f", c=8),
            src.rearrange("(c p) f -> p c f", p=128)),
            writes=[w.buf], dma=True, tag="ldw")
        W[nm] = w
    BR = {}
    for nm in ("bq", "bk", "bv", "bo"):
        b = K.sb("A_" + nm, [1, 1024], BF16)
        src = K.din["L0_attn_" + nm]
        ctx.op("pool", lambda g, b=b, src=src: g.dma_start(
            b.ap, src.rearrange("(a f) -> a f", a=1)),
            writes=[b.buf], dma=True, tag="ldb")
        BR[nm] = b

    xT2 = [K.sb("A_xT%d" % i, [128, 8 * 128], BF16) for i in range(2)]
    qb2 = [K.sb("A_qb%d" % i, [128, 1024], BF16) for i in range(2)]
    kb2 = [K.sb("A_kb%d" % i, [128, 1024], BF16) for i in range(2)]
    vb2 = [K.sb("A_vb%d" % i, [128, 1024], BF16) for i in range(2)]
    ksob = K.sb("A_ksob", [128, 1024], BF16)
    qsib = K.sb("A_qsib", [128, 1024], BF16)
    qpb = K.sb("A_qpb", [128, 1024], BF16)
    vpb = K.sb("A_vpb", [128, 1024], BF16)
    scr = K.sb("A_scr", [128, 1024], F32)
    qT = K.sb("A_qT", [64, 16 * 128], BF16)
    kT = K.sb("A_kT", [64, 16 * 128], BF16)
    scm = K.sb("A_scm", [128, 1024], BF16)
    aob = K.sb("A_aob", [128, 1024], BF16)
    aoT = K.sb("A_aoT", [128, 8 * 128], BF16)
    KVf = K.sb("A_KVf", [64, 1024], F32)
    KVsb = K.sb("A_KV", [64, 1024], BF16)
    cars = {nm: K.sb("A_car_" + nm, [1, 1024], BF16) for nm in ("k", "q", "3", "4")}
    carCS = K.sb("A_carCS", [1, 16], BF16)
    f16 = {nm: K.sb("A_f_" + nm, [128, 16], F32)
           for nm in ("siv", "qsc", "si", "sov", "so", "csk", "alo", "csr",
                      "sg", "cs", "ccs", "cmp", "t0", "cor", "cor2")}
    csb = K.sb("A_csb", [128, 16], BF16)

    ctx.op("dve", lambda v: v.memset(KVf.ap, 0.0), writes=[KVf.buf], tag="kv0")
    ctx.op("dve", lambda v: v.memset(KVsb.ap, 0.0), writes=[KVsb.buf], tag="kv0b")
    for nm in cars:
        ctx.op("dve", lambda v, nm=nm: v.memset(cars[nm].ap, 0.0),
               writes=[cars[nm].buf], tag="car0")
    ctx.op("dve", lambda v: v.memset(carCS.ap, 0.0), writes=[carCS.buf])

    def front(c):
        par = c % 2
        pp = pairs[c % 3]
        tile = K.str_tiles[c]
        K.transpose8(K.STR, 1024 * c, xT2[par], rbuf=tile.buf)
        K.proj_tok(pp, xT2[par], W["Wq"], BR["bq"])
        ctx.op("act", lambda s, pp=pp, par=par: s.activation(
            qb2[par].ap, pp.ap, AF.Sigmoid),
            reads=[pp.buf], writes=[qb2[par].buf], tag="sigq")
        K.proj_tok(pp, xT2[par], W["Wk"], BR["bk"])
        ctx.op("act", lambda s, pp=pp, par=par: s.activation(
            kb2[par].ap, pp.ap, AF.Sigmoid),
            reads=[pp.buf], writes=[kb2[par].buf], tag="sigk")
        K.proj_tok(pp, xT2[par], W["Wv"], BR["bv"])
        ctx.op("act", lambda s, pp=pp, par=par: s.copy(vb2[par].ap, pp.ap),
               reads=[pp.buf], writes=[vb2[par].buf], tag="evacv")

    def cumsum_mm(ps, src_b, car, tag):
        for half in range(2):
            o = 512 * half
            ctx.op("pe", lambda t, o=o, ps=ps, src_b=src_b: t.matmul(
                ps.ap[:, o:o + 512], C["triU"].ap, src_b.ap[:, o:o + 512],
                start=True, stop=False),
                reads=[C["triU"].buf, src_b.buf], writes=[ps.buf], tag="cum" + tag)
            ctx.op("pe", lambda t, o=o, ps=ps, car=car: t.matmul(
                ps.ap[:, o:o + 512], C["ones_b"].ap, car.ap[:, o:o + 512],
                start=False, stop=True),
                reads=[C["ones_b"].buf, car.buf], writes=[ps.buf], tag="cumcar")
        for half in range(2):
            o = 512 * half
            ctx.op("pe", lambda t, o=o, src_b=src_b: t.matmul(
                P3.ap[0:1, 0:512], C["onescol_b"].ap, src_b.ap[:, o:o + 512],
                start=True, stop=False),
                reads=[C["onescol_b"].buf, src_b.buf], writes=[P3.buf], tag="colsum")
            ctx.op("pe", lambda t, o=o, car=car: t.matmul(
                P3.ap[0:1, 0:512], C["one1_b"].ap, car.ap[:, o:o + 512],
                start=False, stop=True),
                reads=[C["one1_b"].buf, car.buf], writes=[P3.buf], tag="colcar")
            ctx.op("dve", lambda v, o=o, car=car: v.tensor_copy(
                car.ap[:, o:o + 512], P3.ap[0:1, 0:512]),
                reads=[P3.buf], writes=[car.buf], tag="carcopy")

    scrB = K.sb("A_scrB", [128, 1024], F32)

    def dot16_ops(ps, qk, dst, tag, scrx, corx):
        return [
            lambda: ctx.op("dve", lambda v: v.scalar_tensor_tensor(
                scrx.ap, ps.ap, EPS_A, qk.ap, ALU.add, ALU.mult),
                reads=[ps.buf, qk.buf], writes=[scrx.buf], tag=tag),
            lambda: ctx.op("dve", lambda v: v.tensor_reduce(
                f16[dst].ap, as3d(scrx.ap), AX.X, ALU.add),
                reads=[scrx.buf], writes=[f16[dst].buf], tag="red_" + tag),
            lambda: ctx.op("dve", lambda v: v.tensor_reduce(
                f16[corx].ap, as3d(ps.ap), AX.X, ALU.add),
                reads=[ps.buf], writes=[f16[corx].buf], tag="redc_" + tag),
            lambda: ctx.op("dve", lambda v: v.tensor_scalar(
                f16[corx].ap, f16[corx].ap, EPS_A, 64.0 * EPS_A * EPS_A,
                ALU.mult, ALU.add),
                reads=[f16[corx].buf], writes=[f16[corx].buf], tag="corsc"),
            lambda: ctx.op("dve", lambda v: v.tensor_add(
                f16[dst].ap, f16[dst].ap, f16[corx].ap),
                reads=[f16[dst].buf, f16[corx].buf], writes=[f16[dst].buf],
                tag="coradd"),
        ]

    def ilv(a, b):
        for x, y in zip(a, b):
            x(); y()

    front(0)
    for c in range(nch):
        par = c % 2
        qb, kb, vb, xT = qb2[par], kb2[par], vb2[par], xT2[par]
        PC = pairs[c % 3]
        PA = pairs[(c + 2) % 3]
        tile = K.str_tiles[c]
        ncol = C["normal"].ap[:, c:c + 1]
        incol = C["inv_normal"].ap[:, c:c + 1]
        if c + 1 < nch:
            front(c + 1)
        PC2 = pairs[(c + 1) % 3]
        cumsum_mm(PC, qb, cars["q"], "q")
        cumsum_mm(PC2, kb, cars["k"], "k")
        ilv(dot16_ops(PC, kb, "sov", "t2", scr, "cor"),
            dot16_ops(PC2, qb, "siv", "t1", scrB, "cor2"))
        ctx.op("dve", lambda v: v.reciprocal(f16["t0"].ap, f16["sov"].ap),
               reads=[f16["sov"].buf], writes=[f16["t0"].buf])
        ctx.op("dve", lambda v: v.reciprocal(f16["qsc"].ap, f16["siv"].ap),
               reads=[f16["siv"].buf], writes=[f16["qsc"].buf])
        ctx.op("dve", lambda v, ncol=ncol: v.tensor_scalar_mul(
            f16["so"].ap, f16["t0"].ap, ncol),
            reads=[f16["t0"].buf], writes=[f16["so"].buf])
        ctx.op("dve", lambda v, ncol=ncol: v.tensor_scalar_mul(
            f16["si"].ap, f16["qsc"].ap, ncol),
            reads=[f16["qsc"].buf], writes=[f16["si"].buf])
        ctx.op("dve", lambda v, kb=kb: v.tensor_tensor(
            as3d(ksob.ap), as3d(kb.ap), bc16(f16["so"].ap), ALU.mult),
            reads=[kb.buf, f16["so"].buf], writes=[ksob.buf], tag="kso")
        ctx.op("dve", lambda v, qb=qb: v.tensor_tensor(
            as3d(qsib.ap), as3d(qb.ap), bc16(f16["si"].ap), ALU.mult),
            reads=[qb.buf, f16["si"].buf], writes=[qsib.buf], tag="qsi")
        cumsum_mm(PC, ksob, cars["3"], "3")
        cumsum_mm(PC2, qsib, cars["4"], "4")
        ilv(dot16_ops(PC, qb, "csk", "t3", scr, "cor"),
            dot16_ops(PC2, kb, "csr", "t4", scrB, "cor2"))
        ctx.op("dve", lambda v, incol=incol: v.tensor_scalar_mul(
            f16["csk"].ap, f16["csk"].ap, incol),
            reads=[f16["csk"].buf], writes=[f16["csk"].buf])
        ctx.op("act", lambda s: s.activation(f16["alo"].ap, f16["csk"].ap, AF.Sigmoid),
               reads=[f16["csk"].buf], writes=[f16["alo"].buf], tag="sig_alloc")
        ctx.op("dve", lambda v, incol=incol: v.tensor_scalar_mul(
            f16["csr"].ap, f16["csr"].ap, incol),
            reads=[f16["csr"].buf], writes=[f16["csr"].buf])
        ctx.op("act", lambda s: s.activation(f16["sg"].ap, f16["csr"].ap, AF.Sigmoid),
               reads=[f16["csr"].buf], writes=[f16["sg"].buf], tag="sig_cs")
        ctx.op("dve", lambda v: v.tensor_scalar(
            f16["t0"].ap, f16["sg"].ap, -1.0, 1.0, ALU.mult, ALU.add),
            reads=[f16["sg"].buf], writes=[f16["t0"].buf])
        ctx.op("dve", lambda v: v.reciprocal(f16["t0"].ap, f16["t0"].ap),
               reads=[f16["t0"].buf], writes=[f16["t0"].buf])
        ctx.op("dve", lambda v: v.tensor_mul(f16["cs"].ap, f16["sg"].ap, f16["t0"].ap),
               reads=[f16["sg"].buf, f16["t0"].buf], writes=[f16["cs"].buf])
        ctx.op("dve", lambda v: v.tensor_copy(csb.ap, f16["cs"].ap),
               reads=[f16["cs"].buf], writes=[csb.buf])
        ctx.op("pe", lambda t, PC=PC: t.matmul(PC.ap[:, 0:16], C["triU"].ap, csb.ap,
                                               start=True, stop=False),
               reads=[C["triU"].buf, csb.buf], writes=[PC.buf], tag="cumcs")
        ctx.op("pe", lambda t, PC=PC: t.matmul(PC.ap[:, 0:16], C["ones_b"].ap,
                                               carCS.ap, start=False, stop=True),
               reads=[C["ones_b"].buf, carCS.buf], writes=[PC.buf])
        ctx.op("dve", lambda v, PC=PC: v.tensor_copy(f16["ccs"].ap, PC.ap[:, 0:16]),
               reads=[PC.buf], writes=[f16["ccs"].buf])
        ctx.op("pe", lambda t: t.matmul(P3.ap[0:1, 0:16], C["onescol_b"].ap, csb.ap,
                                        start=True, stop=False),
               reads=[C["onescol_b"].buf, csb.buf], writes=[P3.buf])
        ctx.op("pe", lambda t: t.matmul(P3.ap[0:1, 0:16], C["one1_b"].ap,
                                        carCS.ap, start=False, stop=True),
               reads=[C["one1_b"].buf, carCS.buf], writes=[P3.buf])
        ctx.op("dve", lambda v: v.tensor_copy(carCS.ap, P3.ap[0:1, 0:16]),
               reads=[P3.buf], writes=[carCS.buf])
        ctx.op("dve", lambda v: v.reciprocal(f16["ccs"].ap, f16["ccs"].ap),
               reads=[f16["ccs"].buf], writes=[f16["ccs"].buf])
        ctx.op("dve", lambda v: v.tensor_mul(f16["cmp"].ap, f16["cs"].ap, f16["ccs"].ap),
               reads=[f16["cs"].buf, f16["ccs"].buf], writes=[f16["cmp"].buf])
        ctx.op("dve", lambda v, ncol=ncol: v.tensor_scalar_mul(
            f16["cmp"].ap, f16["cmp"].ap, ncol),
            reads=[f16["cmp"].buf], writes=[f16["cmp"].buf])
        ctx.op("dve", lambda v, qb=qb: v.tensor_tensor(
            as3d(qpb.ap), as3d(qb.ap), bc16(f16["qsc"].ap), ALU.mult),
            reads=[qb.buf, f16["qsc"].buf], writes=[qpb.buf], tag="qp")
        ctx.op("dve", lambda v, vb=vb: v.tensor_tensor(
            as3d(vpb.ap), as3d(vb.ap), bc16(f16["cmp"].ap), ALU.mult),
            reads=[vb.buf, f16["cmp"].buf], writes=[vpb.buf], tag="vp")
        for h in range(H):
            ctx.op("pe", lambda t, h=h, PC=PC: t.matmul(
                PC.ap[0:64, 64 * h:64 * (h + 1)],
                kb.ap[:, 64 * h:64 * (h + 1)],
                vpb.ap[:, 64 * h:64 * (h + 1)],
                start=True, stop=True),
                reads=[kb.buf, vpb.buf], writes=[PC.buf], tag="kvupd")
        K.transpose_heads(qpb, qT)
        K.transpose_heads(kb, kT)
        for sg in range(4):
            slot = 512 * (sg % 2)
            for hh in range(4):
                h = 4 * sg + hh
                ctx.op("pe", lambda t, hh=hh, h=h: t.matmul(
                    P3.ap[:, 128 * hh:128 * (hh + 1)],
                    kT.ap[0:64, 128 * h:128 * (h + 1)],
                    qT.ap[0:64, 128 * h:128 * (h + 1)],
                    start=True, stop=True),
                    reads=[kT.buf, qT.buf], writes=[P3.buf], tag="scores")
            ctx.op("dve", lambda v, slot=slot: v.tensor_tensor(
                scm.ap[:, slot:slot + 512].rearrange("p (h w) -> p h w", w=128),
                P3.ap.rearrange("p (h w) -> p h w", w=128),
                C["triU"].ap.unsqueeze(1).broadcast_to([128, 4, 128]),
                ALU.mult),
                reads=[P3.buf, C["triU"].buf], writes=[scm.buf], tag="mask")
            for hh in range(4):
                h = 4 * sg + hh
                ctx.op("pe", lambda t, h=h, hh=hh, c=c, slot=slot, PA=PA: t.matmul(
                    PA.ap[:, 64 * h:64 * (h + 1)],
                    scm.ap[:, slot + 128 * hh:slot + 128 * (hh + 1)],
                    vpb.ap[:, 64 * h:64 * (h + 1)], start=True, stop=(c == 0)),
                    reads=[scm.buf, vpb.buf], writes=[PA.buf], tag="intra")
                if c > 0:
                    ctx.op("pe", lambda t, h=h, PA=PA: t.matmul(
                        PA.ap[:, 64 * h:64 * (h + 1)],
                        qT.ap[0:64, 128 * h:128 * (h + 1)],
                        KVsb.ap[0:64, 64 * h:64 * (h + 1)], start=False, stop=True),
                        reads=[qT.buf, KVsb.buf], writes=[PA.buf], tag="inter")
        ctx.op("dve", lambda v, PC=PC: v.tensor_add(KVf.ap, KVf.ap, PC.ap[0:64, :]),
               reads=[KVf.buf, PC.buf], writes=[KVf.buf], tag="kvadd")
        ctx.op("dve", lambda v: v.tensor_copy(KVsb.ap, KVf.ap),
               reads=[KVf.buf], writes=[KVsb.buf], tag="kvevac")
        ctx.op("dve", lambda v, PA=PA: v.tensor_tensor(
            as3d(aob.ap), as3d(PA.ap), bc16(f16["alo"].ap), ALU.mult),
            reads=[PA.buf, f16["alo"].buf], writes=[aob.buf], tag="gate_alloc")
        K.transpose8(aob, 0, aoT)
        K.proj_tok(PA, aoT, W["Wo"], BR["bo"])
        ctx.op("dve", lambda v, c=c, tile=tile, PA=PA: v.scalar_tensor_tensor(
            tile.ap, PA.ap, 0.0, tile.ap, ALU.add, ALU.add,
            accum_out=K.stats_s_t[c].ap),
            reads=[PA.buf, tile.buf],
            writes=[tile.buf, K.stats_s_t[c].buf], tag="resid")
        ctx.op("act", lambda s, c=c, tile=tile: s.activation(
            scr.ap, tile.ap, AF.Square, accum_out=K.stats_q_t[c].ap),
            reads=[tile.buf], writes=[scr.buf, K.stats_q_t[c].buf], tag="sqdump")


def emit_ln_sweep(K, extra_per_tile=None, mean_out=None, rstd_out=None):
    ctx = K.ctx
    mean = mean_out or K.sb("sw_mean", [128, NCH], F32)
    var = K.sb("sw_var", [128, NCH], F32)
    rstd = rstd_out or K.sb("sw_rstd", [128, NCH], F32)
    sbufs = [t.buf for t in K.stats_s_t]
    qbufs = [t.buf for t in K.stats_q_t]
    ctx.op("dve", lambda v: v.tensor_scalar_mul(mean.ap, K.stats_s.ap, 1.0 / D),
           reads=sbufs + [K.stats_s.buf], writes=[mean.buf], tag="sw_mean")
    ctx.op("dve", lambda v: v.tensor_scalar_mul(var.ap, K.stats_q.ap, 1.0 / D),
           reads=qbufs + [K.stats_q.buf], writes=[var.buf], tag="sw_var")
    ctx.op("dve", lambda v: v.tensor_mul(K.stats_q.ap, mean.ap, mean.ap),
           reads=[mean.buf], writes=[K.stats_q.buf] + qbufs, tag="sw_m2")
    ctx.op("dve", lambda v: v.tensor_sub(var.ap, var.ap, K.stats_q.ap),
           reads=[var.buf, K.stats_q.buf], writes=[var.buf], tag="sw_vm")
    epsc = K.constcol(EPS_LN)
    ctx.op("act", lambda s: s.activation(var.ap, var.ap, AF.Ln, bias=epsc.ap),
           reads=[var.buf, epsc.buf], writes=[var.buf], tag="sw_ln")
    ctx.op("act", lambda s: s.activation(rstd.ap, var.ap, AF.Exp, scale=-0.5),
           reads=[var.buf], writes=[rstd.buf], tag="sw_exp")
    for c in range(NCH):
        tile = K.str_tiles[c]
        ctx.op("dve", lambda v, c=c, tile=tile: v.tensor_scalar(
            tile.ap, tile.ap, mean.ap[:, c:c + 1], rstd.ap[:, c:c + 1],
            ALU.subtract, ALU.mult),
            reads=[tile.buf, mean.buf, rstd.buf], writes=[tile.buf], tag="sw_norm")
        if extra_per_tile is not None:
            extra_per_tile(c)
    return mean, rstd


def emit_castsB(K):
    nc, ctx = K.nc, K.ctx
    wfc_bf = nc.dram_tensor("wfc_bf", [4096, 1024], BF16, kind="Internal").ap()
    wproj_bf = nc.dram_tensor("wproj_bf", [1024, 4096], BF16, kind="Internal").ap()
    bfc = Buf("wfc_bf"); bpj = Buf("wproj_bf")
    ctx.op("pool", lambda g: g.dma_start(wfc_bf, K.din["L0_ffn_W_fc"]),
           writes=[bfc], dma=True, tag="castWfc")
    ctx.op("pool", lambda g: g.dma_start(wproj_bf, K.din["L0_ffn_W_proj"]),
           writes=[bpj], dma=True, tag="castWproj")
    return (wfc_bf, bfc), (wproj_bf, bpj)


def emit_prepB(K, casts):
    nc, ctx = K.nc, K.ctx
    (wfc_bf, bfc), (wproj_bf, bpj) = casts
    wfcT_s = nc.dram_tensor("wfcT_s", [32, 128, 8, 128], BF16, kind="Internal").ap()
    wprojT_s = nc.dram_tensor("wprojT_s", [32, 128, 1024], BF16, kind="Internal").ap()
    bfcT = Buf("wfcT_s"); bpjT = Buf("wprojT_s")
    stg = [K.sb("B_stage%d" % i, [128, 4096], BF16) for i in range(2)]
    for dc in range(8):
        st = stg[dc % 2]
        ctx.op("sp", lambda s, dc=dc, st=st: s.dma_start(
            st.ap, wfc_bf[:, 128 * dc:128 * (dc + 1)], transpose=True),
            reads=[bfc], writes=[st.buf], dma=True, tag="Tfc",
            lane="B_stage%d" % (dc % 2)),
        ctx.op("act", lambda s, dc=dc, st=st: s.dma_start(
            wfcT_s[:, :, dc, :].rearrange("fc p w -> p fc w"),
            st.ap.rearrange("p (fc w) -> p fc w", w=128)),
            reads=[st.buf], writes=[bfcT], dma=True, tag="stfc", lane="wfcT_s")
    for fc in range(32):
        st = stg[fc % 2]
        ctx.op("sp", lambda s, fc=fc, st=st: s.dma_start(
            st.ap[:, 0:1024], wproj_bf[:, 128 * fc:128 * (fc + 1)],
            transpose=True),
            reads=[bpj], writes=[st.buf], dma=True, tag="Tpj",
            lane="B_stage%d" % (fc % 2)),
        ctx.op("act", lambda s, fc=fc, st=st: s.dma_start(
            wprojT_s[fc, :, :], st.ap[:, 0:1024]),
            reads=[st.buf], writes=[bpjT], dma=True, tag="stpj", lane="wprojT_s")
    return (wfcT_s, bfcT), (wprojT_s, bpjT)


def emit_phaseB(K, wfcT, wprojT):
    ctx = K.ctx
    C = K.C
    (wfcT_s, bfcT), (wprojT_s, bpjT) = wfcT, wprojT
    P0, P1, P2, P3 = K.P0, K.P1, K.P2, K.P3
    PTf = TBuf(K.PT.ap.bitcast(F32), K.PT.buf)
    TBLK, NB, GRP = 512, 8, 2
    xTb = K.sb("B_xTb", [128, 8 * TBLK], BF16)
    wfc_g = [K.sb("B_wfcg%d" % i, [128, GRP * 8 * 128], BF16) for i in range(2)]
    wpj_g = [K.sb("B_wpjg%d" % i, [128, GRP * 1024], BF16) for i in range(2)]
    hbT_full = K.sb("B_hbTf", [128, 32 * TBLK], BF16)
    hbT_t = K.carve(hbT_full, 32, TBLK)
    rb = K.sb("B_rb", [128, TBLK], BF16)
    yblk2 = [K.sb("B_yblk%d" % i, [128, 4 * 1024], F32) for i in range(2)]
    yblk_t2 = [K.carve(y, 4, 1024) for y in yblk2]
    mem = {e: K.sb("B_mem" + e, [128, 1024], F32) for e in ("dve", "pool")}
    spk = {e: K.sb("B_spk" + e, [128, 1024], F32) for e in ("dve", "pool")}
    ssum = {e: K.sb("B_ssum" + e, [128, 1024], F32) for e in ("dve", "pool")}
    scr = K.sb("B_scr", [128, 1024], F32)
    ybank = [TBuf(P0.ap[:, 0:512], P0.buf), TBuf(P0.ap[:, 512:1024], P0.buf),
             TBuf(P1.ap[:, 0:512], P1.buf), TBuf(P1.ap[:, 512:1024], P1.buf),
             TBuf(P2.ap[:, 0:512], P2.buf), TBuf(P2.ap[:, 512:1024], P2.buf),
             TBuf(P3.ap[:, 0:512], P3.buf), TBuf(PTf.ap[:, 0:512], K.PT.buf)]

    def emit_block(b):
        for g in range(4):
            tile = K.str_tiles[4 * b + g]
            K.transpose8(K.STR, 1024 * (4 * b + g), xTb,
                         dst_view_fn=lambda ap, g=g: ap.rearrange(
                             "p (dc w) -> p dc w", w=TBLK)[:, :, 128 * g:128 * (g + 1)],
                         rbuf=tile.buf)
        hrot = [TBuf(P3.ap[:, 0:512], P3.buf), TBuf(P2.ap[:, 0:512], P2.buf),
                TBuf(P2.ap[:, 512:1024], P2.buf)]
        for fcg in range(32 // GRP):
            wf = wfc_g[fcg % 2]
            ctx.op("sp", lambda s, fcg=fcg, wf=wf: s.dma_start(
                wf.ap.rearrange("p (fc q) -> p fc q", q=8 * 128),
                wfcT_s[GRP * fcg:GRP * (fcg + 1), :, :, :].rearrange(
                    "fc p dc w -> p fc (dc w)")),
                reads=[bfcT], writes=[wf.buf], dma=True, tag="ld_wfc")
            drain(4)
            for fci in range(GRP):
                fc = GRP * fcg + fci
                hps = hrot[fc % 3]
                for dc in range(8):
                    ctx.op("pe", lambda t, fci=fci, dc=dc, wf=wf, hps=hps: t.matmul(
                        hps.ap, wf.ap[:, (fci * 8 + dc) * 128:(fci * 8 + dc + 1) * 128],
                        xTb.ap[:, TBLK * dc:TBLK * (dc + 1)],
                        start=(dc == 0), stop=(dc == 7)),
                        reads=[wf.buf, xTb.buf], writes=[hps.buf], tag="mmh")
                ctx.op("act", lambda s, hps=hps: s.activation(rb.ap, hps.ap, AF.Relu),
                       reads=[hps.buf], writes=[rb.buf], tag="relu")
                ctx.op("dve", lambda v, fc=fc: v.tensor_mul(
                    hbT_t[fc].ap, rb.ap, rb.ap),
                    reads=[rb.buf], writes=[hbT_t[fc].buf], tag="sq")
        for fcg in range(32 // GRP):
            wp = wpj_g[fcg % 2]
            ctx.op("sp", lambda s, fcg=fcg, wp=wp: s.dma_start(
                wp.ap.rearrange("p (fc q) -> p fc q", q=1024),
                wprojT_s[GRP * fcg:GRP * (fcg + 1), :, :].rearrange(
                    "fc p q -> p fc q")),
                reads=[bpjT], writes=[wp.buf], dma=True, tag="ld_wpj")
            drain(4)
            for fci in range(GRP):
                fc = GRP * fcg + fci
                for g in range(4):
                    for half in range(2):
                        yb = ybank[2 * g + half]
                        ctx.op("pe", lambda t, fci=fci, fc=fc, g=g, half=half, yb=yb, wp=wp: t.matmul(
                            yb.ap, hbT_t[fc].ap[:, 128 * g:128 * (g + 1)],
                            wp.ap[:, 1024 * fci + 512 * half:1024 * fci + 512 * (half + 1)],
                            start=(fc == 0), stop=(fc == 31)),
                            reads=[hbT_t[fc].buf, wp.buf], writes=[yb.buf], tag="mmy")
        yblk_t = yblk_t2[b % 2]
        for g in range(4):
            for half in range(2):
                yb = ybank[2 * g + half]
                ctx.op("act", lambda s, g=g, half=half, yb=yb, yblk_t=yblk_t: s.copy(
                    yblk_t[g].ap[:, 512 * half:512 * (half + 1)], yb.ap),
                    reads=[yb.buf], writes=[yblk_t[g].buf], tag="yevac")
    pending = []

    def drain(n):
        for _ in range(n):
            if pending:
                pending.pop(0)()

    def scan_block(bs):
        TH = pending.append
        negone = K.constcol(-1.0)
        yblk_t = yblk_t2[bs % 2]
        for g in range(4):
            e = "dve" if g % 2 == 0 else "pool"
            y = yblk_t[g]
            me, sp_, su = mem[e], spk[e], ssum[e]
            tile = K.str_tiles[4 * bs + g]
            c = 4 * bs + g
            TH(lambda y=y, sp_=sp_: ctx.op("act", lambda s, y=y, sp_=sp_: s.activation(
                sp_.ap, y.ap, AF.Sign, bias=negone.ap),
                reads=[y.buf, negone.buf], writes=[sp_.buf], tag="scan_sg"))
            TH(lambda sp_=sp_: ctx.op("act", lambda s, sp_=sp_: s.activation(sp_.ap, sp_.ap, AF.Relu),
                   reads=[sp_.buf], writes=[sp_.buf], tag="scan_rl"))
            TH(lambda sp_=sp_, su=su: ctx.op("dve", lambda v, sp_=sp_, su=su: v.tensor_copy(su.ap, sp_.ap),
                   reads=[sp_.buf], writes=[su.buf], tag="scan_su"))
            for step in range(4):
                TH(lambda y=y, sp_=sp_: ctx.op("dve", lambda v, y=y, sp_=sp_: v.tensor_sub(
                    scr.ap, y.ap, sp_.ap),
                    reads=[y.buf, sp_.buf], writes=[scr.buf], tag="scan_t"))
                if step == 0:
                    TH(lambda me=me, y=y: ctx.op("dve", lambda v, me=me, y=y: v.scalar_tensor_tensor(
                        me.ap, y.ap, 0.9, scr.ap, ALU.mult, ALU.add),
                        reads=[y.buf, scr.buf], writes=[me.buf], tag="scan_m0"))
                else:
                    TH(lambda me=me: ctx.op("dve", lambda v, me=me: v.scalar_tensor_tensor(
                        me.ap, me.ap, 0.9, scr.ap, ALU.mult, ALU.add),
                        reads=[me.buf, scr.buf], writes=[me.buf], tag="scan_m"))
                TH(lambda me=me, sp_=sp_: ctx.op("act", lambda s, me=me, sp_=sp_: s.activation(
                    sp_.ap, me.ap, AF.Sign, bias=negone.ap),
                    reads=[me.buf, negone.buf], writes=[sp_.buf], tag="scan_sg2"))
                TH(lambda sp_=sp_: ctx.op("act", lambda s, sp_=sp_: s.activation(sp_.ap, sp_.ap, AF.Relu),
                       reads=[sp_.buf], writes=[sp_.buf], tag="scan_rl2"))
                TH(lambda su=su, sp_=sp_: ctx.op("dve", lambda v, su=su, sp_=sp_: v.tensor_add(
                    su.ap, su.ap, sp_.ap),
                    reads=[su.buf, sp_.buf], writes=[su.buf], tag="scan_su2"))
            TH(lambda su=su, tile=tile, c=c: ctx.op("dve", lambda v, su=su, tile=tile, c=c: v.scalar_tensor_tensor(
                tile.ap, su.ap, 0.2, tile.ap, ALU.mult, ALU.add,
                accum_out=K.stats_s_t[c].ap),
                reads=[su.buf, tile.buf],
                writes=[tile.buf, K.stats_s_t[c].buf], tag="residB"))
            TH(lambda tile=tile, c=c: ctx.op("act", lambda s, tile=tile, c=c: s.activation(
                scr.ap, tile.ap, AF.Square, accum_out=K.stats_q_t[c].ap),
                reads=[tile.buf], writes=[scr.buf, K.stats_q_t[c].buf], tag="sqdumpB"))

    for b in range(NB):
        emit_block(b)
        scan_block(b)
    drain(1000)


def build_launch2():
    inputs = {
        "x": ((T, D), F32),
        "L0_ffn_W_fc": ((4096, D), F32),
        "L0_ffn_W_proj": ((D, 4096), F32),
    }
    outputs = {"stream2": ((T, D), F32)}
    k = K(inputs, outputs)
    k.add_const_inputs(["ident", "ones_b", "ones_f", "onescol_b", "one1_f"])
    k.alloc_stream()
    k.load_stream_from_dram(k.din["x"], cast=True)
    wfcT, wprojT = emit_prepB(k, emit_castsB(k))
    emit_phaseB(k, wfcT, wprojT)
    emit_ln_sweep(k)
    k.store_stream_to_dram_f32(k.dout["stream2"])
    k.ctx.finalize()
    return k



JCW = [128] * 21 + [42]
JCO = [0]
for _w in JCW[:-1]:
    JCO.append(JCO[-1] + _w)
HID = 2730


def emit_batch_rstd(K, mean, var, rstd, eps):
    ctx = K.ctx
    ctx.op("dve", lambda v: v.tensor_scalar_mul(mean.ap, mean.ap, 1.0 / D),
           reads=[mean.buf], writes=[mean.buf], tag="br_mean")
    ctx.op("dve", lambda v: v.tensor_scalar_mul(var.ap, var.ap, 1.0 / D),
           reads=[var.buf], writes=[var.buf], tag="br_v")
    ctx.op("dve", lambda v: v.tensor_mul(rstd.ap, mean.ap, mean.ap),
           reads=[mean.buf], writes=[rstd.buf], tag="br_m2")
    ctx.op("dve", lambda v: v.tensor_sub(var.ap, var.ap, rstd.ap),
           reads=[var.buf, rstd.buf], writes=[var.buf], tag="br_vm")
    epsc = K.constcol(eps)
    ctx.op("act", lambda s: s.activation(var.ap, var.ap, AF.Ln, bias=epsc.ap),
           reads=[var.buf, epsc.buf], writes=[var.buf], tag="br_ln")
    ctx.op("act", lambda s: s.activation(rstd.ap, var.ap, AF.Exp, scale=-0.5),
           reads=[var.buf], writes=[rstd.buf], tag="br_exp")


def emit_stats_presweep(K, mean, var):
    ctx = K.ctx
    scr = K.sb("psw_scr", [128, 1024], F32)
    for c in range(NCH):
        tile = K.str_tiles[c]
        ctx.op("dve", lambda v, c=c, tile=tile: v.tensor_reduce(
            mean.ap[:, c:c + 1], tile.ap, AX.X, ALU.add),
            reads=[tile.buf], writes=[mean.buf], tag="psw_s")
        ctx.op("act", lambda s, c=c, tile=tile: s.activation(
            scr.ap, tile.ap, AF.Square, accum_out=var.ap[:, c:c + 1]),
            reads=[tile.buf], writes=[scr.buf, var.buf], tag="psw_q")


def emit_castsC(K):
    nc, ctx = K.nc, K.ctx
    wo_bf = nc.dram_tensor("C_wo_bf", [1024, 1024], BF16, kind="Internal").ap()
    bwo = Buf("C_wo_bf")
    ctx.op("pool", lambda g: g.dma_start(wo_bf, K.din["L1_attn_wo"]),
           writes=[bwo], dma=True, tag="castwo")
    wout_bf = nc.dram_tensor("C_wout_bf", [2730, 1024], BF16, kind="Internal").ap()
    bwout = Buf("C_wout_bf")
    ctx.op("pool", lambda g: g.dma_start(wout_bf, K.din["L1_attn_w_out"]),
           writes=[bwout], dma=True, tag="castwout")
    K._castsC = ((wo_bf, bwo), (wout_bf, bwout))


def emit_prepC(K):
    nc, ctx = K.nc, K.ctx
    res = {}
    cols = {}
    for nm in ("ln1_w", "ln1_b", "ln2_w", "ln2_b"):
        rb = K.sb("C_row_" + nm, [1, 1024], BF16)
        ctx.op("pool", lambda g, rb=rb, nm=nm: g.dma_start(
            rb.ap, K.din["L1_attn_" + nm].rearrange("(a f) -> a f", a=1)),
            writes=[rb.buf], dma=True, tag="ldrow")
        col = K.sb("C_col_" + nm, [128, 8], F32)
        for dc in range(8):
            ctx.op("pe", lambda t, rb=rb, dc=dc: t.transpose(
                K.PT.ap[:, 2 * dc:2 * dc + 1], rb.ap[:, 128 * dc:128 * (dc + 1)],
                K.C["one1_b"].ap),
                reads=[rb.buf, K.C["one1_b"].buf], writes=[K.PT.buf], tag="Trow")
        ctx.op("dve", lambda v, col=col: v.tensor_copy(
            col.ap.unsqueeze(2),
            K.PT.ap[:, 0:16].rearrange("p (a b) -> p a b", b=2)[:, :, 0:1]),
            reads=[K.PT.buf], writes=[col.buf], tag="Trowcp")
        cols[nm] = col
    wtile2 = [K.sb("C_wtile%d" % i, [128, 1024], F32) for i in range(2)]
    wtileb2 = [K.sb("C_wtileb%d" % i, [128, 1024], BF16) for i in range(2)]
    bcol2 = [K.sb("C_bcol%d" % i, [128, 1], F32) for i in range(2)]
    for nm, scale in (("wq", HD ** -0.5), ("wk", 1.0), ("wv", 1.0), ("wg", 1.0)):
        dst = nc.dram_tensor("C_%s_f" % nm, [1024, 1024], BF16, kind="Internal").ap()
        bdst = Buf("C_%s_f" % nm)
        brow = K.sb("C_brow_" + nm, [1, 1024], BF16)
        src = K.din["L1_attn_" + nm]
        for dc in range(8):
            wtile, wtileb, bcol = wtile2[dc % 2], wtileb2[dc % 2], bcol2[dc % 2]
            ctx.op("sp", lambda s, src=src, dc=dc, wtile=wtile: s.dma_start(
                wtile.ap, src[128 * dc:128 * (dc + 1), :]),
                writes=[wtile.buf], dma=True, tag="ldw")
            ctx.op("dve", lambda v, dc=dc, scale=scale, wtile=wtile, wtileb=wtileb: v.tensor_scalar(
                wtileb.ap, wtile.ap, cols["ln1_w"].ap[:, dc:dc + 1], scale,
                ALU.mult, ALU.mult),
                reads=[wtile.buf, cols["ln1_w"].buf], writes=[wtileb.buf], tag="fold")
            ctx.op("sp", lambda s, dst=dst, dc=dc, wtileb=wtileb: s.dma_start(
                dst[128 * dc:128 * (dc + 1), :], wtileb.ap),
                reads=[wtileb.buf], writes=[bdst], dma=True, tag="stw")
            ctx.op("dve", lambda v, dc=dc, scale=scale, bcol=bcol: v.tensor_scalar(
                bcol.ap, cols["ln1_b"].ap[:, dc:dc + 1], scale, None,
                ALU.mult, ALU.bypass),
                reads=[cols["ln1_b"].buf], writes=[bcol.buf], tag="bcol")
            ctx.op("pe", lambda t, dc=dc, bcol=bcol, wtile=wtile: t.matmul(
                K.P3.ap[0:1, 0:512], bcol.ap, wtile.ap[:, 0:512],
                start=(dc == 0), stop=(dc == 7)),
                reads=[bcol.buf, wtile.buf], writes=[K.P3.buf], tag="mmbr0")
            ctx.op("pe", lambda t, dc=dc, bcol=bcol, wtile=wtile: t.matmul(
                K.P2.ap[0:1, 0:512], bcol.ap, wtile.ap[:, 512:1024],
                start=(dc == 0), stop=(dc == 7)),
                reads=[bcol.buf, wtile.buf], writes=[K.P2.buf], tag="mmbr1")
        ctx.op("dve", lambda v, brow=brow: v.tensor_copy(
            brow.ap[:, 0:512], K.P3.ap[0:1, 0:512]),
            reads=[K.P3.buf], writes=[brow.buf], tag="brcp0")
        ctx.op("dve", lambda v, brow=brow: v.tensor_copy(
            brow.ap[:, 512:1024], K.P2.ap[0:1, 0:512]),
            reads=[K.P2.buf], writes=[brow.buf], tag="brcp1")
        brow_d = nc.dram_tensor("C_brow_%s_d" % nm, [1, 1024], BF16,
                                kind="Internal").ap()
        bbrd = Buf("C_brow_%s_d" % nm)
        ctx.op("sp", lambda s, brow=brow, brow_d=brow_d: s.dma_start(brow_d, brow.ap),
               reads=[brow.buf], writes=[bbrd], dma=True, tag="stbrow")
        res[nm] = (dst, bdst, (brow_d, bbrd))
    win_f = nc.dram_tensor("C_win_f", [1024, 5460], BF16, kind="Internal").ap()
    bwin = Buf("C_win_f")
    wint = K.sb("C_wint", [128, 5460], F32)
    wintb = K.sb("C_wintb", [128, 5460], BF16)
    browf = K.sb("C_browf", [1, 5460], F32)
    ctx.op("dve", lambda v: v.memset(browf.ap, 0.0), writes=[browf.buf])
    NSL = 11
    for dc in range(8):
        ctx.op("sp", lambda s, dc=dc: s.dma_start(
            wint.ap, K.din["L1_attn_w_in"][128 * dc:128 * (dc + 1), :]),
            writes=[wint.buf], dma=True, tag="ldwin")
        ctx.op("dve", lambda v, dc=dc: v.tensor_scalar_mul(
            wintb.ap, wint.ap, cols["ln2_w"].ap[:, dc:dc + 1]),
            reads=[wint.buf, cols["ln2_w"].buf], writes=[wintb.buf], tag="foldin")
        ctx.op("sp", lambda s, dc=dc: s.dma_start(
            win_f[128 * dc:128 * (dc + 1), :], wintb.ap),
            reads=[wintb.buf], writes=[bwin], dma=True, tag="stwin")
        for sl in range(NSL):
            o = 512 * sl
            w = min(512, 5460 - o)
            ctx.op("pe", lambda t, dc=dc, o=o, w=w: t.matmul(
                K.P3.ap[0:1, 0:w], cols["ln2_b"].ap[:, dc:dc + 1],
                wint.ap[:, o:o + w], start=True, stop=True),
                reads=[cols["ln2_b"].buf, wint.buf], writes=[K.P3.buf], tag="mmbin")
            ctx.op("dve", lambda v, o=o, w=w: v.tensor_add(
                browf.ap[:, o:o + w], browf.ap[:, o:o + w], K.P3.ap[0:1, 0:w]),
                reads=[K.P3.buf, browf.buf], writes=[browf.buf], tag="accbin")
    binrow_f = K.sb("C_binrow_f", [1, 5460], F32)
    ctx.op("pool", lambda g: g.dma_start(
        binrow_f.ap, K.din["L1_attn_b_in"].rearrange("(a f) -> a f", a=1)),
        writes=[binrow_f.buf], dma=True, tag="ldbin")
    brow_in = K.sb("C_brow_in", [1, 5460], BF16)
    ctx.op("dve", lambda v: v.tensor_add(brow_in.ap, browf.ap, binrow_f.ap),
           reads=[browf.buf, binrow_f.buf], writes=[brow_in.buf], tag="binrow")
    brin_d = nc.dram_tensor("C_brow_in_d", [1, 5460], BF16, kind="Internal").ap()
    bbrin = Buf("C_brow_in_d")
    ctx.op("sp", lambda s: s.dma_start(brin_d, brow_in.ap),
           reads=[brow_in.buf], writes=[bbrin], dma=True, tag="stbrin")
    res["w_in"] = (win_f, bwin, (brin_d, bbrin))
    (wo_bf, bwo), (wout_bf, bwout) = K._castsC
    bout_d = nc.dram_tensor("C_bout_d", [1, 1024], BF16, kind="Internal").ap()
    bbout = Buf("C_bout_d")
    ctx.op("pool", lambda g: g.dma_start(
        bout_d, K.din["L1_attn_b_out"].rearrange("(a f) -> a f", a=1)),
        writes=[bbout], dma=True, tag="ldbout")
    res["wo"] = (wo_bf, bwo, None)
    res["w_out"] = (wout_bf, bwout, None)
    res["b_out"] = (bout_d, bbout)
    return res


def emit_phaseC2(K, prep, m1, rstd1, nch=NCH):
    ctx = K.ctx
    C = K.C
    P0, P1, P2, P3 = K.P0, K.P1, K.P2, K.P3
    W = {}
    for nm in ("wq", "wk", "wv", "wg"):
        w = K.sb("C2_" + nm, [128, 8 * 1024], BF16)
        dst, bdst, (brow_d, bbrd) = prep[nm]
        ctx.op("sp", lambda s, w=w, dst=dst: s.dma_start(
            w.ap.rearrange("p (c f) -> p c f", c=8),
            dst.rearrange("(c p) f -> p c f", p=128)),
            reads=[bdst], writes=[w.buf], dma=True, tag="ldwf")
        brow = K.sb("C2_brow_" + nm, [1, 1024], BF16)
        ctx.op("sp", lambda s, brow=brow, brow_d=brow_d: s.dma_start(
            brow.ap, brow_d),
            reads=[bbrd], writes=[brow.buf], dma=True, tag="ldbrow")
        W[nm] = (w, brow)
    wo = K.sb("C2_wo", [128, 8 * 1024], BF16)
    ctx.op("sp", lambda s: s.dma_start(
        wo.ap.rearrange("p (c f) -> p c f", c=8),
        prep["wo"][0].rearrange("(c p) f -> p c f", p=128)),
        reads=[prep["wo"][1]], writes=[wo.buf], dma=True, tag="ldwo")

    xnb = K.sb("C2_xnb", [128, 1024], BF16)
    xnT = K.sb("C2_xnT", [128, 8 * 128], BF16)
    qb = K.sb("C2_qb", [128, 1024], BF16)
    kb = K.sb("C2_kb", [128, 1024], BF16)
    vb = K.sb("C2_vb", [128, 1024], BF16)
    gb = K.sb("C2_gb", [128, 1024], BF16)
    q2b = K.sb("C2_q2b", [128, 1024], BF16)
    k3b = K.sb("C2_k3b", [128, 1024], BF16)
    gateb = K.sb("C2_gateb", [128, 1024], BF16)
    q2T = K.sb("C2_q2T", [64, 2048], BF16)
    kT = K.sb("C2_kT", [64, 2048], BF16)
    scm = K.sb("C2_scm", [128, 1024], BF16)
    aob = K.sb("C2_aob", [128, 1024], BF16)
    aoT = K.sb("C2_aoT", [128, 8 * 128], BF16)
    scr = K.sb("C2_scr", [128, 1024], F32)
    Sf = K.sb("C2_Sf", [64, 1024], F32)
    Ssb = K.sb("C2_Ssb", [64, 1024], BF16)
    f16 = {nm: K.sb("C2_f_" + nm, [128, 16], F32)
           for nm in ("mu", "sq", "var", "rstd")}
    ctx.op("dve", lambda v: v.memset(Sf.ap, 0.0), writes=[Sf.buf])
    ctx.op("dve", lambda v: v.memset(Ssb.ap, 0.0), writes=[Ssb.buf])

    for c in range(nch):
        tile = K.str_tiles[c]
        ctx.op("dve", lambda v, c=c, tile=tile: v.tensor_scalar(
            xnb.ap, tile.ap, m1.ap[:, c:c + 1], rstd1.ap[:, c:c + 1],
            ALU.subtract, ALU.mult),
            reads=[tile.buf, m1.buf, rstd1.buf], writes=[xnb.buf], tag="ln1")
        K.transpose8(xnb, 0, xnT)
        K.proj_tok(P0, xnT, W["wq"][0], W["wq"][1])
        ctx.op("act", lambda s: s.copy(qb.ap, P0.ap),
               reads=[P0.buf], writes=[qb.buf], tag="evq")
        K.proj_tok(P1, xnT, W["wk"][0], W["wk"][1])
        ctx.op("act", lambda s: s.copy(kb.ap, P1.ap),
               reads=[P1.buf], writes=[kb.buf], tag="evk")
        K.proj_tok(P0, xnT, W["wv"][0], W["wv"][1])
        ctx.op("act", lambda s: s.copy(vb.ap, P0.ap),
               reads=[P0.buf], writes=[vb.buf], tag="evv")
        K.proj_tok(P1, xnT, W["wg"][0], W["wg"][1])
        ctx.op("act", lambda s: s.copy(gb.ap, P1.ap),
               reads=[P1.buf], writes=[gb.buf], tag="evg")
        if c == 0:
            K.debug_dump("xnb", xnb)
            K.debug_dump("qb", qb)
            K.debug_dump("kb", kb)
            K.debug_dump("vb", vb)
            K.debug_dump("gb", gb)
        ctx.op("act", lambda s: s.activation(scr.ap, gb.ap, AF.Exp, scale=-1.0),
               reads=[gb.buf], writes=[scr.buf], tag="gexp")
        ctx.op("dve", lambda v: v.tensor_scalar_add(scr.ap, scr.ap, 1.0),
               reads=[scr.buf], writes=[scr.buf], tag="g1")
        ctx.op("dve", lambda v: v.reciprocal(scr.ap, scr.ap),
               reads=[scr.buf], writes=[scr.buf], tag="grec")
        ctx.op("dve", lambda v: v.tensor_mul(gateb.ap, gb.ap, scr.ap),
               reads=[gb.buf, scr.buf], writes=[gateb.buf], tag="gmul")
        if c == 0:
            K.debug_dump("gateb", gateb)
        ctx.op("dve", lambda v: v.tensor_tensor(
            as3d(q2b.ap), as3d(qb.ap), bc16(C["q2s"].ap), ALU.mult),
            reads=[qb.buf, C["q2s"].buf], writes=[q2b.buf], tag="q2")
        ctx.op("dve", lambda v: v.tensor_tensor(
            as3d(k3b.ap), as3d(kb.ap), bc16(C["k3s"].ap), ALU.mult),
            reads=[kb.buf, C["k3s"].buf], writes=[k3b.buf], tag="k3")
        K.transpose_heads(q2b, q2T)
        K.transpose_heads(kb, kT)
        for grp in range(2):
            for hh in range(8):
                h = grp * 8 + hh
                ctx.op("pe", lambda t, hh=hh, h=h: t.matmul(
                    P1.ap[:, 128 * hh:128 * (hh + 1)],
                    kT.ap[0:64, 128 * h:128 * (h + 1)],
                    q2T.ap[0:64, 128 * h:128 * (h + 1)],
                    start=True, stop=True),
                    reads=[kT.buf, q2T.buf], writes=[P1.buf], tag="rsc")
            ctx.op("dve", lambda v, grp=grp: v.tensor_tensor(
                scm.ap.rearrange("p (h w) -> p h w", w=128),
                P1.ap.rearrange("p (h w) -> p h w", w=128),
                C["DM2"].ap[:, 1024 * grp:1024 * (grp + 1)].rearrange(
                    "p (h w) -> p h w", w=128),
                ALU.mult),
                reads=[P1.buf, C["DM2"].buf], writes=[scm.buf], tag="rmask")
            for hh in range(8):
                h = grp * 8 + hh
                ctx.op("pe", lambda t, h=h, hh=hh, c=c: t.matmul(
                    P0.ap[:, 64 * h:64 * (h + 1)],
                    scm.ap[:, 128 * hh:128 * (hh + 1)],
                    vb.ap[:, 64 * h:64 * (h + 1)], start=True, stop=(c == 0)),
                    reads=[scm.buf, vb.buf], writes=[P0.buf], tag="rintra")
                if c > 0:
                    ctx.op("pe", lambda t, h=h: t.matmul(
                        P0.ap[:, 64 * h:64 * (h + 1)],
                        q2T.ap[0:64, 128 * h:128 * (h + 1)],
                        Ssb.ap[0:64, 64 * h:64 * (h + 1)], start=False, stop=True),
                        reads=[q2T.buf, Ssb.buf], writes=[P0.buf], tag="rinter")
        for h in range(H):
            ctx.op("pe", lambda t, h=h: t.matmul(
                P2.ap[0:64, 64 * h:64 * (h + 1)],
                k3b.ap[:, 64 * h:64 * (h + 1)],
                vb.ap[:, 64 * h:64 * (h + 1)], start=True, stop=True),
                reads=[k3b.buf, vb.buf], writes=[P2.buf], tag="rupd")
        ctx.op("dve", lambda v: v.tensor_tensor(
            Sf.ap.rearrange("p (h d) -> p h d", d=64),
            Sf.ap.rearrange("p (h d) -> p h d", d=64),
            C["gCt"].ap.unsqueeze(2).broadcast_to([64, H, 64]),
            ALU.mult),
            reads=[Sf.buf, C["gCt"].buf], writes=[Sf.buf], tag="sdecay")
        ctx.op("dve", lambda v: v.tensor_add(Sf.ap, Sf.ap, P2.ap[0:64, :]),
               reads=[Sf.buf, P2.buf], writes=[Sf.buf], tag="sadd")
        ctx.op("dve", lambda v: v.tensor_copy(Ssb.ap, Sf.ap),
               reads=[Sf.buf], writes=[Ssb.buf], tag="scp")
        ctx.op("dve", lambda v: v.tensor_reduce(
            f16["mu"].ap, as3d(P0.ap), AX.X, ALU.add),
            reads=[P0.buf], writes=[f16["mu"].buf], tag="gn_s")
        ctx.op("dve", lambda v: v.tensor_scalar_mul(
            f16["mu"].ap, f16["mu"].ap, 1.0 / 64.0),
            reads=[f16["mu"].buf], writes=[f16["mu"].buf])
        ctx.op("dve", lambda v: v.tensor_tensor(
            as3d(scr.ap), as3d(P0.ap), bc16(f16["mu"].ap), ALU.subtract),
            reads=[P0.buf, f16["mu"].buf], writes=[scr.buf], tag="gn_sub")
        ctx.op("act", lambda s: s.activation(scm.ap, scr.ap, AF.Square),
               reads=[scr.buf], writes=[scm.buf], tag="gn_x2")
        ctx.op("dve", lambda v: v.tensor_reduce(
            f16["var"].ap, as3d(scm.ap), AX.X, ALU.add),
            reads=[scm.buf], writes=[f16["var"].buf], tag="gn_q")
        ctx.op("dve", lambda v: v.tensor_scalar_mul(
            f16["var"].ap, f16["var"].ap, 1.0 / 64.0),
            reads=[f16["var"].buf], writes=[f16["var"].buf])
        gnc = K.constcol(EPS_GN)
        ctx.op("act", lambda s: s.activation(
            f16["var"].ap, f16["var"].ap, AF.Ln, bias=gnc.ap),
            reads=[f16["var"].buf, gnc.buf], writes=[f16["var"].buf], tag="gn_ln")
        ctx.op("act", lambda s: s.activation(
            f16["rstd"].ap, f16["var"].ap, AF.Exp, scale=-0.5),
            reads=[f16["var"].buf], writes=[f16["rstd"].buf], tag="gn_exp")
        ctx.op("dve", lambda v: v.tensor_tensor(
            as3d(scr.ap), as3d(scr.ap), bc16(f16["rstd"].ap), ALU.mult),
            reads=[scr.buf, f16["rstd"].buf], writes=[scr.buf], tag="gn_mul")
        ctx.op("dve", lambda v: v.tensor_mul(aob.ap, scr.ap, gateb.ap),
               reads=[scr.buf, gateb.buf], writes=[aob.buf], tag="gn_gate")
        if c == 0:
            K.debug_dump("aob", aob)
            K.debug_dump("rstdgn", f16["rstd"], shape=[128, 16])
            K.debug_dump("mugn", f16["mu"], shape=[128, 16])
        K.transpose8(aob, 0, aoT)
        K.proj_tok(P1, aoT, wo, None)
        ctx.op("dve", lambda v, c=c, tile=tile: v.scalar_tensor_tensor(
            tile.ap, P1.ap, 0.0, tile.ap, ALU.add, ALU.add,
            accum_out=K.stats_s_t[c].ap),
            reads=[P1.buf, tile.buf],
            writes=[tile.buf, K.stats_s_t[c].buf], tag="residC2")
        ctx.op("act", lambda s, c=c, tile=tile: s.activation(
            scr.ap, tile.ap, AF.Square, accum_out=K.stats_q_t[c].ap),
            reads=[tile.buf], writes=[scr.buf, K.stats_q_t[c].buf], tag="sqdC2")


def emit_phaseC3(K, prep, m2, rstd2, r_src_ap, r_cast, r_buf=None):
    ctx = K.ctx
    C = K.C
    P0, P1, P2, P3 = K.P0, K.P1, K.P2, K.P3
    TBLK, NB = 512, 8
    win_f, bwin, (brin_d, bbrin) = prep["w_in"]
    wout_bf, bwout, _ = prep["w_out"]
    bout_d, bbout = prep["b_out"]
    brow_in = K.sb("C3_brow_in", [1, 5460], BF16)
    ctx.op("sp", lambda s: s.dma_start(brow_in.ap, brin_d),
           reads=[bbrin], writes=[brow_in.buf], dma=True, tag="ldbrin")
    bout_row = K.sb("C3_bout", [1, 1024], BF16)
    ctx.op("sp", lambda s: s.dma_start(bout_row.ap, bout_d),
           reads=[bbout], writes=[bout_row.buf], dma=True, tag="ldbout2")
    xnb = K.sb("C3_xnb", [128, 1024], BF16)
    xnTb = K.sb("C3_xnTb", [128, 8 * TBLK], BF16)
    wina = [K.sb("C3_wina%d" % i, [128, 8 * 128], BF16) for i in range(2)]
    winb = [K.sb("C3_winb%d" % i, [128, 8 * 128], BF16) for i in range(2)]
    wout = [K.sb("C3_wout%d" % i, [128, 1024], BF16) for i in range(2)]
    gT = K.sb("C3_gT", [128, 22 * TBLK], BF16)
    gT_t = K.carve(gT, 22, TBLK)
    sb2 = [K.sb("C3_sb%d" % i, [128, TBLK], BF16) for i in range(2)]
    rt = [K.sb("C3_rt%d" % i, [128, 1024], BF16) for i in range(2)]
    yblk = K.sb("C3_yblk", [128, 4 * 1024], F32)
    yblk_t = K.carve(yblk, 4, 1024)
    scr = K.sb("C3_scr", [128, 1024], F32)
    r_view = r_src_ap.rearrange("(t p) f -> p t f", p=128)

    for b in range(NB):
        for g in range(4):
            c = 4 * b + g
            tile = K.str_tiles[c]
            ctx.op("dve", lambda v, c=c, tile=tile: v.tensor_scalar(
                xnb.ap, tile.ap, m2.ap[:, c:c + 1], rstd2.ap[:, c:c + 1],
                ALU.subtract, ALU.mult),
                reads=[tile.buf, m2.buf, rstd2.buf], writes=[xnb.buf], tag="ln2")
            K.transpose8(xnb, 0, xnTb,
                         dst_view_fn=lambda ap, g=g: ap.rearrange(
                             "p (dc w) -> p dc w", w=TBLK)[:, :, 128 * g:128 * (g + 1)])
        aslots = [TBuf(P3.ap[:, 0:512], P3.buf), TBuf(P2.ap[:, 0:512], P2.buf),
                  TBuf(P0.ap[:, 0:512], P0.buf)]
        bslots = [TBuf(P2.ap[:, 512:1024], P2.buf), TBuf(P0.ap[:, 512:1024], P0.buf),
                  TBuf(P1.ap[:, 0:512], P1.buf)]
        for jc in range(22):
            j0, w = JCO[jc], JCW[jc]
            wa, wb_ = wina[jc % 2], winb[jc % 2]
            aps, bps = aslots[jc % 3], bslots[jc % 3]
            ctx.op("sp", lambda s, j0=j0, w=w, wa=wa: s.dma_start(
                wa.ap.rearrange("p (dc u) -> p dc u", u=128)[:, :, 0:w],
                win_f.rearrange("(dc p) f -> p dc f", p=128)[:, :, j0:j0 + w]),
                reads=[bwin], writes=[wa.buf], dma=True, tag="ldwa")
            ctx.op("sp", lambda s, j0=j0, w=w, wb_=wb_: s.dma_start(
                wb_.ap.rearrange("p (dc u) -> p dc u", u=128)[:, :, 0:w],
                win_f.rearrange("(dc p) f -> p dc f", p=128)[:, :, HID + j0:HID + j0 + w]),
                reads=[bwin], writes=[wb_.buf], dma=True, tag="ldwb")
            for dc in range(8):
                ctx.op("pe", lambda t, dc=dc, w=w, wa=wa, aps=aps: t.matmul(
                    aps.ap[0:w, :], wa.ap[:, 128 * dc:128 * dc + w],
                    xnTb.ap[:, TBLK * dc:TBLK * (dc + 1)],
                    start=(dc == 0), stop=False),
                    reads=[wa.buf, xnTb.buf], writes=[aps.buf], tag="mma")
            ctx.op("pe", lambda t, j0=j0, w=w, aps=aps: t.matmul(
                aps.ap[0:w, :], brow_in.ap[:, j0:j0 + w],
                C["ones512_b"].ap[:, 0:TBLK], start=False, stop=True),
                reads=[brow_in.buf, C["ones512_b"].buf], writes=[aps.buf], tag="mmab")
            for dc in range(8):
                ctx.op("pe", lambda t, dc=dc, w=w, wb_=wb_, bps=bps: t.matmul(
                    bps.ap[0:w, :], wb_.ap[:, 128 * dc:128 * dc + w],
                    xnTb.ap[:, TBLK * dc:TBLK * (dc + 1)],
                    start=(dc == 0), stop=False),
                    reads=[wb_.buf, xnTb.buf], writes=[bps.buf], tag="mmb")
            ctx.op("pe", lambda t, j0=j0, w=w, bps=bps: t.matmul(
                bps.ap[0:w, :], brow_in.ap[:, HID + j0:HID + j0 + w],
                C["ones512_b"].ap[:, 0:TBLK], start=False, stop=True),
                reads=[brow_in.buf, C["ones512_b"].buf], writes=[bps.buf], tag="mmbb")
            sbv = sb2[jc % 2]
            ctx.op("act", lambda s, w=w, bps=bps, sbv=sbv: s.activation(
                sbv.ap[0:w, :], bps.ap[0:w, :], AF.Silu),
                reads=[bps.buf], writes=[sbv.buf], tag="silu")
            ctx.op("dve", lambda v, jc=jc, w=w, aps=aps, sbv=sbv: v.tensor_tensor(
                gT_t[jc].ap[0:w, :], aps.ap[0:w, :], sbv.ap[0:w, :],
                ALU.mult),
                reads=[aps.buf, sbv.buf], writes=[gT_t[jc].buf], tag="gmul3")
        for half in range(2):
            for jc in range(22):
                j0, w = JCO[jc], JCW[jc]
                wu = wout[jc % 2]
                ctx.op("sp", lambda s, j0=j0, w=w, wu=wu: s.dma_start(
                    wu.ap[0:w, :], wout_bf[j0:j0 + w, :]),
                    reads=[bwout], writes=[wu.buf], dma=True, tag="ldwu")
                for g in range(4):
                    yps = P0 if g < 2 else P1
                    yo = 512 * (g % 2)
                    ctx.op("pe", lambda t, jc=jc, g=g, half=half, yps=yps, yo=yo, w=w, wu=wu: t.matmul(
                        yps.ap[:, yo:yo + 512],
                        gT_t[jc].ap[0:w, 128 * g:128 * (g + 1)],
                        wu.ap[0:w, 512 * half:512 * (half + 1)],
                        start=(jc == 0), stop=False),
                        reads=[gT_t[jc].buf, wu.buf], writes=[yps.buf], tag="mmy3")
            for g in range(4):
                yps = P0 if g < 2 else P1
                yo = 512 * (g % 2)
                ctx.op("pe", lambda t, g=g, half=half, yps=yps, yo=yo: t.matmul(
                    yps.ap[:, yo:yo + 512], C["ones_b"].ap,
                    bout_row.ap[:, 512 * half:512 * (half + 1)],
                    start=False, stop=True),
                    reads=[C["ones_b"].buf, bout_row.buf], writes=[yps.buf],
                    tag="mmy3b")
                ctx.op("act", lambda s, g=g, half=half, yps=yps, yo=yo: s.copy(
                    yblk_t[g].ap[:, 512 * half:512 * (half + 1)],
                    yps.ap[:, yo:yo + 512]),
                    reads=[yps.buf], writes=[yblk_t[g].buf], tag="yev3")
        if b == 0:
            K.debug_dump("yblk0", yblk_t[0], shape=[128, 1024])
            K.debug_dump("gT0", gT_t[0], shape=[128, 512])
            K.debug_dump("xnTb0", xnTb, shape=[128, 4096])
        for g in range(4):
            c = 4 * b + g
            tile = K.str_tiles[c]
            rtb = rt[g % 2]
            eng = "pool" if r_cast else "sp"
            ctx.op(eng, lambda s, c=c, rtb=rtb: s.dma_start(
                rtb.ap, r_view[:, c, :]),
                reads=([r_buf] if r_buf is not None else []),
                writes=[rtb.buf], dma=True, tag="ldr")
            ctx.op("dve", lambda v, g=g, tile=tile: v.scalar_tensor_tensor(
                tile.ap, yblk_t[g].ap, 0.0, tile.ap, ALU.add, ALU.add),
                reads=[yblk_t[g].buf, tile.buf], writes=[tile.buf], tag="resC3a")
            ctx.op("dve", lambda v, tile=tile, rtb=rtb, c=c: v.scalar_tensor_tensor(
                tile.ap, rtb.ap, 0.0, tile.ap, ALU.add, ALU.add,
                accum_out=K.stats_s_t[c].ap),
                reads=[rtb.buf, tile.buf],
                writes=[tile.buf, K.stats_s_t[c].buf], tag="resC3b")
            ctx.op("act", lambda s, tile=tile, c=c: s.activation(
                scr.ap, tile.ap, AF.Square, accum_out=K.stats_q_t[c].ap),
                reads=[tile.buf], writes=[scr.buf, K.stats_q_t[c].buf], tag="sqdC3")


def build_launch3():
    inputs = {
        "x": ((T, D), F32),
        "L1_attn_ln1_w": ((D,), F32), "L1_attn_ln1_b": ((D,), F32),
        "L1_attn_wq": ((D, D), F32), "L1_attn_wk": ((D, D), F32),
        "L1_attn_wv": ((D, D), F32), "L1_attn_wg": ((D, D), F32),
        "L1_attn_wo": ((D, D), F32),
        "L1_attn_ln2_w": ((D,), F32), "L1_attn_ln2_b": ((D,), F32),
        "L1_attn_w_in": ((D, 5460), F32), "L1_attn_b_in": ((5460,), F32),
        "L1_attn_w_out": ((HID, D), F32), "L1_attn_b_out": ((D,), F32),
    }
    outputs = {"stream3": ((T, D), F32)}
    k = K(inputs, outputs)
    k.add_const_inputs()
    k.alloc_stream()
    k.mk_arena(136 * 1024 - 512)
    k.load_stream_from_dram(k.din["x"], cast=True)
    m1 = k.sb("m1", [128, NCH], F32, perm=True)
    v1 = k.sb("v1", [128, NCH], F32, perm=True)
    rstd1 = k.sb("rstd1", [128, NCH], F32, perm=True)
    m2 = k.sb("m2", [128, NCH], F32, perm=True)
    rstd2 = k.sb("rstd2", [128, NCH], F32, perm=True)
    k.push_region()
    emit_castsC(k)
    prep = emit_prepC(k)
    emit_stats_presweep(k, m1, v1)
    k.pop_region()
    emit_batch_rstd(k, m1, v1, rstd1, EPS_LN)
    import os as _os
    _st3 = int(_os.environ.get("STAGE3", "9"))
    k.push_region()
    emit_phaseC2(k, prep, m1, rstd1, nch=min(NCH, 32 if _st3 >= 2 else int(_os.environ.get("NCH3", "32"))))
    k.pop_region()
    if _st3 < 2:
        k.store_stream_to_dram_f32(k.dout["stream3"])
        k.ctx.finalize()
        return k
    k.ctx.op("dve", lambda v: v.tensor_copy(m2.ap, k.stats_s.ap),
             reads=[b.buf for b in k.stats_s_t], writes=[m2.buf])
    v2 = k.stats_q
    emit_batch_rstd_from(k, m2, v2, rstd2, EPS_LN)
    k.debug_dump("m2", m2, shape=[128, NCH])
    k.debug_dump("rstd2", rstd2, shape=[128, NCH])
    k.push_region()
    emit_phaseC3(k, prep, m2, rstd2, k.din["x"], r_cast=True)
    k.pop_region()
    emit_ln_sweep(k)
    k.store_stream_to_dram_f32(k.dout["stream3"])
    k.ctx.finalize()
    return k


def emit_batch_rstd_from(K, mean, var_tb, rstd, eps):
    ctx = K.ctx
    qbufs = [b.buf for b in K.stats_q_t]
    ctx.op("dve", lambda v: v.tensor_scalar_mul(mean.ap, mean.ap, 1.0 / D),
           reads=[mean.buf], writes=[mean.buf], tag="b2_mean")
    ctx.op("dve", lambda v: v.tensor_scalar_mul(var_tb.ap, var_tb.ap, 1.0 / D),
           reads=qbufs + [var_tb.buf], writes=[var_tb.buf] + qbufs, tag="b2_v")
    ctx.op("dve", lambda v: v.tensor_mul(rstd.ap, mean.ap, mean.ap),
           reads=[mean.buf], writes=[rstd.buf], tag="b2_m2")
    ctx.op("dve", lambda v: v.tensor_sub(var_tb.ap, var_tb.ap, rstd.ap),
           reads=[var_tb.buf, rstd.buf], writes=[var_tb.buf] + qbufs, tag="b2_vm")
    epsc = K.constcol(eps)
    ctx.op("act", lambda s: s.activation(var_tb.ap, var_tb.ap, AF.Ln, bias=epsc.ap),
           reads=[var_tb.buf, epsc.buf], writes=[var_tb.buf] + qbufs, tag="b2_ln")
    ctx.op("act", lambda s: s.activation(rstd.ap, var_tb.ap, AF.Exp, scale=-0.5),
           reads=[var_tb.buf], writes=[rstd.buf], tag="b2_exp")


def emit_castsD(K):
    nc, ctx = K.nc, K.ctx
    out = {}
    for nm in ("W_r", "W_k", "W_v"):
        w_bf = nc.dram_tensor("D_%s_bf" % nm, [1024, 1024], BF16, kind="Internal").ap()
        bw = Buf("D_%s_bf" % nm)
        ctx.op("pool", lambda g, nm=nm, w_bf=w_bf: g.dma_start(
            w_bf, K.din["L1_ffn_" + nm]),
            writes=[bw], dma=True, tag="castD")
        out[nm] = (w_bf, bw)
    return out


def emit_prepD(K, castsD):
    nc, ctx = K.nc, K.ctx
    res = {}
    stage = K.sb("D_stage", [128, 1024], BF16)
    for nm in ("W_r", "W_k", "W_v"):
        w_bf, bw = castsD[nm]
        wT = nc.dram_tensor("D_%sT" % nm, [1024, 1024], BF16, kind="Internal").ap()
        bT = Buf("D_%sT" % nm)
        for dc in range(8):
            ctx.op("sp", lambda s, dc=dc, w_bf=w_bf: s.dma_start(
                stage.ap, w_bf[:, 128 * dc:128 * (dc + 1)], transpose=True),
                reads=[bw], writes=[stage.buf], dma=True, tag="TD", lane="D_stage")
            ctx.op("sp", lambda s, dc=dc, wT=wT: s.dma_start(
                wT[128 * dc:128 * (dc + 1), :], stage.ap),
                reads=[stage.buf], writes=[bT], dma=True, tag="stTD",
                lane="D_%sT" % nm)
        res[nm] = (wT, bT)
    for nm in ("mu_r", "mu_k"):
        rb = K.sb("D_row_" + nm, [1, 1024], BF16)
        ctx.op("pool", lambda g, rb=rb, nm=nm: g.dma_start(
            rb.ap, K.din["L1_ffn_" + nm].rearrange("(a f) -> a f", a=1)),
            writes=[rb.buf], dma=True, tag="ldmu")
        col = K.sb("D_col_" + nm, [128, 8], F32, perm=True)
        om = K.sb("D_om_" + nm, [128, 8], F32, perm=True)
        for dc in range(8):
            ctx.op("pe", lambda t, rb=rb, dc=dc: t.transpose(
                K.PT.ap[:, 2 * dc:2 * dc + 1], rb.ap[:, 128 * dc:128 * (dc + 1)],
                K.C["one1_b"].ap),
                reads=[rb.buf, K.C["one1_b"].buf], writes=[K.PT.buf], tag="Tmu")
        ctx.op("dve", lambda v, col=col: v.tensor_copy(
            col.ap.unsqueeze(2),
            K.PT.ap[:, 0:16].rearrange("p (a b) -> p a b", b=2)[:, :, 0:1]),
            reads=[K.PT.buf], writes=[col.buf], tag="Tmucp")
        ctx.op("dve", lambda v, col=col, om=om: v.tensor_scalar(
            om.ap, col.ap, -1.0, 1.0, ALU.mult, ALU.add),
            reads=[col.buf], writes=[om.buf], tag="ommu")
        res[nm] = (col, om)
    return res


def emit_phaseD(K, prepd):
    ctx = K.ctx
    C = K.C
    P0, P1, P2, P3 = K.P0, K.P1, K.P2, K.P3
    TBLK, NB = 512, 8
    W = {}
    for nm in ("W_r", "W_k", "W_v"):
        wT, bT = prepd[nm]
        w = K.sb("D_" + nm + "T", [128, 8 * 1024], BF16)
        ctx.op("sp", lambda s, w=w, wT=wT: s.dma_start(
            w.ap.rearrange("p (c f) -> p c f", c=8),
            wT.rearrange("(c p) f -> p c f", p=128)),
            reads=[bT], writes=[w.buf], dma=True, tag="ldWD")
        W[nm] = w
    murc, omrc = prepd["mu_r"]
    mukc, omkc = prepd["mu_k"]
    xTb = K.sb("D_xTb", [128, 8 * TBLK], BF16)
    xsh = K.sb("D_xsh", [128, 8 * TBLK], BF16)
    tdif = K.sb("D_tdif", [128, 8 * TBLK], BF16)
    xrT = K.sb("D_xrT", [128, 8 * TBLK], BF16)
    xkT = K.sb("D_xkT", [128, 8 * TBLK], BF16)
    kaT = K.sb("D_kaT", [128, 8 * TBLK], BF16)
    rb2 = K.sb("D_rb2", [128, TBLK], BF16)
    rblk = K.sb("D_rblk", [128, 4 * 1024], F32)
    rblk_t = K.carve(rblk, 4, 1024)
    scr = K.sb("D_scr", [128, 1024], F32)
    pv = K.sb("D_pv", [128, 8], BF16)
    v3 = lambda ap: ap.rearrange("p (dc w) -> p dc w", w=TBLK)
    ctx.op("dve", lambda v: v.memset(pv.ap, 0.0), writes=[pv.buf])

    for b in range(NB):
        if b > 0:
            ctx.op("dve", lambda v: v.tensor_copy(
                pv.ap.unsqueeze(2), v3(xTb.ap)[:, :, TBLK - 1:TBLK]),
                reads=[xTb.buf], writes=[pv.buf], tag="pvcp")
        for g in range(4):
            tile = K.str_tiles[4 * b + g]
            K.transpose8(K.STR, 1024 * (4 * b + g), xTb,
                         dst_view_fn=lambda ap, g=g: v3(ap)[:, :, 128 * g:128 * (g + 1)],
                         rbuf=tile.buf)
        ctx.op("dve", lambda v: v.tensor_copy(
            v3(xsh.ap)[:, :, 1:TBLK], v3(xTb.ap)[:, :, 0:TBLK - 1]),
            reads=[xTb.buf], writes=[xsh.buf], tag="shcp")
        ctx.op("dve", lambda v: v.tensor_copy(
            v3(xsh.ap)[:, :, 0:1], pv.ap.unsqueeze(2)),
            reads=[pv.buf], writes=[xsh.buf], tag="shcp0")
        ctx.op("dve", lambda v: v.tensor_sub(tdif.ap, xTb.ap, xsh.ap),
               reads=[xTb.buf, xsh.buf], writes=[tdif.buf], tag="tdif")
        for dst, mu in ((xrT, murc), (xkT, mukc)):
            ctx.op("dve", lambda v, dst=dst, mu=mu: v.tensor_tensor(
                v3(dst.ap), v3(tdif.ap),
                mu.ap.unsqueeze(2).broadcast_to([128, 8, TBLK]), ALU.mult),
                reads=[tdif.buf, mu.buf], writes=[dst.buf], tag="xmu")
            ctx.op("dve", lambda v, dst=dst: v.tensor_add(dst.ap, dst.ap, xsh.ap),
                   reads=[dst.buf, xsh.buf], writes=[dst.buf], tag="xmu2")
        for g in range(4):
            rps = P0 if g % 2 == 0 else P1
            for half in range(2):
                o = 512 * half
                for cc in range(8):
                    ctx.op("pe", lambda t, g=g, cc=cc, o=o, rps=rps: t.matmul(
                        rps.ap[:, o:o + 512],
                        v3(xrT.ap)[:, cc, 128 * g:128 * (g + 1)],
                        W["W_r"].ap[:, 1024 * cc + o:1024 * cc + o + 512],
                        start=(cc == 0), stop=(cc == 7)),
                        reads=[xrT.buf, W["W_r"].buf], writes=[rps.buf], tag="mmr")
            ctx.op("act", lambda s, rps=rps: s.activation(
                scr.ap, rps.ap, AF.Tanh, scale=0.5),
                reads=[rps.buf], writes=[scr.buf], tag="rtanh")
            ctx.op("dve", lambda v, g=g: v.tensor_scalar(
                rblk_t[g].ap, scr.ap, 0.5, 0.5, ALU.mult, ALU.add),
                reads=[scr.buf], writes=[rblk_t[g].buf], tag="rsig")
        for dc2 in range(8):
            kps = P3 if dc2 % 2 == 0 else P2
            kap = kps.ap[:, 0:TBLK]
            for cc in range(8):
                ctx.op("pe", lambda t, dc2=dc2, cc=cc, kap=kap: t.matmul(
                    kap, W["W_k"].ap[:, 1024 * cc + 128 * dc2:1024 * cc + 128 * dc2 + 128],
                    v3(xkT.ap)[:, cc, :],
                    start=(cc == 0), stop=(cc == 7)),
                    reads=[xkT.buf, W["W_k"].buf], writes=[kps.buf], tag="mmk")
            ctx.op("act", lambda s, kap=kap, kps=kps: s.activation(
                rb2.ap, kap, AF.Relu),
                reads=[kps.buf], writes=[rb2.buf], tag="krelu")
            ctx.op("dve", lambda v, dc2=dc2: v.tensor_mul(
                v3(kaT.ap)[:, dc2, :], rb2.ap, rb2.ap),
                reads=[rb2.buf], writes=[kaT.buf], tag="ksq")
        for g in range(4):
            c = 4 * b + g
            tile = K.str_tiles[c]
            kvps = P0 if g % 2 == 0 else P1
            for half in range(2):
                o = 512 * half
                for cc in range(8):
                    ctx.op("pe", lambda t, g=g, cc=cc, o=o, kvps=kvps: t.matmul(
                        kvps.ap[:, o:o + 512],
                        v3(kaT.ap)[:, cc, 128 * g:128 * (g + 1)],
                        W["W_v"].ap[:, 1024 * cc + o:1024 * cc + o + 512],
                        start=(cc == 0), stop=(cc == 7)),
                        reads=[kaT.buf, W["W_v"].buf], writes=[kvps.buf], tag="mmkv")
            ctx.op("dve", lambda v, g=g, kvps=kvps: v.tensor_mul(
                scr.ap, kvps.ap, rblk_t[g].ap),
                reads=[kvps.buf, rblk_t[g].buf], writes=[scr.buf], tag="kvr")
            ctx.op("dve", lambda v, tile=tile, c=c: v.scalar_tensor_tensor(
                tile.ap, scr.ap, 0.0, tile.ap, ALU.add, ALU.add,
                accum_out=K.stats_s_t[c].ap),
                reads=[scr.buf, tile.buf],
                writes=[tile.buf, K.stats_s_t[c].buf], tag="residD")
            ctx.op("act", lambda s, tile=tile, c=c: s.activation(
                scr.ap, tile.ap, AF.Square, accum_out=K.stats_q_t[c].ap),
                reads=[tile.buf], writes=[scr.buf, K.stats_q_t[c].buf], tag="sqdD")


def emit_final_sweep(K, out_ap):
    ctx = K.ctx
    mean = K.sb("fs_mean", [128, NCH], F32)
    var = K.sb("fs_var", [128, NCH], F32)
    rstd = K.sb("fs_rstd", [128, NCH], F32)
    ctx.op("dve", lambda v: v.tensor_copy(mean.ap, K.stats_s.ap),
           reads=[b.buf for b in K.stats_s_t], writes=[mean.buf], tag="fs_cp")
    emit_batch_rstd_from(K, mean, K.stats_q, rstd, EPS_LN)
    stage = [K.sb("fs_stage%d" % i, [128, 1024], F32) for i in range(2)]
    ov = out_ap.rearrange("(t p) f -> p t f", p=128)
    for c in range(NCH):
        tile = K.str_tiles[c]
        st = stage[c % 2]
        ctx.op("dve", lambda v, c=c, tile=tile, st=st: v.tensor_scalar(
            st.ap, tile.ap, mean.ap[:, c:c + 1], rstd.ap[:, c:c + 1],
            ALU.subtract, ALU.mult),
            reads=[tile.buf, mean.buf, rstd.buf], writes=[st.buf], tag="fs_norm")
        ctx.op("sp", lambda s, c=c, st=st: s.dma_start(ov[:, c, :], st.ap),
               reads=[st.buf], dma=True, tag="fs_st", lane="out" + str(c % 2))


def build_launch4():
    inputs = {
        "x": ((T, D), F32),
        "L1_ffn_W_r": ((D, D), F32), "L1_ffn_W_k": ((D, D), F32),
        "L1_ffn_W_v": ((D, D), F32),
        "L1_ffn_mu_r": ((D,), F32), "L1_ffn_mu_k": ((D,), F32),
    }
    outputs = {"out": ((T, D), F32)}
    k = K(inputs, outputs)
    k.add_const_inputs(["ident", "ones_b", "ones_f", "onescol_b", "one1_f", "one1_b"])
    k.alloc_stream()
    k.mk_arena(136 * 1024 - 512)
    k.load_stream_from_dram(k.din["x"], cast=True)
    k.push_region()
    prepd = emit_prepD(k, castsD)
    emit_phaseD(k, prepd)
    k.pop_region()
    k.push_region()
    emit_final_sweep(k, k.dout["out"])
    k.pop_region()
    k.ctx.finalize()
    return k




def build_launch1(nch=NCH, sweep=True):
    inputs = {
        "x": ((T, D), F32),
        "L0_attn_Wq": ((D, D), F32), "L0_attn_bq": ((D,), F32),
        "L0_attn_Wk": ((D, D), F32), "L0_attn_bk": ((D,), F32),
        "L0_attn_Wv": ((D, D), F32), "L0_attn_bv": ((D,), F32),
        "L0_attn_Wo": ((D, D), F32), "L0_attn_bo": ((D,), F32),
    }
    outputs = {"stream1": ((T, D), F32)}
    k = K(inputs, outputs)
    k.add_const_inputs(["ident", "triU", "normal", "inv_normal",
                        "ones_b", "ones_f", "onescol_b", "one1_f"])
    k.alloc_stream()
    k.load_stream_from_dram(k.din["x"], cast=True)
    emit_phaseA(k, nch=nch)
    if sweep:
        emit_ln_sweep(k)
    k.store_stream_to_dram_f32(k.dout["stream1"])
    k.ctx.finalize()
    return k


def const_inmap():
    c = host_consts()
    return {"c_" + n: np.ascontiguousarray(v) for n, v in c.items()}




ALL_INPUTS = {
    "L0_attn_x": ((T, D), F32),
    "L0_attn_Wq": ((D, D), F32), "L0_attn_bq": ((D,), F32),
    "L0_attn_Wk": ((D, D), F32), "L0_attn_bk": ((D,), F32),
    "L0_attn_Wv": ((D, D), F32), "L0_attn_bv": ((D,), F32),
    "L0_attn_Wo": ((D, D), F32), "L0_attn_bo": ((D,), F32),
    "L0_ffn_W_fc": ((4096, D), F32), "L0_ffn_W_proj": ((D, 4096), F32),
    "L1_attn_ln1_w": ((D,), F32), "L1_attn_ln1_b": ((D,), F32),
    "L1_attn_wq": ((D, D), F32), "L1_attn_wk": ((D, D), F32),
    "L1_attn_wv": ((D, D), F32), "L1_attn_wg": ((D, D), F32),
    "L1_attn_wo": ((D, D), F32),
    "L1_attn_ln2_w": ((D,), F32), "L1_attn_ln2_b": ((D,), F32),
    "L1_attn_w_in": ((D, 5460), F32), "L1_attn_b_in": ((5460,), F32),
    "L1_attn_w_out": ((HID, D), F32), "L1_attn_b_out": ((D,), F32),
    "L1_ffn_W_r": ((D, D), F32), "L1_ffn_W_k": ((D, D), F32),
    "L1_ffn_W_v": ((D, D), F32),
    "L1_ffn_mu_r": ((D,), F32), "L1_ffn_mu_k": ((D,), F32),
}


def build_fused():
    k = K(ALL_INPUTS, {"out": ((T, D), F32)})
    k.add_const_inputs()
    k.alloc_stream()
    k.mk_arena(136 * 1024 - 512)
    m1 = k.sb("m1", [128, NCH], F32, perm=True)
    v1 = k.sb("v1", [128, NCH], F32, perm=True)
    rstd1 = k.sb("rstd1", [128, NCH], F32, perm=True)
    m2 = k.sb("m2", [128, NCH], F32, perm=True)
    rstd2 = k.sb("rstd2", [128, NCH], F32, perm=True)
    r_spill = k.nc.dram_tensor("r_spill", [T, D], BF16, kind="Internal").ap()
    b_rspill = Buf("r_spill")

    k.load_stream_from_dram(k.din["L0_attn_x"], cast=True)
    castsB = emit_castsB(k)
    emit_castsC(k)
    castsD = emit_castsD(k)
    k.push_region()
    emit_phaseA(k)
    k.pop_region()
    k.push_region()
    emit_ln_sweep(k)
    k.pop_region()
    k.push_region()
    wfcT, wprojT = emit_prepB(k, castsB)
    emit_phaseB(k, wfcT, wprojT)
    k.pop_region()
    k.push_region()
    scrS = k.sb("S2_scr", [128, 1024], F32)
    def extra(c):
        tile = k.str_tiles[c]
        k.ctx.op("dve", lambda v, c=c, tile=tile: v.tensor_reduce(
            m1.ap[:, c:c + 1], tile.ap, AX.X, ALU.add),
            reads=[tile.buf], writes=[m1.buf], tag="s2_m1")
        k.ctx.op("act", lambda s, c=c, tile=tile: s.activation(
            scrS.ap, tile.ap, AF.Square, accum_out=v1.ap[:, c:c + 1]),
            reads=[tile.buf], writes=[scrS.buf, v1.buf], tag="s2_v1")
    emit_ln_sweep(k, extra_per_tile=extra)
    k.ctx.op("sp", lambda s: s.dma_start(
        r_spill.rearrange("(t p) f -> p t f", p=128),
        k.STR.ap.rearrange("p (t f) -> p t f", t=NCH)),
        reads=[t.buf for t in k.str_tiles], writes=[b_rspill],
        dma=True, tag="spill_r", lane="r_spill")
    k.pop_region()
    emit_batch_rstd(k, m1, v1, rstd1, EPS_LN)
    k.push_region()
    prep = emit_prepC(k)
    k.pop_region()
    k.push_region()
    emit_phaseC2(k, prep, m1, rstd1)
    k.pop_region()
    k.ctx.op("dve", lambda v: v.tensor_copy(m2.ap, k.stats_s.ap),
             reads=[b.buf for b in k.stats_s_t], writes=[m2.buf], tag="s3cp")
    emit_batch_rstd_from(k, m2, k.stats_q, rstd2, EPS_LN)
    k.push_region()
    emit_phaseC3(k, prep, m2, rstd2, r_spill, r_cast=False, r_buf=b_rspill)
    k.pop_region()
    k.push_region()
    emit_ln_sweep(k)
    k.pop_region()
    k.push_region()
    prepd = emit_prepD(k, castsD)
    emit_phaseD(k, prepd)
    k.pop_region()
    k.push_region()
    emit_final_sweep(k, k.dout["out"])
    k.pop_region()
    k.ctx.finalize()
    return k



_BUILD_CACHE = {}


def _get_fused():
    if "kf" not in _BUILD_CACHE:
        _BUILD_CACHE["kf"] = build_fused()
        _BUILD_CACHE["c"] = const_inmap()
    return _BUILD_CACHE["kf"], _BUILD_CACHE["c"]


def kernel(**inputs):
    k, cm = _get_fused()
    NCORES = 8
    x = np.asarray(inputs["L0_attn_x"], dtype=np.float32)
    base = {("c_" + n): v for n, v in cm.items() if False}
    base = {n: np.ascontiguousarray(v) for n, v in cm.items()}
    for nm in ALL_INPUTS:
        if nm == "L0_attn_x":
            continue
        base[nm] = np.ascontiguousarray(np.asarray(inputs[nm], dtype=np.float32))
    in_maps = [dict(base, L0_attn_x=np.ascontiguousarray(x[c]))
               for c in range(NCORES)]
    res = run_bass_kernel_spmd(k.nc, in_maps, core_ids=list(range(NCORES)))
    return np.stack([np.asarray(res.results[c]["out"]) for c in range(NCORES)],
                    axis=0)
